# version 01
# baseline (speedup 1.0000x reference)
import contextlib
import numpy as np

import concourse.bass as bass
import concourse.mybir as mybir
from concourse.bass_utils import run_bass_kernel_spmd

F32 = mybir.dt.float32
I32 = mybir.dt.int32
AF = mybir.ActivationFunctionType
OP = mybir.AluOpType
AX = mybir.AxisListType

B, T, D, H, DH, PD = 4, 4096, 1024, 16, 64, 6
HID, E, R = 2048, 8, 16
NB = T // 128
NRUN, RUNTOK = 4, 512
OWN = NRUN * RUNTOK
KLEN = [8, 16, 24, 32]
SCALE = 0.125
EPS = 1e-5
RUNS_A = [0, 8, 20, 28]
RUNS_B = [4, 12, 16, 24]
NPAIR = 8
SPAD = 8

ENGS = ("sync", "tensor", "vector", "scalar", "gpsimd")
ARN = 52736

A1 = 0
A2 = 16384
A3 = 32768
XT = 49152

XC = XT
XROW = XT + 640
XB2GB = XT + 1024
XGT = XT + 2048
XMONE = XT + 2176
C_CIHS = XT + 2176
XMZERO = XT + 2688
XGATES = XT + 3200
XSPARE = XT + 3328

C_UCL = XC + 0
C_CIH = XC + 128
C_CFW = XC + 144
C_CF2 = XC + 160
C_MSB = XC + 176
C_MOD = XC + 192
C_QB = XC + 240
C_OB = XC + 264
C_MVB = XC + 272
C_B1 = XC + 280
C_B3 = XC + 408
C_APM = XC + 536
C_WGP = XC + 538
C_C1B = XC + 602
C_C2B = XC + 610
C_RS2C = XC + 618
C_M2RS = XC + 634
XROW = XT + 660

R_ONES = XROW + 0
R_IOTA = XROW + 128
R_LND = XROW + 256
R_C128 = XROW + 272
R_MGB = XROW + 288
R_MSC = XROW + 304
R_IMX = XROW + 320
R_DSG = XROW + 321
assert R_DSG + 16 <= XB2GB

P0_MODROW = A2
P0_WMB = A2 + 6144
P0_CSC = A2 + 10240

P1_XB = A2
P1_HB = A2 + 4096
P1_MROW = A2 + 8192
P1_SROW = A2 + 12288

P2_WB = A2
P2_W1L = A2 + 2112
P2_W2L = A2 + 2176
P2_R1L = A2 + 2240
P2_R2L = A2 + 2304
P2_LW = A2 + 2368
P2_SQ = A2 + 2464
P2_N2 = A2 + 2560
P2_JWS = A2 + 2576
P2_XP = A2 + 2672
P2_RD = A2 + 3248
P2_PP = A2 + 3344
P2_RDM = A2 + 3920
P2_S1 = A2 + 4016
P2_S2 = A2 + 4032
P2_S3 = A2 + 4048
P2_CROW = A2 + 4096
P2_TROW = A2 + 4672
P2_MSCB = A2 + 5248
P2_GT = A2 + 5264

P3_KT = A3
P3_VE = A3 + 4096
P3_QT = A3 + 8256
P3_PB = A3 + 8768
P3_HF = A3 + 9280
P3_WS = A3 + 11328
P3_OS = A3 + 13376
P3_ATL = A3 + 13888

P4_XO = A3
P4_MSC = A3 + 4096
P4_GATE = A3 + 4608
P4_GROW = A3 + 4864
P4_GCOL = A3 + 6912
P4_TMP = A3 + 6928

P5_XSQ = A3
P5_M2 = A3 + 4096
P5_RS2 = A3 + 6144
P5_LG = A3 + 8192
P5_T1 = A3 + 8320
P5_T2 = A3 + 8448
P5_M1 = A3 + 8576
P5_M2V = A3 + 8592
P5_W1N = A3 + 8608
P5_W2N = A3 + 8624

P6_G = A1
P6_W1S = A1 + 8192
P6_W3S = A1 + 10240
P6_W2M = A1 + 12288
P6_U1 = A1 + 12800
P6_U3 = A1 + 13312
P6_A1S = A1 + 13824
P6_A3S = A1 + 14848
P6_B1T = A1 + 15872
P6_B3T = A1 + 16128

C_ID = XSPARE
C_ONEC = XSPARE + 128
C_DUM = XSPARE + 192

DS_MOD = 0
DS_GBROW = 6144
DS_MSC = 8192
DS_GATE = 73728
DS_GROW = 106496
DS_ROW = 108544
DS_U1 = 112640
DS_U3 = 178176
DS_END = 243712


def av(handle, off, shape, p0=0, pn=128):
    n = int(np.prod(shape))
    full = handle[p0:p0 + pn, off:off + n]
    if len(shape) == 1:
        return full
    spec = " ".join(f"s{i}" for i in range(len(shape)))
    return full.rearrange(f"p ({spec}) -> p {spec}",
                          **{f"s{i}": shape[i] for i in range(len(shape))})


class PB:
    def __init__(self):
        self.steps = {e: [] for e in ENGS}
        self.cnt = {s: 0 for s in ("dS", "dG", "tT", "tV", "tS", "tG")}

    def tok(self, sem):
        return (sem, self.cnt[sem])

    def do(self, eng, fn=None, wait=(), inc=None, incn=1, is_dma=False):
        import traceback
        waits = [(s, v) for (s, v) in wait if v > 0]
        tokn = None
        if eng == "gpsimd" and fn is not None and not is_dma:
            if getattr(self, "_gprev", None):
                waits.append(self._gprev)
            if inc is None:
                inc = "tG"
        if inc is not None:
            self.cnt[inc] += incn
            tokn = (inc, self.cnt[inc])
        if eng == "gpsimd" and fn is not None and not is_dma:
            self._gprev = tokn
        site = traceback.extract_stack(limit=4)
        self.steps[eng].append((fn, waits, inc, incn, site))
        return tokn

    def dma(self, q, out, in_, wait=(), accum=False):
        sem = "dS" if q == "sync" else "dG"
        kw = {"accum_op": OP.add} if accum else {}
        return self.do(q, lambda e, o=out, i=in_, kw=kw: e.dma_start(out=o, in_=i, **kw),
                       wait=wait, inc=sem, incn=16, is_dma=True)

    def barrier(self):
        toks = [self.tok(s) for s in self.cnt]
        for e in ENGS:
            self.do(e, None, wait=toks)

    def emit(self, block, sems):
        def mk(name):
            def run(eng):
                for (fn, waits, inc, incn, site) in self.steps[name]:
                    for (s, v) in waits:
                        eng.wait_ge(sems[s], v)
                    try:
                        ins = fn(eng) if fn is not None else None
                    except Exception as ex:
                        raise RuntimeError(
                            "step failed, created at:\n" +
                            "".join(x for x in map(str, site[:-1]))) from ex
                    if inc is not None:
                        ins.then_inc(sems[inc], incn)
            return run
        block.sync(mk("sync"))
        block.tensor(mk("tensor"))
        block.vector(mk("vector"))
        block.scalar(mk("scalar"))
        block.gpsimd(mk("gpsimd"))


class KB:
    def __init__(self, dbg=(), stop_after=None):
        self.stop_after = stop_after
        self.nc = bass.Bass("TRN2", target_bir_lowering=False)
        self.p = PB()
        self.dbg = dict(dbg)
        self.dbg_out = {}
        self.inp = {}

    def A(self, off, shape, p0=0, pn=128):
        return av(self.ar, off, shape, p0, pn)

    def mm(self, outap, lhsT, rhs, start, stop, wait=(), inc=None, tp=None):
        def f(t, o=outap, l=lhsT, r=rhs, st=start, sp=stop, tp=tp):
            kw = {} if tp is None else {"tile_position": tp}
            return t.matmul(o, l, r, start=st, stop=sp, **kw)
        return self.p.do("tensor", f, wait=wait, inc=inc)

    def act(self, outap, inap, func, wait=(), inc=None, bias=0.0, scale=1.0,
            accum=None):
        def f(s, o=outap, i=inap, fu=func, b=bias, sc=scale, a=accum):
            kw = {} if a is None else {"accum_out": a}
            return s.activation(out=o, in_=i, func=fu, bias=b, scale=sc, **kw)
        return self.p.do("scalar", f, wait=wait, inc=inc)

    def scopy(self, outap, inap, wait=(), inc=None):
        return self.p.do("scalar", lambda s, o=outap, i=inap: s.copy(o, i),
                         wait=wait, inc=inc)

    def v(self, fname, *args, wait=(), inc=None, **kw):
        return self.p.do("vector",
                         lambda ve, fn=fname, a=args, k=kw: getattr(ve, fn)(*a, **k),
                         wait=wait, inc=inc)

    def g(self, fname, *args, wait=(), inc=None, **kw):
        return self.p.do("gpsimd",
                         lambda ge, fn=fname, a=args, k=kw: getattr(ge, fn)(*a, **k),
                         wait=wait, inc=inc)

    def dump(self, name, src_ap, wait):
        if name in self.dbg:
            self.p.dma("sync", self.dbg_out[name], src_ap, wait=wait)

    def declare_io(self):
        nc = self.nc

        def di(name, shape, dt=F32):
            self.inp[name] = nc.dram_tensor(name, list(shape), dt,
                                            kind="ExternalInput").ap()
            return self.inp[name]

        self.xg = di("xg", [128, 8, T])
        self.xo = di("xo", [128, 8, NRUN, RUNTOK])
        self.crow = di("crow", [D])
        self.wmod = di("wmod", [D, 6 * D])
        self.qkvwt = di("qkvwt", [D, 3 * D])
        self.qkvb = di("qkvb", [3 * D])
        self.wwt = di("wwt", [D, 256])
        self.memvwt = di("memvwt", [D, D])
        self.memvb = di("memvb", [D])
        self.memgwt = di("memgwt", [D, H])
        self.memgb = di("memgb", [H])
        self.mscale = di("mscale", [H])
        self.imix = di("imix", [1])
        self.outwt = di("outwt", [D, D])
        self.outb = di("outb", [D])
        self.dlog = di("dlog", [H])
        self.wgt = di("wgt", [D, E])
        self.w1t = di("w1t", [E, D, HID])
        self.w2t = di("w2t", [E, HID, D])
        self.w3t = di("w3t", [E, D, HID])
        self.b1 = di("b1", [E, HID])
        self.b2 = di("b2", [E, D])
        self.b3 = di("b3", [E, HID])
        self.a1t = di("a1t", [D, E * R])
        self.a3t = di("a3t", [D, E * R])
        self.b1t = di("b1t", [E, R, HID])
        self.b3t = di("b3t", [E, R, HID])
        self.a2 = di("a2", [E, R, HID])
        self.b2t = di("b2t", [E, R, D])
        self.masksel = di("masksel", [NRUN * 32], I32)
        self.maskt = di("maskt", [32, 128, 512])
        self.runoff = di("runoff", [8], I32)

        self.out = nc.dram_tensor("out", [OWN, D], F32, kind="ExternalOutput").ap()
        for name, shape in self.dbg.items():
            self.dbg_out[name] = nc.dram_tensor(
                "dbg_" + name, list(shape), F32, kind="ExternalOutput").ap()

        self.hsp = nc.dram_tensor("hsp", [128, 8, SPAD + T], F32, kind="Internal").ap()
        self.dscr = nc.dram_tensor("dscr", [1 << 18], F32, kind="Internal").ap()
        self.w2scr = nc.dram_tensor("w2scr", [E, HID, D], F32, kind="Internal").ap()


    def p0(self):
        p, A = self.p, self.A
        p.dma("sync", A(C_QB, [24]), self.qkvb.rearrange("(c p) -> p c", p=128))
        p.dma("sync", A(C_OB, [8]), self.outb.rearrange("(c p) -> p c", p=128))
        p.dma("sync", A(C_MVB, [8]), self.memvb.rearrange("(c p) -> p c", p=128))
        p.dma("sync", A(C_B1, [8, 16]), self.b1.rearrange("e (c p) -> p e c", p=128))
        p.dma("sync", A(C_B3, [8, 16]), self.b3.rearrange("e (c p) -> p e c", p=128))
        t_r2 = p.dma("sync", A(R_MGB, [16], pn=1), self.memgb[None, :])
        t_r3 = p.dma("sync", A(R_DSG, [16], pn=1), self.dlog[None, :])
        t_r4 = p.dma("sync", A(R_MSC, [16], pn=1), self.mscale[None, :])
        t_r5 = p.dma("sync", A(R_IMX, [1], pn=1), self.imix[None, :])
        t_cc = p.dma("sync", A(P0_CSC, [8, 1]),
                     self.crow.rearrange("(c p) -> p c", p=128)[:, :, None])
        p.dma("sync", self.itab[0:1, 0:128], self.masksel[None, :])
        p.dma("sync", self.itab[0:1, 128:136], self.runoff[None, :])

        self.g("memset", A(C_UCL, [128]), 1.0)
        self.g("affine_select", A(C_UCL, [128]), A(C_UCL, [128]),
               pattern=[[1, 128]], compare_op=OP.is_gt, fill=0.0, base=0,
               channel_multiplier=-1)
        self.g("iota", A(R_IOTA, [128], pn=1), pattern=[[1, 128]], base=0,
               channel_multiplier=0, allow_small_or_imprecise_dtypes=True)

        self._slot = [A3 + 8000]
        def _stage_dump(name, off, n, waits, pn=1, src_ap=None):
            slot = self._slot[0]
            self._slot[0] += n
            src = src_ap if src_ap is not None else A(off, [n], pn=pn)
            tcpx = self.v("tensor_copy", A(slot, [n], pn=pn), src,
                          wait=waits, inc="tV")
            self.dump(name, A(slot, [n], pn=pn), [tcpx])
        self.g("memset", A(C_ID, [128]), 1.0)
        self.g("affine_select", A(C_ID, [128]), A(C_ID, [128]),
               pattern=[[1, 128]], compare_op=OP.is_equal, fill=0.0, base=0,
               channel_multiplier=-1)
        self.g("memset", A(C_ONEC, [1]), 1.0)
        self.g("memset", A(R_ONES, [128], p0=64, pn=1), 1.0)
        t_g = self.g("memset", A(R_ONES, [128], pn=1), 1.0, inc="tG")

        t_dsg = self.act(A(R_DSG, [16], pn=1), A(R_DSG, [16], pn=1), AF.Sigmoid,
                         wait=[t_r3], inc="tS")
        t_lnd = self.act(A(R_LND, [16], pn=1), A(R_DSG, [16], pn=1), AF.Ln,
                         wait=[t_dsg], inc="tS")
        t_c128 = self.act(A(R_C128, [16], pn=1), A(R_LND, [16], pn=1), AF.Exp,
                          scale=128.0, wait=[t_lnd], inc="tS")
        t_al = self.act(A(R_IMX, [1], pn=1), A(R_IMX, [1], pn=1), AF.Sigmoid,
                        wait=[t_r5], inc="tS")
        t_m0 = self.mm(self.ps[:, 0, 0:1], A(R_ONES, [128], pn=1),
                       A(R_IMX, [1], pn=1), True, True, wait=[t_al, t_g], inc="tT")
        t_a1 = self.scopy(A(C_APM, [1]), self.ps[:, 0, 0:1], wait=[t_m0], inc="tS")
        t_a2 = self.act(A(C_APM + 1, [1]), self.ps[:, 0, 0:1], AF.Copy, scale=-1.0,
                        bias=1.0, wait=[t_m0], inc="tS")
        t_m1 = self.mm(self.ps[:, 1, 0:16], A(R_IOTA, [128], pn=1),
                       A(R_LND, [16], pn=1), True, True,
                       wait=[t_lnd, t_g, self.p.tok("tV")], inc="tT")
        t_ci = self.act(A(C_CIH, [16]), self.ps[:, 1, 0:16], AF.Exp, scale=-0.5,
                        wait=[t_m1], inc="tS")
        self.act(A(C_CFW, [16]), self.ps[:, 1, 0:16], AF.Exp, scale=1.0, inc="tS")
        self.act(A(C_CF2, [16]), self.ps[:, 1, 0:16], AF.Exp, scale=2.0, inc="tS")
        t_m2 = self.mm(self.ps[:, 2, 0:16], A(R_ONES, [128], pn=1),
                       A(R_MSC, [16], pn=1), True, True, wait=[t_r4, t_g], inc="tT")
        self.scopy(A(C_MSB, [16]), self.ps[:, 2, 0:16], wait=[t_m2], inc="tS")
        t_sg1 = self.g("memset", A(C_CIHS, [16, 6]), 1.0)
        t_sg2 = self.v("tensor_scalar_mul",
                       bass.AP(self.ar, C_CIHS + 1, [[ARN, 128], [6, 16], [3, 2]]),
                       bass.AP(self.ar, C_CIHS + 1, [[ARN, 128], [6, 16], [3, 2]]),
                       -1.0, wait=[t_sg1], inc="tV")
        t_sg3 = self.p.do("vector", lambda ve: ve.drain(), inc="tV")
        cihb0 = bass.AP(self.ar, C_CIH, [[ARN, 128], [1, 16], [0, 6]])
        t_cihs = self.v("tensor_mul", A(C_CIHS, [16, 6]), A(C_CIHS, [16, 6]),
                        cihb0, wait=[t_sg3, t_ci], inc="tV")
        _stage_dump("cihsf", C_CIHS, 96, [t_cihs], pn=128)

        t_sc = self.act(A(P0_CSC, [8, 1]), A(P0_CSC, [8, 1]), AF.Silu,
                        wait=[t_cc], inc="tS")
        prev = None
        for j in range(12):
            tw = p.dma("sync", A(P0_WMB, [8, 512]),
                       self.wmod.rearrange("(c p) m -> p c m", p=128)
                       [:, :, j * 512:(j + 1) * 512], wait=[p.tok("tT")])
            last = None
            for cc in range(8):
                w = [tw, t_sc] + ([prev] if (prev and cc == 0) else [])
                last = self.mm(self.ps[0:1, 3, :], A(P0_CSC + cc, [1]),
                               A(P0_WMB + cc * 512, [512]), cc == 0, cc == 7,
                               wait=w, inc=("tT" if cc == 7 else None))
            prev = self.scopy(A(P0_MODROW + j * 512, [512], pn=1),
                              self.ps[0:1, 3, :], wait=[last], inc="tS")
        t_md = p.dma("sync", self.dscr[None, DS_MOD:DS_MOD + 6144],
                     A(P0_MODROW, [6144], pn=1), wait=[prev])
        for vv in range(6):
            src = bass.AP(self.dscr.tensor, DS_MOD + vv * 1024,
                          [[1, 128], [128, 8]])
            p.dma("sync", A(C_MOD + 8 * vv, [8]), src, wait=[t_md])
        t_ml = p.tok("dS")
        tmp = P0_WMB
        for vv in (2, 5):
            gv = A(C_MOD + 8 * vv, [8])
            self.v("tensor_scalar_max", A(tmp, [8]), gv, 0.25, wait=[t_ml])
            self.v("tensor_scalar_min", A(tmp + 8, [8]), gv, -0.25)
            self.v("tensor_scalar", A(tmp + 16, [8]), gv, 0.0, None, op0=OP.is_ge)
            self.p.do("vector", lambda ve: ve.drain())
            self.v("tensor_sub", A(tmp, [8]), A(tmp, [8]), A(tmp + 8, [8]))
            self.p.do("vector", lambda ve: ve.drain())
            self.v("tensor_mul", A(tmp, [8]), A(tmp + 16, [8]), A(tmp, [8]))
            self.p.do("vector", lambda ve: ve.drain())
            self.v("tensor_add", gv, A(tmp, [8]), A(tmp + 8, [8]), inc="tV")
            self.p.do("vector", lambda ve: ve.drain())
        for vv in (1, 4):
            gv = A(C_MOD + 8 * vv, [8])
            self.v("tensor_scalar_add", gv, gv, 1.0, inc="tV")
        t_mod = p.tok("tV")
        p.dma("sync", bass.AP(self.dscr.tensor, DS_GBROW, [[1, 128], [128, 8]]),
              A(C_MOD + 8 * 5, [8]), wait=[t_mod])
        p.barrier()

    def ln_h_pass(self, src_dram, nchunk, mrow, srow, out_spill, own):
        p, A = self.p, self.A
        for ch in range(nchunk):
            tx = p.dma("sync", A(P1_XB, [8, 512]), src_dram[:, :, ch * 512:(ch + 1) * 512],
                       wait=[p.tok("tS"), p.tok("tT")])
            tsq = []
            for cc in range(8):
                tsq.append(self.act(A(P1_HB + cc * 512, [512]),
                                    A(P1_XB + cc * 512, [512]), AF.Square,
                                    wait=[tx] if cc == 0 else [], inc="tS"))
            lastm = lasts = None
            for cc in range(8):
                w = [tx] if cc == 0 else []
                lastm = self.mm(self.ps[0:1, 0, :], A(C_ONEC, [1]),
                                A(P1_XB + cc * 512, [512]),
                                cc == 0, cc == 7, wait=w,
                                inc=("tT" if cc == 7 else None))
            for cc in range(8):
                w = [tsq[cc]] if cc == 0 else [tsq[cc]]
                lasts = self.mm(self.ps[0:1, 1, :], A(C_ONEC, [1]),
                                A(P1_HB + cc * 512, [512]), cc == 0, cc == 7,
                                wait=w, inc=("tT" if cc == 7 else None))
            self.act(A(mrow + ch * 512, [512], pn=1), self.ps[0:1, 0, :], AF.Copy,
                     scale=1.0 / D, wait=[lastm], inc="tS")
            t_ev = self.act(A(srow + ch * 512, [512], pn=1), self.ps[0:1, 1, :],
                            AF.Copy, scale=1.0 / D, wait=[lasts], inc="tS")
            p.do("tensor", None, wait=[t_ev, p.tok("tS")])
        n = nchunk * 512
        tv1 = self.act(A(P1_HB, [n], pn=1), A(mrow, [n], pn=1), AF.Square,
                       wait=[p.tok("tS"), p.tok("tV"), p.tok("dG")], inc="tS")
        tv2 = self.v("tensor_sub", A(srow, [n], pn=1), A(srow, [n], pn=1),
                     A(P1_HB, [n], pn=1), wait=[tv1])
        tv2 = self.v("tensor_scalar_add", A(srow, [n], pn=1), A(srow, [n], pn=1),
                     EPS, inc="tV")
        tv3 = self.act(A(srow, [n], pn=1), A(srow, [n], pn=1), AF.Sqrt,
                       wait=[tv2], inc="tS")
        tv4 = self.v("reciprocal", A(srow, [n], pn=1), A(srow, [n], pn=1),
                     wait=[tv3], inc="tV")
        t_rs = self.p.do("vector", lambda ve: ve.drain(), wait=[tv4], inc="tV")
        for ch in range(nchunk):
            tx = p.dma("sync", A(P1_XB, [8, 512]), src_dram[:, :, ch * 512:(ch + 1) * 512],
                       wait=[p.tok("tV")])
            tmb = self.mm(self.ps[:, 2, :], A(R_ONES, [128], pn=1),
                          A(mrow + ch * 512, [512], pn=1), True, True,
                          wait=[t_rs, p.tok("tV")], inc="tT")
            trb = self.mm(self.ps[:, 3, :], A(R_ONES, [128], pn=1),
                          A(srow + ch * 512, [512], pn=1), True, True, inc="tT")
            tdone = []
            for cc in range(8):
                w = [tx, tmb, trb] if cc == 0 else []
                self.v("tensor_sub", A(P1_HB + cc * 512, [512]),
                       A(P1_XB + cc * 512, [512]), self.ps[:, 2, :], wait=w)
                self.v("tensor_mul", A(P1_HB + cc * 512, [512]),
                       A(P1_HB + cc * 512, [512]), self.ps[:, 3, :])
                td = self.v("tensor_scalar", A(P1_HB + cc * 512, [512]),
                            A(P1_HB + cc * 512, [512]),
                            A(C_MOD + 8 * 1 + cc, [1]), A(C_MOD + 8 * 0 + cc, [1]),
                            op0=OP.mult, op1=OP.add, inc="tV")
                tdone.append(td)
            if own:
                tc = self.v("tensor_copy", A(A1 + 0, [8, NRUN, 512])[:, :, ch, :],
                            A(P1_HB, [8, 512]), wait=[tdone[-1]], inc="tV")
                self.p.do("vector", None, wait=[tc])
            else:
                tspill = p.dma("gpsimd", self.hsp[:, :, SPAD + ch * 512:SPAD + (ch + 1) * 512],
                               A(P1_HB, [8, 512]), wait=[tdone[-1]])
                p.do("vector", None, wait=[tspill])
            p.do("tensor", None, wait=[tdone[-1]])

    def p1(self):
        p, A = self.p, self.A
        tz = self.g("memset", A(P1_HB, [8, SPAD]), 0.0, inc="tG")
        p.dma("gpsimd", self.hsp[:, :, 0:SPAD], A(P1_HB, [8, SPAD]), wait=[tz])
        self.ln_h_pass(self.xg, 8, P1_MROW, P1_SROW, True, False)
        p.barrier()
        self.ln_h_pass(self.xo.rearrange("p c r t -> p c (r t)"), 4,
                       P1_MROW, P1_SROW, False, True)
        p.barrier()
        self.dump("h_own", A(A1, [8, NRUN, 512]), [p.tok("tV")])
        self.dump("mrow", A(P1_MROW, [4096], pn=1), [p.tok("tS")])

    PAIRS4 = [(0, 1), (0, 2), (0, 3), (1, 2), (1, 3), (2, 3)]

    def exterior(self, p1off, p2off, outoff, wait):
        A = self.A
        def e6(off, k):
            return bass.AP(self.ar, off + k, [[ARN, 128], [6, 16], [1, 1]])

        def e4(off, k):
            return bass.AP(self.ar, off + k, [[ARN, 128], [4, 16], [1, 1]])

        first = True
        for k, (i, j) in enumerate(self.PAIRS4):
            self.v("tensor_mul", e6(outoff, k), e4(p1off, i), e4(p2off, j),
                   wait=(wait if first else ()))
            first = False
            self.v("tensor_mul", e6(P2_SQ, k), e4(p1off, j), e4(p2off, i))
        self.p.do("vector", lambda ve: ve.drain())
        self.v("tensor_sub", A(outoff, [16, 6]), A(outoff, [16, 6]),
               A(P2_SQ, [16, 6]))
        self.p.do("vector", lambda ve: ve.drain())
        self.v("tensor_mul", A(P2_SQ, [16, 6]), A(outoff, [16, 6]),
               A(outoff, [16, 6]))
        t0 = self.v("reduce_sum", A(P2_N2, [16]), A(P2_SQ, [16, 6]), axis=AX.X,
                    inc="tV")
        t1 = self.p.do("vector", lambda ve: ve.drain(), inc="tV")
        t1b = self.v("tensor_scalar_max", A(P2_N2, [16]), A(P2_N2, [16]), 1e-24,
                     wait=[t1], inc="tV")
        ts = self.act(A(P2_N2, [16]), A(P2_N2, [16]), AF.Sqrt, wait=[t1b], inc="tS")
        t2 = self.v("reciprocal", A(P2_N2, [16]), A(P2_N2, [16]), wait=[ts], inc="tV")
        t3 = self.p.do("vector", lambda ve: ve.drain(), inc="tV")
        rn_b = bass.AP(self.ar, P2_N2, [[ARN, 128], [1, 16], [0, 6]])
        t4 = self.v("tensor_mul", A(outoff, [16, 6]), A(outoff, [16, 6]), rn_b,
                    wait=[t3], inc="tV")
        return t4

    def p2(self):
        p, A = self.p, self.A
        WWTS = A2 + 5280
        t_ww = p.dma("sync", A(WWTS, [8, 256]),
                     self.wwt.rearrange("(c p) m -> p c m", p=128))
        MGWS = A2 + 7328
        t_mg = p.dma("sync", A(MGWS, [8, 16]),
                     self.memgwt.rearrange("(c p) m -> p c m", p=128))
        t_cz = self.g("memset", A(P2_CROW, [576], pn=1), 0.0, inc="tG")

        cihb = bass.AP(self.ar, C_CIH, [[ARN, 128], [1, 16], [0, 6]])
        t_prev_upd = t_cz
        for b in range(NB):
            wb = P2_WB + (b % 2) * 1056
            t_w = p.dma("gpsimd", A(wb, [8, 132])[:, :, 0:129],
                        self.hsp[:, :, SPAD + b * 128 - 1: SPAD + b * 128 + 128],
                        wait=[])
            l0 = l1 = None
            for c in range(8):
                w = [t_w, t_ww] if c == 0 else []
                l0 = self.mm(self.ps[:, 0, 0:64], A(wb, [8, 132])[:, c, 0:128],
                             A(WWTS, [8, 256])[:, c, 0:64], c == 0, c == 7,
                             wait=w, inc=("tT" if c == 7 else None))
            for c in range(8):
                l1 = self.mm(self.ps[:, 1, 0:192], A(wb, [8, 132])[:, c, 1:129],
                             A(WWTS, [8, 256])[:, c, 64:256], c == 0, c == 7,
                             inc=("tT" if c == 7 else None))
            self.v("tensor_copy", A(P2_W1L, [16, 4]),
                   self.ps[:, 0, 0:64].rearrange("p (h f) -> p h f", h=16),
                   wait=[l0])
            t_ev = self.v("tensor_copy", A(P2_W2L, [16, 4]),
                          self.ps[:, 1, 0:64].rearrange("p (h f) -> p h f", h=16),
                          wait=[l1])
            self.v("tensor_copy", A(P2_R1L, [16, 4]),
                   self.ps[:, 1, 64:128].rearrange("p (h f) -> p h f", h=16))
            t_ev2 = self.v("tensor_copy", A(P2_R2L, [16, 4]),
                           self.ps[:, 1, 128:192].rearrange("p (h f) -> p h f", h=16),
                           inc="tV")
            if b == 1:
                self.dump("w1l", A(P2_W1L, [16, 4]), [t_ev2])
                self.dump("w2l", A(P2_W2L, [16, 4]), [t_ev2])
            if b == 1:
                self.dump("lw1b", A(P2_LW, [16, 6]), [self.p.tok("tV")])
            t_extw = self.exterior(P2_W1L, P2_W2L, P2_LW, [t_ev2])
            if b == 1:
                self.dump("lw", A(P2_LW, [16, 6]), [t_extw])
                self.dump("jws1", A(P2_JWS, [16, 6]), [self.p.tok("tV"), t_extw])
            rev = bass.AP(self.ar, P2_LW + 5, [[ARN, 128], [6, 16], [-1, 6]])
            self.v("tensor_copy", A(P2_JWS, [16, 6]), rev)
            t_jw = self.v("tensor_mul", A(P2_JWS, [16, 6]), A(P2_JWS, [16, 6]),
                          A(C_CIHS, [16, 6]), inc="tV")
            jw_j = bass.AP(self.ar, P2_JWS, [[ARN, 128], [6, 16], [1, 6], [0, 6]])
            jw_i = bass.AP(self.ar, P2_JWS, [[ARN, 128], [6, 16], [0, 6], [1, 6]])
            t_xp = self.v("tensor_mul", A(P2_XP, [16, 6, 6]), jw_j, jw_i, inc="tV")
            if b == 0:
                self.dump("jws", A(P2_JWS, [16, 6]), [t_xp])
                self.dump("xp", A(P2_XP, [16, 6, 6]), [t_xp])
            t_rd = self.exterior(P2_R1L, P2_R2L, P2_RD, [t_ev2])
            tE = []
            for hh in range(2):
                m0 = self.mm(self.ps[:, 2 + hh, 0:288], A(C_UCL, [128]),
                             A(P2_XP + 288 * hh, [288]), True, False,
                             wait=[t_xp, t_prev_upd])
                m1 = self.mm(self.ps[:, 2 + hh, 0:288], A(R_ONES, [128], pn=1),
                             A(P2_CROW + 288 * hh, [288], pn=1), False, True,
                             inc="tT")
                tE.append(m1)
            tT_ = []
            for hh in range(2):
                m = self.mm(self.ps[0:1, 4 + hh, 0:288], A(C_ONEC, [1]),
                            A(P2_XP + 288 * hh, [288]), True, True, inc="tT")
                tT_.append(m)
            e1 = self.scopy(A(P2_TROW, [288], pn=1), self.ps[0:1, 4, 0:288],
                            wait=[tT_[0]], inc="tS")
            e2 = self.scopy(A(P2_TROW + 288, [288], pn=1), self.ps[0:1, 5, 0:288],
                            wait=[tT_[1]], inc="tS")
            g1 = self.g("tensor_add", A(P2_CROW, [576], pn=1),
                        A(P2_CROW, [576], pn=1), A(P2_TROW, [576], pn=1),
                        wait=[e1, e2] + tE)
            c128b = bass.AP(self.ar, R_C128, [[ARN, 1], [1, 16], [0, 36]])
            t_prev_upd = self.g("tensor_mul", A(P2_CROW, [576], pn=1),
                                bass.AP(self.ar, P2_CROW, [[ARN, 1], [36, 16], [1, 36]]),
                                c128b, inc="tG")
            if b == 0:
                tcw = self.v("tensor_copy", av(self.ar, A3 + 8000, [576], 0, 1),
                             A(P2_CROW, [576], pn=1), wait=[t_prev_upd], inc="tV")
                self.dump("crow1", av(self.ar, A3 + 8000, [576], 0, 1), [tcw])
                tcw2 = self.v("tensor_copy", av(self.ar, A3 + 8600, [576], 0, 1),
                              A(P2_TROW, [576], pn=1), wait=[t_prev_upd], inc="tV")
                self.dump("trow0", av(self.ar, A3 + 8600, [576], 0, 1), [tcw2])
            rd_i = bass.AP(self.ar, P2_RD, [[ARN, 128], [6, 16], [0, 6], [1, 6]])
            self.v("tensor_mul", A(P2_PP, [16, 6, 6])[:, 0:8, :, :],
                   self.ps[:, 2, 0:288].rearrange("p (h j i) -> p h j i", h=8, j=6),
                   bass.AP(self.ar, P2_RD, [[ARN, 128], [6, 8], [0, 6], [1, 6]]),
                   wait=[t_rd, tE[0]])
            self.v("tensor_mul", A(P2_PP, [16, 6, 6])[:, 8:16, :, :],
                   self.ps[:, 3, 0:288].rearrange("p (h j i) -> p h j i", h=8, j=6),
                   bass.AP(self.ar, P2_RD + 48, [[ARN, 128], [6, 8], [0, 6], [1, 6]]),
                   wait=[tE[1]])
            t_r0 = self.v("reduce_sum", A(P2_RDM, [16, 6]), A(P2_PP, [16, 6, 6]),
                          axis=AX.X, inc="tV")
            if b == 1:
                self.dump("pp1", A(P2_PP, [16, 6, 6]), [t_r0])
                self.dump("rd1", A(P2_RD, [16, 6]), [t_r0])
            self.v("tensor_mul", A(P2_SQ, [16, 6]), A(P2_RDM, [16, 6]),
                   A(P2_RD, [16, 6]))
            self.v("reduce_sum", A(P2_S1, [16]), A(P2_SQ, [16, 6]), axis=AX.X)
            self.v("tensor_mul", A(P2_SQ, [16, 6]), A(P2_RDM, [16, 6]),
                   A(P2_RDM, [16, 6]))
            t_r1 = self.v("reduce_sum", A(P2_S2, [16]), A(P2_SQ, [16, 6]),
                          axis=AX.X, inc="tV")
            t_dr = self.p.do("vector", lambda ve: ve.drain(), inc="tV")
            if b == 0:
                self.dump("rdm0", A(P2_RDM, [16, 6]), [t_dr])
                self.dump("s10", A(P2_S1, [16]), [t_dr])
                self.dump("s20", A(P2_S2, [16]), [t_dr])
                tec = self.v("tensor_copy", A(A2 + 8192, [288]), self.ps[:, 2, 0:288],
                             wait=[t_dr], inc="tV")
                self.dump("e0", A(A2 + 8192, [288]), [tec])
            self.v("tensor_mul", A(P2_S1, [16]), A(P2_S1, [16]), A(C_CFW, [16]),
                   wait=[t_dr])
            self.v("tensor_mul", A(P2_S2, [16]), A(P2_S2, [16]), A(C_CF2, [16]))
            self.v("tensor_scalar", A(P2_S1, [16]), A(P2_S1, [16]),
                   A(C_APM + 1, [1]), None, op0=OP.mult)
            t_ms = self.v("scalar_tensor_tensor", A(P2_MSCB, [16]),
                          A(P2_S2, [16]), A(C_APM, [1]), A(P2_S1, [16]),
                          op0=OP.mult, op1=OP.add, inc="tV")
            t_out = p.dma("sync",
                          bass.AP(self.dscr.tensor, DS_MSC + b * 2048,
                                  [[16, 128], [1, 16]]),
                          A(P2_MSCB, [16]), wait=[t_ms])
            p.do("tensor", None, wait=[p.tok("tV"), p.tok("tS"), t_out])
        for r in range(NRUN):
            for tt in range(4):
                last = None
                for c in range(8):
                    last = self.mm(self.ps[:, 6, 0:16],
                                   A(A1 + (c * NRUN + r) * 512 + tt * 128, [128]),
                                   A(MGWS, [8, 16])[:, c, :], c == 0, False,
                                   wait=([t_mg] if c == 0 else []))
                last = self.mm(self.ps[:, 6, 0:16], A(R_ONES, [128], pn=1),
                               A(R_MGB, [16], pn=1), False, True, inc="tT")
                t_sg = self.act(A(P2_GT, [16]), self.ps[:, 6, 0:16], AF.Sigmoid,
                                wait=[last], inc="tS")
                t_go = p.dma("sync",
                             bass.AP(self.dscr.tensor,
                                     DS_GATE + (r * 4 + tt) * 2048,
                                     [[16, 128], [1, 16]]),
                             A(P2_GT, [16]), wait=[t_sg])
                p.do("tensor", None, wait=[t_go])
        p.barrier()
        self.dump("msc", self.dscr[DS_MSC:DS_MSC + NB * 2048],
                  [p.tok("dS")])
        self.dump("gate", self.dscr[DS_GATE:DS_GATE + 16 * 2048],
                  [p.tok("dS")])

    def p3(self):
        p, A = self.p, self.A
        t_atl = self.g("memset", A(XSPARE + 200, [1]), 0.0, inc="tG")
        nmask = [0]

        for pair in range(NPAIR):
            t_prevpair = [p.tok("tT"), p.tok("tV"), p.tok("tS")]
            t_wk = p.dma("sync", A(P3_WS, [8, 128]),
                         self.qkvwt.rearrange("(c p) m -> p c m", p=128)
                         [:, :, D + pair * 128: D + (pair + 1) * 128],
                         wait=t_prevpair)
            t_wv = p.dma("sync", A(P3_WS + 1024, [8, 128]),
                         self.qkvwt.rearrange("(c p) m -> p c m", p=128)
                         [:, :, 2 * D + pair * 128: 2 * D + (pair + 1) * 128])
            self.v("memset", bass.AP(self.ar, P3_VE + 64, [[ARN, 128], [130, 32], [1, 1]]), 1.0,
                   wait=t_prevpair)
            t_vo = self.v("memset", bass.AP(self.ar, P3_VE + 129, [[ARN, 128], [130, 32], [1, 1]]),
                          1.0, inc="tV")
            t_hf_prev = p.tok("tT")
            for gc in range(16):
                t_hf = p.dma("gpsimd", A(P3_HF, [8, 256]),
                             self.hsp[:, :, SPAD + gc * 256: SPAD + (gc + 1) * 256],
                             wait=[t_hf_prev])
                lk = None
                for c in range(8):
                    w = [t_hf, t_wk, p.tok("tS")] if c == 0 else []
                    lk = self.mm(self.ps[:, 5, 0:256], A(P3_WS, [8, 128])[:, c, :],
                                 A(P3_HF, [8, 256])[:, c, :], c == 0, c == 7,
                                 wait=w, inc=("tT" if c == 7 else None))
                t_ke = self.act(A(P3_KT + gc * 256, [256]), self.ps[:, 5, 0:256],
                                AF.Identity, bias=A(C_QB + 8 + pair, [1]),
                                wait=[lk], inc="tS")
                t_ve = None
                for jj in range(2):
                    kt = gc * 2 + jj
                    lv = None
                    for c in range(8):
                        w = [t_hf, t_wv, t_vo, p.tok("tV")] if c == 0 else []
                        lv = self.mm(self.ps[:, 6, 0:128],
                                     A(P3_HF, [8, 256])[:, c, jj * 128:(jj + 1) * 128],
                                     A(P3_WS + 1024, [8, 128])[:, c, :],
                                     c == 0, c == 7, wait=w,
                                     inc=("tT" if c == 7 else None))
                    t_ve = self.v("tensor_copy",
                                  bass.AP(self.ar, P3_VE + kt * 130, [[ARN, 128], [65, 2], [1, 64]]),
                                  self.ps[:, 6, 0:128].rearrange("p (h f) -> p h f", h=2),
                                  wait=[lv], inc="tV")
                    p.do("tensor", None, wait=[t_ve])
                t_hf_prev = p.tok("tT")
                p.do("gpsimd", None, wait=[t_ke, t_ve])
            if pair == 3:
                self.dump("kt0", A(P3_KT, [32, 128]), [p.tok("tS")])
                self.dump("ve0", A(P3_VE, [32, 130]), [p.tok("tV")])
            t_wq = p.dma("sync", A(P3_WS, [8, 128]),
                         self.qkvwt.rearrange("(c p) m -> p c m", p=128)
                         [:, :, pair * 128:(pair + 1) * 128],
                         wait=[p.tok("tT")])
            for r in range(NRUN):
                lq = None
                for c in range(8):
                    w = [t_wq, p.tok("tS")] if c == 0 else []
                    lq = self.mm(self.ps[:, 5, :], A(P3_WS, [8, 128])[:, c, :],
                                 A(A1 + (c * NRUN + r) * 512, [512]),
                                 c == 0, c == 7, wait=w,
                                 inc=("tT" if c == 7 else None))
                t_qe = self.act(A(P3_QT, [512]), self.ps[:, 5, :], AF.Identity,
                                bias=A(C_QB + pair, [1]),
                                wait=[lq, p.tok("tT")], inc="tS")
                if pair == 4 and r == 0:
                    self.dump("qt0", A(P3_QT, [512]), [t_qe])
                kl = KLEN[r]
                for head in range(2):
                    hp0 = head * 64
                    tp = (64, 0) if head else None
                    t_prevav = getattr(self, "_t_lastav", None)
                    t_preve = None
                    for kt in range(kl):
                        w_s = [t_qe, t_atl] + ([t_preve] if t_preve else [])
                        t_s = self.mm(self.ps[:, head, :],
                                      A(P3_KT + kt * 128, [128], p0=hp0, pn=64),
                                      A(P3_QT, [512], p0=hp0, pn=64),
                                      True, True, wait=w_s, inc="tT",
                                      tp=tp)
                        masked = kt >= kl - 8
                        if masked:
                            mi = r * 8 + (kt - (kl - 8))
                            mslot = P3_ATL + (nmask[0] % 2) * 512
                            nmask[0] += 1
                            t_ml = p.dma("gpsimd", A(mslot, [512]),
                                         self.maskt[mi], wait=[p.tok("tV")])
                        w_exp = [t_s] + ([t_prevav] if t_prevav else [])
                        t_e = self.act(A(P3_PB, [512]), self.ps[:, head, :],
                                       AF.Exp, scale=SCALE, wait=w_exp, inc="tS")
                        t_preve = t_e
                        t_m = t_e
                        if masked:
                            t_m = self.v("tensor_mul", A(P3_PB, [512]),
                                         A(P3_PB, [512]), A(mslot, [512]),
                                         wait=[t_e, t_ml], inc="tV")
                        w_av = [t_m] + ([p.tok("tS"), p.tok("tV")] if kt == 0 else [])
                        t_prevav = self.mm(self.ps[0:65, 2 + head, :],
                                           A(P3_VE + kt * 130 + head * 65, [65]),
                                           A(P3_PB, [512]), kt == 0, kt == kl - 1,
                                           wait=w_av, inc="tT")
                        self._t_lastav = t_prevav
                    self.scopy(av(self.ar, P3_OS, [512], 0, 64),
                               self.ps[0:64, 2 + head, :],
                               wait=[t_prevav, p.tok("tV")], inc="tS")
                    t_oe = self.scopy(av(self.ar, P3_OS, [512], 64, 1),
                                      self.ps[64:65, 2 + head, :],
                                      wait=[t_prevav], inc="tS")
                    if pair == 0 and r == 0 and head == 0:
                        self.dump("os0", av(self.ar, P3_OS, [512], 0, 65), [t_oe])
                    t_rc = self.v("reciprocal", av(self.ar, P3_OS, [512], 64, 1),
                                  av(self.ar, P3_OS, [512], 64, 1), wait=[t_oe],
                                  inc="tV")
                    t_dr = self.p.do("vector", lambda ve: ve.drain(), inc="tV")
                    t_rb = self.mm(self.ps[0:64, 4, :],
                                   A(R_ONES, [64], p0=64, pn=1),
                                   av(self.ar, P3_OS, [512], 64, 1), True, True,
                                   wait=[t_dr, p.tok("tV")], inc="tT")
                    dst = av(self.ar, A2 + (pair * NRUN + r) * 512, [512],
                             head * 64, 64)
                    self.v("tensor_mul", dst, av(self.ar, P3_OS, [512], 0, 64),
                           self.ps[0:64, 4, :], wait=[t_rb])
                    t_sq = self.v("tensor_scalar_add", dst, dst,
                                  A(C_QB + 16 + pair, [1], p0=head * 64, pn=64),
                                  inc="tV")
                    p.do("tensor", None, wait=[t_sq])
        p.barrier()
        self.dump("seqo", A(A2, [8, NRUN, 512]), [p.tok("tV")])

    def p4(self):
        p, A = self.p, self.A
        t_ms = p.dma("sync", A(P4_MSC, [32, 16]),
                     self.dscr[DS_MSC:DS_MSC + NB * 2048]
                     .rearrange("(b p f) -> p b f", p=128, f=16)
                     if False else
                     bass.AP(self.dscr.tensor, DS_MSC,
                             [[16, 128], [2048, 32], [1, 16]]))
        t_gt = p.dma("sync", A(P4_GATE, [16, 16]),
                     bass.AP(self.dscr.tensor, DS_GATE,
                             [[16, 128], [2048, 16], [1, 16]]))
        msb_b = bass.AP(self.ar, C_MSB, [[ARN, 128], [0, 4], [1, 16]])
        for r in range(NRUN):
            def mscop(ve, r=r):
                off = ve.value_load(self.itab[0:1, 128 + r:129 + r])
                src = bass.AP(self.ar, off, [[ARN, 128], [16, 4], [1, 16]])
                return ve.tensor_tensor(av(self.ar, P4_TMP, [4, 16]), src, msb_b,
                                        op=OP.mult)
            t1 = p.do("vector", mscop, wait=[t_ms, t_gt], inc="tV")
            t2 = self.act(A(P4_TMP, [4, 16]), A(P4_TMP, [4, 16]), AF.Sigmoid,
                          wait=[t1], inc="tS")
            t3 = self.v("tensor_mul", A(P4_TMP, [4, 16]), A(P4_TMP, [4, 16]),
                        A(P4_GATE, [16, 16])[:, r * 4:(r + 1) * 4, :], wait=[t2])
            self.p.do("vector", lambda ve: ve.drain())
            t4 = self.v("reduce_sum", A(P4_GCOL + r * 4, [4]), A(P4_TMP, [4, 16]),
                        axis=AX.X, inc="tV")
        t_dr = self.p.do("vector", lambda ve: ve.drain(), inc="tV")
        t5 = self.v("tensor_scalar_mul", A(P4_GCOL, [16]), A(P4_GCOL, [16]),
                    1.0 / 16, wait=[t_dr], inc="tV")
        t_tr = self.p.do("tensor", lambda t: t.transpose(
            self.ps[0:16, 0, 0:128], av(self.ar, P4_GCOL, [16]),
            av(self.ar, C_ID, [128])), wait=[t5], inc="tT")
        t_ge = self.scopy(av(self.ar, P4_TMP, [128], 0, 16), self.ps[0:16, 0, 0:128],
                          wait=[t_tr], inc="tS")
        t_gw = p.dma("sync",
                     bass.AP(self.dscr.tensor, DS_GROW, [[128, 16], [1, 128]]),
                     av(self.ar, P4_TMP, [128], 0, 16), wait=[t_ge])
        t_gr = p.dma("sync", A(P4_GROW, [2048], pn=1),
                     self.dscr[None, DS_GROW:DS_GROW + 2048],
                     wait=[t_gw])
        self.dump("gcol", A(P4_GCOL, [16]), [t5])
        tgrs = self.v("tensor_copy", av(self.ar, P4_TMP + 200, [2048], 0, 1),
                      A(P4_GROW, [2048], pn=1), wait=[t_gr], inc="tV")
        self.dump("grow", av(self.ar, P4_TMP + 200, [2048], 0, 1), [tgrs])
        MVWS = A3 + 7040
        OWS = A3 + 8064
        MVT = A3 + 9088
        for r in range(NRUN):
            t_xo = p.dma("sync", A(P4_XO, [8, 512]), self.xo[:, :, r, :],
                         wait=[p.tok("tV")])
            t_gb = self.mm(self.ps[:, 1, :], A(R_ONES, [128], pn=1),
                           A(P4_GROW + r * 512, [512], pn=1), True, True,
                           wait=[t_gr, p.tok("tV")], inc="tT")
            for dc in range(8):
                t_wv = p.dma("sync", A(MVWS, [8, 128]),
                             self.memvwt.rearrange("(c p) m -> p c m", p=128)
                             [:, :, dc * 128:(dc + 1) * 128], wait=[p.tok("tT")])
                lm = None
                for c in range(8):
                    w = [t_wv, p.tok("tS")] if c == 0 else []
                    lm = self.mm(self.ps[:, 0, :], A(MVWS, [8, 128])[:, c, :],
                                 A(A1 + (c * NRUN + r) * 512, [512]),
                                 c == 0, c == 7, wait=w,
                                 inc=("tT" if c == 7 else None))
                t_me = self.act(A(MVT, [512]), self.ps[:, 0, :], AF.Identity,
                                bias=A(C_MVB + dc, [1]),
                                wait=[lm, p.tok("tV")], inc="tS")
                tmvg = self.v("tensor_mul", A(MVT, [512]), A(MVT, [512]),
                       self.ps[:, 1, :], wait=[t_me, t_gb], inc="tV")
                if r == 0 and dc == 0:
                    self.dump("mvg0", A(MVT, [512]), [tmvg])
                t_ac = self.v("tensor_add",
                              av(self.ar, A2 + (dc * NRUN + r) * 512, [512]),
                              av(self.ar, A2 + (dc * NRUN + r) * 512, [512]),
                              A(MVT, [512]), inc="tV")
                if r == 0 and dc == 0:
                    self.dump("ain0", av(self.ar, A2 + (dc * NRUN + r) * 512, [512]), [t_ac])
                p.do("tensor", None, wait=[t_ac])
            for dc in range(8):
                t_wo = p.dma("sync", A(OWS, [8, 128]),
                             self.outwt.rearrange("(c p) m -> p c m", p=128)
                             [:, :, dc * 128:(dc + 1) * 128], wait=[p.tok("tT")])
                lo = None
                for c in range(8):
                    w = [t_wo, p.tok("tV"), p.tok("tS")] if c == 0 else []
                    lo = self.mm(self.ps[:, 2, :], A(OWS, [8, 128])[:, c, :],
                                 av(self.ar, A2 + (c * NRUN + r) * 512, [512]),
                                 c == 0, c == 7, wait=w,
                                 inc=("tT" if c == 7 else None))
                t_oe = self.act(A(MVT, [512]), self.ps[:, 2, :], AF.Identity,
                                bias=A(C_OB + dc, [1]),
                                wait=[lo, p.tok("tV")], inc="tS")
                if r == 0 and dc == 0:
                    self.dump("y0", A(MVT, [512]), [t_oe])
                t_x2 = self.v("scalar_tensor_tensor",
                              av(self.ar, A1 + (dc * NRUN + r) * 512, [512]),
                              A(MVT, [512]), A(C_MOD + 16 + dc, [1]),
                              A(P4_XO, [8, 512])[:, dc, :],
                              op0=OP.mult, op1=OP.add, wait=[t_oe, t_xo], inc="tV")
                p.do("tensor", None, wait=[t_x2])
        p.barrier()
        self.dump("x2", A(A1, [8, NRUN, 512]), [p.tok("tV")])

    def p5(self):
        p, A = self.p, self.A
        for r in range(NRUN):
            tsq = []
            for c in range(8):
                tsq.append(self.act(A(P5_XSQ + c * 512, [512]),
                                    av(self.ar, A1 + (c * NRUN + r) * 512, [512]),
                                    AF.Square, inc="tS"))
            lm = ls = None
            for c in range(8):
                lm = self.mm(self.ps[0:1, 0, :], A(C_ONEC, [1]),
                             av(self.ar, A1 + (c * NRUN + r) * 512, [512]),
                             c == 0, c == 7, inc=("tT" if c == 7 else None))
            for c in range(8):
                ls = self.mm(self.ps[0:1, 1, :], A(C_ONEC, [1]),
                             A(P5_XSQ + c * 512, [512]), c == 0, c == 7,
                             wait=[tsq[c]], inc=("tT" if c == 7 else None))
            self.act(A(P5_M2 + r * 512, [512], pn=1), self.ps[0:1, 0, :], AF.Copy,
                     scale=1.0 / D, wait=[lm], inc="tS")
            t_e = self.act(A(P5_RS2 + r * 512, [512], pn=1), self.ps[0:1, 1, :],
                           AF.Copy, scale=1.0 / D, wait=[ls], inc="tS")
            p.do("tensor", None, wait=[t_e, p.tok("tS")])
        t1 = self.act(A(P5_XSQ, [2048], pn=1), A(P5_M2, [2048], pn=1), AF.Square,
                      wait=[p.tok("tS")], inc="tS")
        t2 = self.v("tensor_sub", A(P5_RS2, [2048], pn=1), A(P5_RS2, [2048], pn=1),
                    A(P5_XSQ, [2048], pn=1), wait=[t1])
        t2 = self.v("tensor_scalar_add", A(P5_RS2, [2048], pn=1),
                    A(P5_RS2, [2048], pn=1), EPS, inc="tV")
        t3 = self.act(A(P5_RS2, [2048], pn=1), A(P5_RS2, [2048], pn=1), AF.Sqrt,
                      wait=[t2], inc="tS")
        t4 = self.v("reciprocal", A(P5_RS2, [2048], pn=1), A(P5_RS2, [2048], pn=1),
                    wait=[t3], inc="tV")
        t_rs = self.p.do("vector", lambda ve: ve.drain(), inc="tV")
        t5 = self.v("tensor_mul", A(P5_XSQ, [2048], pn=1), A(P5_M2, [2048], pn=1),
                    A(P5_RS2, [2048], pn=1), wait=[t_rs], inc="tV")
        t_w1 = p.dma("sync", self.dscr[None, DS_ROW:DS_ROW + 2048],
                     A(P5_RS2, [2048], pn=1), wait=[t_rs])
        t_w2 = p.dma("sync", self.dscr[None, DS_ROW + 2048:DS_ROW + 4096],
                     A(P5_XSQ, [2048], pn=1), wait=[t5])
        t_c1 = p.dma("sync", A(C_RS2C, [16]),
                     bass.AP(self.dscr.tensor, DS_ROW, [[1, 128], [128, 16]]),
                     wait=[t_w1])
        t_c2 = p.dma("sync", A(C_M2RS, [16]),
                     bass.AP(self.dscr.tensor, DS_ROW + 2048,
                             [[1, 128], [128, 16]]), wait=[t_w2])

        t_wg = p.dma("sync", A(C_WGP, [8, 8]),
                     self.wgt.rearrange("(c p) e -> p c e", p=128))
        lc2 = None
        for c in range(8):
            lc2 = self.mm(self.ps[0:1, 5, 8:16], A(C_MOD + 24 + c, [1]),
                          A(C_WGP, [8, 8])[:, c, :], c == 0, c == 7,
                          wait=([t_wg] if c == 0 else []),
                          inc=("tT" if c == 7 else None))
        cb1b = bass.AP(self.ar, C_MOD + 32, [[ARN, 128], [1, 8], [0, 8]])
        t_sc = self.v("tensor_mul", A(C_WGP, [8, 8]), A(C_WGP, [8, 8]), cb1b,
                      wait=[lc2], inc="tV")
        lc1 = None
        for c in range(8):
            lc1 = self.mm(self.ps[0:1, 5, 0:8], A(C_ONEC, [1]),
                          A(C_WGP, [8, 8])[:, c, :], c == 0, c == 7,
                          wait=([t_sc] if c == 0 else []),
                          inc=("tT" if c == 7 else None))
        t_e1 = self.scopy(A(P5_LG, [16], pn=1), self.ps[0:1, 5, 0:16],
                          wait=[lc1], inc="tS")
        t_b1 = self.mm(self.ps[:, 6, 0:16], A(R_ONES, [128], pn=1),
                       A(P5_LG, [16], pn=1), True, True, wait=[t_e1], inc="tT")
        t_bc = self.scopy(A(C_C1B, [16]), self.ps[:, 6, 0:16], wait=[t_b1], inc="tS")

        for tt in range(16):
            r, sub = tt // 4, tt % 4
            lg = None
            for c in range(8):
                lg = self.mm(self.ps[:, 4, 0:8],
                             av(self.ar, A1 + (c * NRUN + r) * 512 + sub * 128, [128]),
                             A(C_WGP, [8, 8])[:, c, :], c == 0, c == 7,
                             wait=([t_sc, t_bc, t_c1, t_c2] if c == 0 else []),
                             inc=("tT" if c == 7 else None))
            self.v("tensor_scalar", A(P5_T1, [8]), self.ps[:, 4, 0:8],
                   A(C_RS2C + tt, [1]), None, op0=OP.mult, wait=[lg])
            self.v("tensor_scalar", A(P5_T2, [8]), A(C_C1B, [16])[:, 0:8],
                   A(C_M2RS + tt, [1]), None, op0=OP.mult)
            self.p.do("vector", lambda ve: ve.drain())
            self.v("tensor_sub", A(P5_T1, [8]), A(P5_T1, [8]), A(P5_T2, [8]))
            self.p.do("vector", lambda ve: ve.drain())
            t_lg = self.v("tensor_add", A(P5_LG, [8]), A(P5_T1, [8]),
                          A(C_C1B, [16])[:, 8:16], inc="tV")
            self.p.do("vector", lambda ve: ve.drain())
            t_m1 = self.v("reduce_max", A(P5_M1, [1]), A(P5_LG, [8]), axis=AX.X,
                          inc="tV")
            t_d1 = self.p.do("vector", lambda ve: ve.drain(), inc="tV")
            if tt == 0:
                self.dump("lg0", A(P5_LG, [8]), [t_d1])
            tm1 = self.v("tensor_scalar", A(P5_T1, [8]), A(P5_LG, [8]), A(P5_M1, [1]),
                   None, op0=OP.is_equal, wait=[t_d1], inc="tV")
            self.p.do("vector", lambda ve: ve.drain())
            if tt == 0:
                self.dump("msk1", A(P5_T1, [8]), [p.tok("tV")])
                self.dump("m1v", A(P5_M1, [1]), [p.tok("tV")])
            self.v("scalar_tensor_tensor", A(P5_T2, [8]), A(P5_T1, [8]), -1e30,
                   A(P5_LG, [8]), op0=OP.mult, op1=OP.add)
            self.p.do("vector", lambda ve: ve.drain())
            t_m2 = self.v("reduce_max", A(P5_M2V, [1]), A(P5_T2, [8]), axis=AX.X,
                          inc="tV")
            t_d2 = self.p.do("vector", lambda ve: ve.drain(), inc="tV")
            t_sb = self.v("tensor_sub", A(P5_W1N, [1]), A(P5_M1, [1]),
                          A(P5_M2V, [1]), wait=[t_d2], inc="tV")
            t_db = self.p.do("vector", lambda ve: ve.drain(), inc="tV")
            t_sg = self.act(A(P5_W1N, [1]), A(P5_W1N, [1]), AF.Sigmoid,
                            wait=[t_db], inc="tS")
            t_w2n = self.act(A(P5_W2N, [1]), A(P5_W1N, [1]), AF.Copy, scale=-1.0,
                             bias=1.0, wait=[t_sg], inc="tS")
            self.v("tensor_scalar", A(P5_T2, [8]), A(P5_T2, [8]), A(P5_M2V, [1]),
                   None, op0=OP.is_equal, wait=[t_w2n])
            self.p.do("vector", lambda ve: ve.drain())
            self.v("tensor_scalar", A(P5_T1, [8]), A(P5_T1, [8]), A(P5_W1N, [1]),
                   None, op0=OP.mult)
            self.v("tensor_scalar", A(P5_T2, [8]), A(P5_T2, [8]), A(P5_W2N, [1]),
                   None, op0=OP.mult)
            self.p.do("vector", lambda ve: ve.drain())
            t_g = self.v("tensor_add", A(XGATES + tt * 8, [8]), A(P5_T1, [8]),
                         A(P5_T2, [8]), inc="tV")
            p.do("tensor", None, wait=[t_g])
        self.dump("gates", A(XGATES, [16, 8]), [p.tok("tV")])

        for r in range(NRUN):
            tmb = self.mm(self.ps[:, 2, :], A(R_ONES, [128], pn=1),
                          A(P5_M2 + r * 512, [512], pn=1), True, True,
                          wait=[t_rs, p.tok("tV")], inc="tT")
            trb = self.mm(self.ps[:, 3, :], A(R_ONES, [128], pn=1),
                          A(P5_RS2 + r * 512, [512], pn=1), True, True, inc="tT")
            td = None
            for c in range(8):
                dst = av(self.ar, A2 + (c * NRUN + r) * 512, [512])
                self.v("tensor_sub", dst,
                       av(self.ar, A1 + (c * NRUN + r) * 512, [512]),
                       self.ps[:, 2, :], wait=([tmb, trb] if c == 0 else []))
                self.v("tensor_mul", dst, dst, self.ps[:, 3, :])
                td = self.v("tensor_scalar", dst, dst,
                            A(C_MOD + 32 + c, [1]), A(C_MOD + 24 + c, [1]),
                            op0=OP.mult, op1=OP.add, inc="tV")
            p.do("tensor", None, wait=[td])
        self.dump("h2", A(A2, [8, NRUN, 512]), [p.tok("tV")])

        t_b2 = p.dma("sync", av(self.ar, XB2GB, [1024], 0, 8), self.b2[:, :])
        t_gbr = p.dma("sync", A(P5_LG + 64, [1024], pn=1),
                      self.dscr[None, DS_GBROW:DS_GBROW + 1024])
        for hh in range(2):
            t_r1m = self.mm(self.ps[0:8, 6 + hh, :], A(R_ONES, [8], pn=1),
                            A(P5_LG + 64 + hh * 512, [512], pn=1), True, True,
                            wait=[t_gbr, t_b2], inc="tT")
            self.v("tensor_mul", av(self.ar, XB2GB + hh * 512, [512], 0, 8),
                   av(self.ar, XB2GB + hh * 512, [512], 0, 8),
                   self.ps[0:8, 6 + hh, :], wait=[t_r1m], inc="tV")

        for tt in range(16):
            r, sub = tt // 4, tt % 4
            t_tr = self.p.do("tensor", lambda t, tt=tt: t.transpose(
                self.ps[0:8, 5, 0:128], av(self.ar, XGATES + tt * 8, [8]),
                av(self.ar, C_ID, [128])), wait=[p.tok("tV")], inc="tT")
            t_gte = self.scopy(av(self.ar, XGT, [128], 0, 8), self.ps[0:8, 5, 0:128],
                               wait=[t_tr], inc="tS")
            tx = []
            for c in range(8):
                w_t = (tx[-1:] or []) + ([p.tok("tV"), p.tok("tS")] if c == 0 else [])
                t_t2 = self.p.do("tensor", lambda t, c=c, r=r, sub=sub: t.transpose(
                    self.ps[:, 4, 0:128],
                    av(self.ar, A1 + (c * NRUN + r) * 512 + sub * 128, [128]),
                    av(self.ar, C_ID, [128])), wait=w_t, inc="tT")
                t_ae = self.scopy(av(self.ar, A3 + tt * 1024 + c * 128, [128]),
                                  self.ps[:, 4, 0:128], wait=[t_t2], inc="tS")
                tx.append(t_ae)
            for hh in range(2):
                t_bm = self.mm(self.ps[:, 6, :], av(self.ar, XGT, [128], 0, 8),
                               av(self.ar, XB2GB + hh * 512, [512], 0, 8),
                               True, True, wait=[t_gte, p.tok("tV")], inc="tT")
                t_aa = self.v("tensor_add",
                              av(self.ar, A3 + tt * 1024 + hh * 512, [512]),
                              av(self.ar, A3 + tt * 1024 + hh * 512, [512]),
                              self.ps[:, 6, :], wait=[t_bm, tx[-1]], inc="tV")
                p.do("tensor", None, wait=[t_aa])
        p.barrier()
        self.dump("accinit", A(A3, [16, 1024]), [p.tok("tV")])

    def p6(self):
        p, A = self.p, self.A
        sub = self.stop_after if (self.stop_after or "").startswith("p6") else None
        FA2S, FB2TS, FFB, FGBB, FGBR = A1, A1 + 2048, A1 + 4096, A1 + 5120, A1 + 6144
        t_gbr = p.dma("sync", A(FGBR, [1024], pn=1),
                      self.dscr[None, DS_GBROW:DS_GBROW + 1024])
        if sub == "p6z":
            p.barrier()
            return
        t_gc = None
        for hh in range(2):
            t_m = self.mm(self.ps[:, 7, :], A(R_ONES, [128], pn=1),
                          A(FGBR + hh * 512, [512], pn=1), True, True,
                          wait=([t_gbr] if hh == 0 else [t_gc]), inc="tT")
            t_gc = self.v("tensor_copy", A(FGBB + hh * 512, [512]), self.ps[:, 7, :],
                          wait=[t_m], inc="tV")
        if sub == "p6g":
            p.barrier()
            return
        for e in (range(E) if sub != "p6f0" else [0]):
            t_a2 = p.dma("sync", av(self.ar, FA2S, [2048], 0, 16), self.a2[e])
            t_b2t = p.dma("sync", av(self.ar, FB2TS, [1024], 0, 16), self.b2t[e])
            for i in range(32 if sub != "p6f0" else 2):
                hc, doh = i // 2, i % 2
                fb = FFB + (i % 2) * 512
                t_f = self.mm(self.ps[:, 4, :],
                              av(self.ar, FA2S + hc * 128, [128], 0, 16),
                              av(self.ar, FB2TS + doh * 512, [512], 0, 16),
                              True, True, wait=[t_a2, t_b2t, p.tok("dG"),
                                                p.tok("dS")], inc="tT")
                t_fe = self.scopy(A(fb, [512]), self.ps[:, 4, :], wait=[t_f],
                                  inc="tS")
                t_acc = p.dma("gpsimd", A(fb, [512]),
                              self.w2t[e, hc * 128:(hc + 1) * 128,
                                       doh * 512:(doh + 1) * 512],
                              wait=[t_fe], accum=True)
                t_gb = self.v("tensor_mul", A(fb, [512]), A(fb, [512]),
                              A(FGBB + doh * 512, [512]),
                              wait=[t_acc, p.tok("tV")], inc="tV")
                p.dma("sync", self.w2scr[e, hc * 128:(hc + 1) * 128,
                                         doh * 512:(doh + 1) * 512],
                      A(fb, [512]), wait=[t_gb])
        p.barrier()
        if sub == "p6f":
            return

        for tg in (range(NRUN) if sub in (None, "p6") else [0]):
            for (src, uoff, bank) in ((self.a1t, P6_U1, 5), (self.a3t, P6_U3, 6)):
                t_as = p.dma("sync", A(P6_A1S if uoff == P6_U1 else P6_A3S, [8, 128]),
                             src.rearrange("(c p) m -> p c m", p=128))
                lu = None
                for c in range(8):
                    lu = self.mm(self.ps[:, bank, :],
                                 A((P6_A1S if uoff == P6_U1 else P6_A3S), [8, 128])[:, c, :],
                                 av(self.ar, A2 + (c * NRUN + tg) * 512, [512]),
                                 c == 0, c == 7, wait=([t_as] if c == 0 else []),
                                 inc=("tT" if c == 7 else None))
                t_ue = self.v("tensor_copy", A(uoff, [512]), self.ps[:, bank, :],
                              wait=[lu], inc="tV")
                t_ust = p.dma("sync",
                              bass.AP(self.dscr.tensor,
                                      (DS_U1 if uoff == P6_U1 else DS_U3),
                                      [[512, 128], [1, 512]]),
                              A(uoff, [512]), wait=[t_ue])
                if uoff == P6_U1:
                    t_u1st = t_ust
                else:
                    t_u3st = t_ust
            for e in (range(E) if sub != "p6e0" else [0]):
                t_u1 = p.dma("sync", av(self.ar, P6_U1, [512], 0, 16),
                             bass.AP(self.dscr.tensor, DS_U1 + e * 16 * 512,
                                     [[512, 16], [1, 512]]),
                             wait=[p.tok("tT"), t_u1st])
                t_u3 = p.dma("sync", av(self.ar, P6_U3, [512], 0, 16),
                             bass.AP(self.dscr.tensor, DS_U3 + e * 16 * 512,
                                     [[512, 16], [1, 512]]),
                             wait=[t_u3st])
                for hg in range(8):
                    w1r = self.w1t.rearrange("e (c p) m -> e p c m", p=128)
                    w3r = self.w3t.rearrange("e (c p) m -> e p c m", p=128)
                    t_w1 = p.dma("sync", A(P6_W1S, [8, 2, 128]),
                                 w1r[e][:, :, hg * 256:(hg + 1) * 256]
                                 .rearrange("p c (h f) -> p c h f", h=2),
                                 wait=[p.tok("tT")])
                    t_w3 = p.dma("sync", A(P6_W3S, [8, 2, 128]),
                                 w3r[e][:, :, hg * 256:(hg + 1) * 256]
                                 .rearrange("p c (h f) -> p c h f", h=2))
                    t_bb1 = p.dma("sync", av(self.ar, P6_B1T, [256], 0, 16),
                                  self.b1t[e][:, hg * 256:(hg + 1) * 256])
                    t_bb3 = p.dma("sync", av(self.ar, P6_B3T, [256], 0, 16),
                                  self.b3t[e][:, hg * 256:(hg + 1) * 256])
                    for hcs in range(2):
                        hc = hg * 2 + hcs
                        l1 = l3 = None
                        for c in range(8):
                            w = [t_w1, t_w3, t_bb1, t_bb3, t_u1, t_u3] if c == 0 else []
                            l1 = self.mm(self.ps[:, 0, :],
                                         A(P6_W1S, [8, 2, 128])[:, c, hcs, :],
                                         av(self.ar, A2 + (c * NRUN + tg) * 512, [512]),
                                         c == 0, False, wait=w)
                        l1 = self.mm(self.ps[:, 0, :],
                                     av(self.ar, P6_B1T + hcs * 128, [128], 0, 16),
                                     av(self.ar, P6_U1, [512], 0, 16),
                                     False, True, inc="tT")
                        for c in range(8):
                            l3 = self.mm(self.ps[:, 1, :],
                                         A(P6_W3S, [8, 2, 128])[:, c, hcs, :],
                                         av(self.ar, A2 + (c * NRUN + tg) * 512, [512]),
                                         c == 0, False)
                        l3 = self.mm(self.ps[:, 1, :],
                                     av(self.ar, P6_B3T + hcs * 128, [128], 0, 16),
                                     av(self.ar, P6_U3, [512], 0, 16),
                                     False, True, inc="tT")
                        t_si = self.act(A(P6_G + hc * 512, [512]), self.ps[:, 0, :],
                                        AF.Silu, bias=A(C_B1 + e * 16 + hc, [1]),
                                        wait=[l1], inc="tS")
                        t_g = self.v("scalar_tensor_tensor", A(P6_G + hc * 512, [512]),
                                     self.ps[:, 1, :], A(C_B3 + e * 16 + hc, [1]),
                                     A(P6_G + hc * 512, [512]),
                                     op0=OP.add, op1=OP.mult,
                                     wait=[l3, t_si], inc="tV")
                        p.do("tensor", None, wait=[t_g])
                for doh in range(2):
                    lys = [None] * 4
                    t_mprev = None
                    for hc in range(16):
                        t_wm = p.dma("sync", A(P6_W2M, [512]),
                                     self.w2scr[e, hc * 128:(hc + 1) * 128,
                                                doh * 512:(doh + 1) * 512],
                                     wait=([p.tok("tT")] if hc == 0
                                           else [t_mprev]))
                        for tt in range(4):
                            lys[tt] = self.mm(self.ps[:, 2 + tt, :],
                                              av(self.ar, P6_G + hc * 512 + tt * 128,
                                                 [128]),
                                              A(P6_W2M, [512]), hc == 0, hc == 15,
                                              wait=([t_wm, p.tok("tV")] if tt == 0 else []),
                                              inc="tT")
                        t_mprev = lys[3]
                    for tt in range(4):
                        gtt = tg * 4 + tt
                        t_y = self.v("scalar_tensor_tensor",
                                     av(self.ar, A3 + gtt * 1024 + doh * 512, [512]),
                                     self.ps[:, 2 + tt, :],
                                     A(XGATES + gtt * 8 + e, [1]),
                                     av(self.ar, A3 + gtt * 1024 + doh * 512, [512]),
                                     op0=OP.mult, op1=OP.add,
                                     wait=[lys[tt]], inc="tV")
                    p.do("tensor", None, wait=[p.tok("tV")])
        p.barrier()

    def p7(self):
        p, A = self.p, self.A
        for tt in range(16):
            self.p.dma("sync", self.out[tt * 128:(tt + 1) * 128, :],
                       av(self.ar, A3 + tt * 1024, [1024]),
                       wait=[p.tok("tV"), p.tok("tT")])
        tok = p.tok("dS")
        for e in ENGS:
            p.do(e, None, wait=[tok])

    def build(self):
        self.declare_io()
        es = contextlib.ExitStack()
        with es:
            self.ar = es.enter_context(self.nc.sbuf_tensor("arena", [128, ARN], F32))
            self.ps = es.enter_context(self.nc.psum_tensor("ps", [128, 8, 512], F32))
            self.itab = es.enter_context(self.nc.sbuf_tensor("itab", [1, 256], I32))
            sems = {s: es.enter_context(self.nc.semaphore(s)) for s in self.p.cnt}
            block = es.enter_context(self.nc.Block())
            self._cm = self.nc.allow_non_contiguous_dma(reason="small strided pp/row loads")
            phases = ["p0", "p1", "p2", "p3", "p4", "p5", "p6"]
            if self.stop_after is not None:
                sa = "p6" if self.stop_after.startswith("p6") else self.stop_after
                phases = phases[:phases.index(sa) + 1]
            for ph in phases:
                getattr(self, ph)()
            self.p7()
            with self._cm:
                self.p.emit(block, sems)
        return self.nc


def _half_runs(half):
    return RUNS_A if half == 0 else RUNS_B


def _stage_shared(i):
    s = {}
    s["wmod"] = np.ascontiguousarray(i["L0_wmod"])
    s["qkvwt"] = np.ascontiguousarray(i["L0_attn_qkv_w"].T)
    s["qkvb"] = np.ascontiguousarray(i["L0_attn_qkv_b"])
    s["wwt"] = np.ascontiguousarray(np.concatenate(
        [i["L0_attn_w1w"].T, i["L0_attn_w2w"].T,
         i["L0_attn_r1w"].T, i["L0_attn_r2w"].T], axis=1))
    s["memvwt"] = np.ascontiguousarray(i["L0_attn_memv_w"].T)
    s["memvb"] = np.ascontiguousarray(i["L0_attn_memv_b"])
    s["memgwt"] = np.ascontiguousarray(i["L0_attn_memg_w"].T)
    s["memgb"] = np.ascontiguousarray(i["L0_attn_memg_b"])
    s["mscale"] = np.ascontiguousarray(i["L0_attn_mem_scale"])
    s["imix"] = np.ascontiguousarray(np.asarray(i["L0_attn_iter_mix"]).reshape(1))
    s["outwt"] = np.ascontiguousarray(i["L0_attn_out_w"].T)
    s["outb"] = np.ascontiguousarray(i["L0_attn_out_b"])
    s["dlog"] = np.ascontiguousarray(i["L0_attn_decay_logits"])
    s["wgt"] = np.ascontiguousarray(i["L0_ffn_Wg"].T)
    s["w1t"] = np.ascontiguousarray(i["L0_ffn_W1"].transpose(0, 2, 1))
    s["w2t"] = np.ascontiguousarray(i["L0_ffn_W2"].transpose(0, 2, 1))
    s["w3t"] = np.ascontiguousarray(i["L0_ffn_W3"].transpose(0, 2, 1))
    s["b1"] = np.ascontiguousarray(i["L0_ffn_b1"])
    s["b2"] = np.ascontiguousarray(i["L0_ffn_b2"])
    s["b3"] = np.ascontiguousarray(i["L0_ffn_b3"])
    s["a1t"] = np.ascontiguousarray(
        i["L0_ffn_A1"].transpose(2, 0, 1).reshape(D, E * R))
    s["a3t"] = np.ascontiguousarray(
        i["L0_ffn_A3"].transpose(2, 0, 1).reshape(D, E * R))
    s["b1t"] = np.ascontiguousarray(i["L0_ffn_B1"].transpose(0, 2, 1))
    s["b3t"] = np.ascontiguousarray(i["L0_ffn_B3"].transpose(0, 2, 1))
    s["a2"] = np.ascontiguousarray(i["L0_ffn_A2"])
    s["b2t"] = np.ascontiguousarray(i["L0_ffn_B2"].transpose(0, 2, 1))
    return s


def _stage_core(i, shared, core):
    seq, half = core // 2, core % 2
    runs = _half_runs(half)
    x = np.asarray(i["L0_attn_x"])[seq]
    xt = np.ascontiguousarray(x.T).reshape(8, 128, T).transpose(1, 0, 2)
    xg = np.ascontiguousarray(xt)
    xo = np.stack([xg[:, :, rb * 128: rb * 128 + RUNTOK] for rb in runs], axis=2)
    xo = np.ascontiguousarray(xo)
    crow = np.ascontiguousarray(np.asarray(i["L0_c"])[seq])
    msel = np.zeros(NRUN * 32, np.int32)
    maskt = np.zeros((32, 128, 512), np.float32)
    colv = np.arange(512)[None, :]
    rowv = np.arange(128)[:, None]
    for r in range(NRUN):
        qb0 = runs[r]
        kl = KLEN[r]
        for idx in range(8):
            kt = kl - 8 + idx
            d = qb0 - kt
            maskt[r * 8 + idx] = (d * 128 + colv >= rowv).astype(np.float32)
    roff = np.zeros(8, np.int32)
    for r in range(NRUN):
        roff[r] = P4_MSC + runs[r] * 16
    m = dict(shared)
    m.update({"xg": xg, "xo": xo, "crow": crow, "masksel": msel, "roff_": roff,
              "runoff": roff, "maskt": maskt})
    m.pop("roff_")
    return m


_NC_CACHE = {}


def _get_nc(dbg=(), stop_after=None):
    key = (tuple(sorted(dict(dbg).items())), stop_after)
    if key not in _NC_CACHE:
        kb = KB(dbg, stop_after)
        _NC_CACHE[key] = (kb.build(), kb)
    return _NC_CACHE[key]


def kernel(**inputs):
    nc, kb = _get_nc()
    shared = _stage_shared(inputs)
    in_maps = [_stage_core(inputs, shared, core) for core in range(8)]
    res = run_bass_kernel_spmd(nc, in_maps, core_ids=list(range(8)))
    full = np.zeros((B, T, D), np.float32)
    for core in range(8):
        seq, half = core // 2, core % 2
        runs = _half_runs(half)
        o = np.asarray(res.results[core]["out"])
        for r, rb in enumerate(runs):
            full[seq, rb * 128: rb * 128 + RUNTOK] = o[r * RUNTOK:(r + 1) * RUNTOK]
    return full


# version 02
# speedup vs baseline: 1.2371x; 1.2371x over previous
import contextlib
import numpy as np

import concourse.bass as bass
import concourse.mybir as mybir
from concourse.bass_utils import run_bass_kernel_spmd

F32 = mybir.dt.float32
I32 = mybir.dt.int32
AF = mybir.ActivationFunctionType
OP = mybir.AluOpType
AX = mybir.AxisListType

B, T, D, H, DH, PD = 4, 4096, 1024, 16, 64, 6
HID, E, R = 2048, 8, 16
NB = T // 128
NRUN, RUNTOK = 4, 512
OWN = NRUN * RUNTOK
KLEN = [8, 16, 24, 32]
SCALE = 0.125
EPS = 1e-5
RUNS_A = [0, 8, 20, 28]
RUNS_B = [4, 12, 16, 24]
NPAIR = 8
SPAD = 8

ENGS = ("sync", "tensor", "vector", "scalar", "gpsimd")
ARN = 52736

A1 = 0
A2 = 16384
A3 = 32768
XT = 49152

XC = XT
XROW = XT + 640
XB2GB = XT + 1024
XGT = XT + 2048
XMONE = XT + 2176
C_CIHS = XT + 2176
XMZERO = XT + 2688
XGATES = XT + 3200
XSPARE = XT + 3328

C_UCL = XC + 0
C_CIH = XC + 128
C_CFW = XC + 144
C_CF2 = XC + 160
C_MSB = XC + 176
C_MOD = XC + 192
C_QB = XC + 240
C_OB = XC + 264
C_MVB = XC + 272
C_B1 = XC + 280
C_B3 = XC + 408
C_APM = XC + 536
C_WGP = XC + 538
C_C1B = XC + 602
C_C2B = XC + 610
C_RS2C = XC + 618
C_M2RS = XC + 634
XROW = XT + 660

R_ONES = XROW + 0
R_IOTA = XROW + 128
R_LND = XROW + 256
R_C128 = XROW + 272
R_MGB = XROW + 288
R_MSC = XROW + 304
R_IMX = XROW + 320
R_DSG = XROW + 321
assert R_DSG + 16 <= XB2GB

P0_MODROW = A2
P0_WMB = A2 + 6144
P0_CSC = A2 + 10240

P1_XB = A2
P1_HB = A2 + 4096
P1_MROW = A2 + 8192
P1_SROW = A2 + 12288

P2_WB = A2
P2_W1L = A2 + 2112
P2_W2L = A2 + 2176
P2_R1L = A2 + 2240
P2_R2L = A2 + 2304
P2_LW = A2 + 2368
P2_SQ = A2 + 2464
P2_N2 = A2 + 2560
P2_JWS = A2 + 2576
P2_XP = A2 + 2672
P2_RD = A2 + 3248
P2_PP = A2 + 3344
P2_RDM = A2 + 3920
P2_S1 = A2 + 4016
P2_S2 = A2 + 4032
P2_S3 = A2 + 4048
P2_CROW = A2 + 4096
P2_TROW = A2 + 4672
P2_MSCB = A2 + 5248
P2_GT = A2 + 5264

P3_KT = A3
P3_VE = A3 + 4096
P3_QT = A3 + 8256
P3_PB = A3 + 8768
P3_HF = A3 + 9280
P3_WS = A3 + 11328
P3_OS = A3 + 13376
P3_ATL = A3 + 13888

P4_XO = A3
P4_MSC = A3 + 4096
P4_GATE = A3 + 4608
P4_GROW = A3 + 4864
P4_GCOL = A3 + 6912
P4_TMP = A3 + 6928

P5_XSQ = A3
P5_M2 = A3 + 4096
P5_RS2 = A3 + 6144
P5_LG = A3 + 8192
P5_T1 = A3 + 8320
P5_T2 = A3 + 8448
P5_M1 = A3 + 8576
P5_M2V = A3 + 8592
P5_W1N = A3 + 8608
P5_W2N = A3 + 8624

P6_G = A1
P6_W1S = A1 + 8192
P6_W3S = A1 + 10240
P6_W2M = A1 + 12288
P6_U1 = A1 + 12800
P6_U3 = A1 + 13312
P6_A1S = A1 + 13824
P6_A3S = A1 + 14848
P6_W1SA = XT + 1152
P6_W3SA = A1 + 13824
P6_W2MA = XT + 640
P6_B1T = A1 + 15872
P6_B3T = A1 + 16128

C_ID = XSPARE
C_ONEC = XSPARE + 128
C_DUM = XSPARE + 192

DS_MOD = 0
DS_GBROW = 6144
DS_MSC = 8192
DS_GATE = 73728
DS_GROW = 106496
DS_ROW = 108544
DS_U1 = 112640
DS_U3 = 178176
DS_END = 243712


def av(handle, off, shape, p0=0, pn=128):
    n = int(np.prod(shape))
    full = handle[p0:p0 + pn, off:off + n]
    if len(shape) == 1:
        return full
    spec = " ".join(f"s{i}" for i in range(len(shape)))
    return full.rearrange(f"p ({spec}) -> p {spec}",
                          **{f"s{i}": shape[i] for i in range(len(shape))})


class PB:
    def __init__(self):
        self.steps = {e: [] for e in ENGS}
        self.cnt = {s: 0 for s in ("dS", "dG", "tT", "tV", "tS", "tG")}

    def tok(self, sem):
        return (sem, self.cnt[sem])

    def do(self, eng, fn=None, wait=(), inc=None, incn=1, is_dma=False):
        import traceback
        waits = [(s, v) for (s, v) in wait if v > 0]
        tokn = None
        if eng == "gpsimd" and fn is not None and not is_dma:
            if getattr(self, "_gprev", None):
                waits.append(self._gprev)
            if inc is None:
                inc = "tG"
        if inc is not None:
            self.cnt[inc] += incn
            tokn = (inc, self.cnt[inc])
        if eng == "gpsimd" and fn is not None and not is_dma:
            self._gprev = tokn
        site = traceback.extract_stack(limit=4)
        self.steps[eng].append((fn, waits, inc, incn, site))
        return tokn

    def dma(self, q, out, in_, wait=(), accum=False):
        sem = "dS" if q == "sync" else "dG"
        kw = {"accum_op": OP.add} if accum else {}
        return self.do(q, lambda e, o=out, i=in_, kw=kw: e.dma_start(out=o, in_=i, **kw),
                       wait=wait, inc=sem, incn=16, is_dma=True)

    def barrier(self):
        toks = [self.tok(s) for s in self.cnt]
        for e in ENGS:
            self.do(e, None, wait=toks)

    def emit(self, block, sems):
        def mk(name):
            def run(eng):
                for (fn, waits, inc, incn, site) in self.steps[name]:
                    for (s, v) in waits:
                        eng.wait_ge(sems[s], v)
                    try:
                        ins = fn(eng) if fn is not None else None
                    except Exception as ex:
                        raise RuntimeError(
                            "step failed, created at:\n" +
                            "".join(x for x in map(str, site[:-1]))) from ex
                    if inc is not None:
                        ins.then_inc(sems[inc], incn)
            return run
        block.sync(mk("sync"))
        block.tensor(mk("tensor"))
        block.vector(mk("vector"))
        block.scalar(mk("scalar"))
        block.gpsimd(mk("gpsimd"))


class KB:
    def __init__(self, dbg=(), stop_after=None):
        self.stop_after = stop_after
        self.nc = bass.Bass("TRN2", target_bir_lowering=False)
        self.p = PB()
        self.dbg = dict(dbg)
        self.dbg_out = {}
        self.inp = {}

    def A(self, off, shape, p0=0, pn=128):
        return av(self.ar, off, shape, p0, pn)

    def mm(self, outap, lhsT, rhs, start, stop, wait=(), inc=None, tp=None):
        def f(t, o=outap, l=lhsT, r=rhs, st=start, sp=stop, tp=tp):
            kw = {} if tp is None else {"tile_position": tp}
            return t.matmul(o, l, r, start=st, stop=sp, **kw)
        return self.p.do("tensor", f, wait=wait, inc=inc)

    def act(self, outap, inap, func, wait=(), inc=None, bias=0.0, scale=1.0,
            accum=None):
        def f(s, o=outap, i=inap, fu=func, b=bias, sc=scale, a=accum):
            kw = {} if a is None else {"accum_out": a}
            return s.activation(out=o, in_=i, func=fu, bias=b, scale=sc, **kw)
        return self.p.do("scalar", f, wait=wait, inc=inc)

    def scopy(self, outap, inap, wait=(), inc=None):
        return self.p.do("scalar", lambda s, o=outap, i=inap: s.copy(o, i),
                         wait=wait, inc=inc)

    def v(self, fname, *args, wait=(), inc=None, **kw):
        return self.p.do("vector",
                         lambda ve, fn=fname, a=args, k=kw: getattr(ve, fn)(*a, **k),
                         wait=wait, inc=inc)

    def g(self, fname, *args, wait=(), inc=None, **kw):
        return self.p.do("gpsimd",
                         lambda ge, fn=fname, a=args, k=kw: getattr(ge, fn)(*a, **k),
                         wait=wait, inc=inc)

    def dump(self, name, src_ap, wait):
        if name in self.dbg:
            self.p.dma("sync", self.dbg_out[name], src_ap, wait=wait)

    def declare_io(self):
        nc = self.nc

        def di(name, shape, dt=F32):
            self.inp[name] = nc.dram_tensor(name, list(shape), dt,
                                            kind="ExternalInput").ap()
            return self.inp[name]

        self.xg = di("xg", [128, 8, T])
        self.xo = di("xo", [128, 8, NRUN, RUNTOK])
        self.crow = di("crow", [D])
        self.wmod = di("wmod", [D, 6 * D])
        self.qkvwt = di("qkvwt", [D, 3 * D])
        self.qkvb = di("qkvb", [3 * D])
        self.wwt = di("wwt", [D, 256])
        self.memvwt = di("memvwt", [D, D])
        self.memvb = di("memvb", [D])
        self.memgwt = di("memgwt", [D, H])
        self.memgb = di("memgb", [H])
        self.mscale = di("mscale", [H])
        self.imix = di("imix", [1])
        self.outwt = di("outwt", [D, D])
        self.outb = di("outb", [D])
        self.dlog = di("dlog", [H])
        self.wgt = di("wgt", [D, E])
        self.w1t = di("w1t", [E, D, HID])
        self.w2t = di("w2t", [E, HID, D])
        self.w3t = di("w3t", [E, D, HID])
        self.b1 = di("b1", [E, HID])
        self.b2 = di("b2", [E, D])
        self.b3 = di("b3", [E, HID])
        self.a1t = di("a1t", [D, E * R])
        self.a3t = di("a3t", [D, E * R])
        self.b1t = di("b1t", [E, R, HID])
        self.b3t = di("b3t", [E, R, HID])
        self.a2 = di("a2", [E, R, HID])
        self.b2t = di("b2t", [E, R, D])
        self.masksel = di("masksel", [NRUN * 32], I32)
        self.maskt = di("maskt", [32, 128, 512])
        self.runoff = di("runoff", [8], I32)

        self.out = nc.dram_tensor("out", [OWN, D], F32, kind="ExternalOutput").ap()
        for name, shape in self.dbg.items():
            self.dbg_out[name] = nc.dram_tensor(
                "dbg_" + name, list(shape), F32, kind="ExternalOutput").ap()

        self.hsp = nc.dram_tensor("hsp", [128, 8, SPAD + T], F32, kind="Internal").ap()
        self.dscr = nc.dram_tensor("dscr", [1 << 18], F32, kind="Internal").ap()
        self.w2scr = nc.dram_tensor("w2scr", [E, HID, D], F32, kind="Internal").ap()


    def p0(self):
        p, A = self.p, self.A
        p.dma("sync", A(C_QB, [24]), self.qkvb.rearrange("(c p) -> p c", p=128))
        p.dma("sync", A(C_OB, [8]), self.outb.rearrange("(c p) -> p c", p=128))
        p.dma("sync", A(C_MVB, [8]), self.memvb.rearrange("(c p) -> p c", p=128))
        p.dma("sync", A(C_B1, [8, 16]), self.b1.rearrange("e (c p) -> p e c", p=128))
        p.dma("sync", A(C_B3, [8, 16]), self.b3.rearrange("e (c p) -> p e c", p=128))
        t_r2 = p.dma("sync", A(R_MGB, [16], pn=1), self.memgb[None, :])
        t_r3 = p.dma("sync", A(R_DSG, [16], pn=1), self.dlog[None, :])
        t_r4 = p.dma("sync", A(R_MSC, [16], pn=1), self.mscale[None, :])
        t_r5 = p.dma("sync", A(R_IMX, [1], pn=1), self.imix[None, :])
        t_cc = p.dma("sync", A(P0_CSC, [8, 1]),
                     self.crow.rearrange("(c p) -> p c", p=128)[:, :, None])
        p.dma("sync", self.itab[0:1, 0:128], self.masksel[None, :])
        p.dma("sync", self.itab[0:1, 128:136], self.runoff[None, :])

        self.g("memset", A(C_UCL, [128]), 1.0)
        self.g("affine_select", A(C_UCL, [128]), A(C_UCL, [128]),
               pattern=[[1, 128]], compare_op=OP.is_gt, fill=0.0, base=0,
               channel_multiplier=-1)
        self.g("iota", A(R_IOTA, [128], pn=1), pattern=[[1, 128]], base=0,
               channel_multiplier=0, allow_small_or_imprecise_dtypes=True)

        self._slot = [A3 + 8000]
        def _stage_dump(name, off, n, waits, pn=1, src_ap=None):
            slot = self._slot[0]
            self._slot[0] += n
            src = src_ap if src_ap is not None else A(off, [n], pn=pn)
            tcpx = self.v("tensor_copy", A(slot, [n], pn=pn), src,
                          wait=waits, inc="tV")
            self.dump(name, A(slot, [n], pn=pn), [tcpx])
        self.g("memset", A(C_ID, [128]), 1.0)
        self.g("affine_select", A(C_ID, [128]), A(C_ID, [128]),
               pattern=[[1, 128]], compare_op=OP.is_equal, fill=0.0, base=0,
               channel_multiplier=-1)
        self.g("memset", A(C_ONEC, [1]), 1.0)
        self.g("memset", A(R_ONES, [128], p0=64, pn=1), 1.0)
        t_g = self.g("memset", A(R_ONES, [128], pn=1), 1.0, inc="tG")

        t_dsg = self.act(A(R_DSG, [16], pn=1), A(R_DSG, [16], pn=1), AF.Sigmoid,
                         wait=[t_r3], inc="tS")
        t_lnd = self.act(A(R_LND, [16], pn=1), A(R_DSG, [16], pn=1), AF.Ln,
                         wait=[t_dsg], inc="tS")
        t_c128 = self.act(A(R_C128, [16], pn=1), A(R_LND, [16], pn=1), AF.Exp,
                          scale=128.0, wait=[t_lnd], inc="tS")
        t_al = self.act(A(R_IMX, [1], pn=1), A(R_IMX, [1], pn=1), AF.Sigmoid,
                        wait=[t_r5], inc="tS")
        t_m0 = self.mm(self.ps[:, 0, 0:1], A(R_ONES, [128], pn=1),
                       A(R_IMX, [1], pn=1), True, True, wait=[t_al, t_g], inc="tT")
        t_a1 = self.scopy(A(C_APM, [1]), self.ps[:, 0, 0:1], wait=[t_m0], inc="tS")
        t_a2 = self.act(A(C_APM + 1, [1]), self.ps[:, 0, 0:1], AF.Copy, scale=-1.0,
                        bias=1.0, wait=[t_m0], inc="tS")
        t_m1 = self.mm(self.ps[:, 1, 0:16], A(R_IOTA, [128], pn=1),
                       A(R_LND, [16], pn=1), True, True,
                       wait=[t_lnd, t_g, self.p.tok("tV")], inc="tT")
        t_ci = self.act(A(C_CIH, [16]), self.ps[:, 1, 0:16], AF.Exp, scale=-0.5,
                        wait=[t_m1], inc="tS")
        self.act(A(C_CFW, [16]), self.ps[:, 1, 0:16], AF.Exp, scale=1.0, inc="tS")
        self.act(A(C_CF2, [16]), self.ps[:, 1, 0:16], AF.Exp, scale=2.0, inc="tS")
        t_m2 = self.mm(self.ps[:, 2, 0:16], A(R_ONES, [128], pn=1),
                       A(R_MSC, [16], pn=1), True, True, wait=[t_r4, t_g], inc="tT")
        self.scopy(A(C_MSB, [16]), self.ps[:, 2, 0:16], wait=[t_m2], inc="tS")
        t_sg1 = self.g("memset", A(C_CIHS, [16, 6]), 1.0)
        t_sg2 = self.v("tensor_scalar_mul",
                       bass.AP(self.ar, C_CIHS + 1, [[ARN, 128], [6, 16], [3, 2]]),
                       bass.AP(self.ar, C_CIHS + 1, [[ARN, 128], [6, 16], [3, 2]]),
                       -1.0, wait=[t_sg1], inc="tV")
        t_sg3 = self.p.do("vector", lambda ve: ve.drain(), inc="tV")
        cihb0 = bass.AP(self.ar, C_CIH, [[ARN, 128], [1, 16], [0, 6]])
        t_cihs = self.v("tensor_mul", A(C_CIHS, [16, 6]), A(C_CIHS, [16, 6]),
                        cihb0, wait=[t_sg3, t_ci], inc="tV")
        _stage_dump("cihsf", C_CIHS, 96, [t_cihs], pn=128)

        t_sc = self.act(A(P0_CSC, [8, 1]), A(P0_CSC, [8, 1]), AF.Silu,
                        wait=[t_cc], inc="tS")
        prev = None
        for j in range(12):
            tw = p.dma("sync", A(P0_WMB, [8, 512]),
                       self.wmod.rearrange("(c p) m -> p c m", p=128)
                       [:, :, j * 512:(j + 1) * 512], wait=[p.tok("tT")])
            last = None
            for cc in range(8):
                w = [tw, t_sc] + ([prev] if (prev and cc == 0) else [])
                last = self.mm(self.ps[0:1, 3, :], A(P0_CSC + cc, [1]),
                               A(P0_WMB + cc * 512, [512]), cc == 0, cc == 7,
                               wait=w, inc=("tT" if cc == 7 else None))
            prev = self.scopy(A(P0_MODROW + j * 512, [512], pn=1),
                              self.ps[0:1, 3, :], wait=[last], inc="tS")
        t_md = p.dma("sync", self.dscr[None, DS_MOD:DS_MOD + 6144],
                     A(P0_MODROW, [6144], pn=1), wait=[prev])
        for vv in range(6):
            src = bass.AP(self.dscr.tensor, DS_MOD + vv * 1024,
                          [[1, 128], [128, 8]])
            p.dma("sync", A(C_MOD + 8 * vv, [8]), src, wait=[t_md])
        t_ml = p.tok("dS")
        tmp = P0_WMB
        for vv in (2, 5):
            gv = A(C_MOD + 8 * vv, [8])
            self.v("tensor_scalar_max", A(tmp, [8]), gv, 0.25, wait=[t_ml])
            self.v("tensor_scalar_min", A(tmp + 8, [8]), gv, -0.25)
            self.v("tensor_scalar", A(tmp + 16, [8]), gv, 0.0, None, op0=OP.is_ge)
            self.p.do("vector", lambda ve: ve.drain())
            self.v("tensor_sub", A(tmp, [8]), A(tmp, [8]), A(tmp + 8, [8]))
            self.p.do("vector", lambda ve: ve.drain())
            self.v("tensor_mul", A(tmp, [8]), A(tmp + 16, [8]), A(tmp, [8]))
            self.p.do("vector", lambda ve: ve.drain())
            self.v("tensor_add", gv, A(tmp, [8]), A(tmp + 8, [8]), inc="tV")
            self.p.do("vector", lambda ve: ve.drain())
        for vv in (1, 4):
            gv = A(C_MOD + 8 * vv, [8])
            self.v("tensor_scalar_add", gv, gv, 1.0, inc="tV")
        t_mod = p.tok("tV")
        p.dma("sync", bass.AP(self.dscr.tensor, DS_GBROW, [[1, 128], [128, 8]]),
              A(C_MOD + 8 * 5, [8]), wait=[t_mod])
        p.barrier()

    def ln_h_pass(self, src_dram, nchunk, mrow, srow, out_spill, own):
        p, A = self.p, self.A
        for ch in range(nchunk):
            tx = p.dma("sync", A(P1_XB, [8, 512]), src_dram[:, :, ch * 512:(ch + 1) * 512],
                       wait=[p.tok("tS"), p.tok("tT")])
            tsq = []
            for cc in range(8):
                tsq.append(self.act(A(P1_HB + cc * 512, [512]),
                                    A(P1_XB + cc * 512, [512]), AF.Square,
                                    wait=[tx] if cc == 0 else [], inc="tS"))
            lastm = lasts = None
            for cc in range(8):
                w = [tx] if cc == 0 else []
                lastm = self.mm(self.ps[0:1, 0, :], A(C_ONEC, [1]),
                                A(P1_XB + cc * 512, [512]),
                                cc == 0, cc == 7, wait=w,
                                inc=("tT" if cc == 7 else None))
            for cc in range(8):
                w = [tsq[cc]] if cc == 0 else [tsq[cc]]
                lasts = self.mm(self.ps[0:1, 1, :], A(C_ONEC, [1]),
                                A(P1_HB + cc * 512, [512]), cc == 0, cc == 7,
                                wait=w, inc=("tT" if cc == 7 else None))
            self.act(A(mrow + ch * 512, [512], pn=1), self.ps[0:1, 0, :], AF.Copy,
                     scale=1.0 / D, wait=[lastm], inc="tS")
            t_ev = self.act(A(srow + ch * 512, [512], pn=1), self.ps[0:1, 1, :],
                            AF.Copy, scale=1.0 / D, wait=[lasts], inc="tS")
            p.do("tensor", None, wait=[t_ev, p.tok("tS")])
        n = nchunk * 512
        tv1 = self.act(A(P1_HB, [n], pn=1), A(mrow, [n], pn=1), AF.Square,
                       wait=[p.tok("tS"), p.tok("tV"), p.tok("dG")], inc="tS")
        tv2 = self.v("tensor_sub", A(srow, [n], pn=1), A(srow, [n], pn=1),
                     A(P1_HB, [n], pn=1), wait=[tv1])
        tv2 = self.v("tensor_scalar_add", A(srow, [n], pn=1), A(srow, [n], pn=1),
                     EPS, inc="tV")
        tv3 = self.act(A(srow, [n], pn=1), A(srow, [n], pn=1), AF.Sqrt,
                       wait=[tv2], inc="tS")
        tv4 = self.v("reciprocal", A(srow, [n], pn=1), A(srow, [n], pn=1),
                     wait=[tv3], inc="tV")
        t_rs = self.p.do("vector", lambda ve: ve.drain(), wait=[tv4], inc="tV")
        for ch in range(nchunk):
            tx = p.dma("sync", A(P1_XB, [8, 512]), src_dram[:, :, ch * 512:(ch + 1) * 512],
                       wait=[p.tok("tV")])
            tmb = self.mm(self.ps[:, 2, :], A(R_ONES, [128], pn=1),
                          A(mrow + ch * 512, [512], pn=1), True, True,
                          wait=[t_rs, p.tok("tV")], inc="tT")
            trb = self.mm(self.ps[:, 3, :], A(R_ONES, [128], pn=1),
                          A(srow + ch * 512, [512], pn=1), True, True, inc="tT")
            tdone = []
            for cc in range(8):
                w = [tx, tmb, trb] if cc == 0 else []
                self.v("tensor_sub", A(P1_HB + cc * 512, [512]),
                       A(P1_XB + cc * 512, [512]), self.ps[:, 2, :], wait=w)
                self.v("tensor_mul", A(P1_HB + cc * 512, [512]),
                       A(P1_HB + cc * 512, [512]), self.ps[:, 3, :])
                td = self.v("tensor_scalar", A(P1_HB + cc * 512, [512]),
                            A(P1_HB + cc * 512, [512]),
                            A(C_MOD + 8 * 1 + cc, [1]), A(C_MOD + 8 * 0 + cc, [1]),
                            op0=OP.mult, op1=OP.add, inc="tV")
                tdone.append(td)
            if own:
                tc = self.v("tensor_copy", A(A1 + 0, [8, NRUN, 512])[:, :, ch, :],
                            A(P1_HB, [8, 512]), wait=[tdone[-1]], inc="tV")
                self.p.do("vector", None, wait=[tc])
            else:
                tspill = p.dma("gpsimd", self.hsp[:, :, SPAD + ch * 512:SPAD + (ch + 1) * 512],
                               A(P1_HB, [8, 512]), wait=[tdone[-1]])
                p.do("vector", None, wait=[tspill])
            p.do("tensor", None, wait=[tdone[-1]])

    def p1(self):
        p, A = self.p, self.A
        tz = self.g("memset", A(P1_HB, [8, SPAD]), 0.0, inc="tG")
        p.dma("gpsimd", self.hsp[:, :, 0:SPAD], A(P1_HB, [8, SPAD]), wait=[tz])
        self.ln_h_pass(self.xg, 8, P1_MROW, P1_SROW, True, False)
        p.barrier()
        self.ln_h_pass(self.xo.rearrange("p c r t -> p c (r t)"), 4,
                       P1_MROW, P1_SROW, False, True)
        p.barrier()
        self.dump("h_own", A(A1, [8, NRUN, 512]), [p.tok("tV")])
        self.dump("mrow", A(P1_MROW, [4096], pn=1), [p.tok("tS")])

    PAIRS4 = [(0, 1), (0, 2), (0, 3), (1, 2), (1, 3), (2, 3)]

    def exterior(self, p1off, p2off, outoff, wait):
        A = self.A
        def e6(off, k):
            return bass.AP(self.ar, off + k, [[ARN, 128], [6, 16], [1, 1]])

        def e4(off, k):
            return bass.AP(self.ar, off + k, [[ARN, 128], [4, 16], [1, 1]])

        first = True
        for k, (i, j) in enumerate(self.PAIRS4):
            self.v("tensor_mul", e6(outoff, k), e4(p1off, i), e4(p2off, j),
                   wait=(wait if first else ()))
            first = False
            self.v("tensor_mul", e6(P2_SQ, k), e4(p1off, j), e4(p2off, i))
        self.p.do("vector", lambda ve: ve.drain())
        self.v("tensor_sub", A(outoff, [16, 6]), A(outoff, [16, 6]),
               A(P2_SQ, [16, 6]))
        self.p.do("vector", lambda ve: ve.drain())
        self.v("tensor_mul", A(P2_SQ, [16, 6]), A(outoff, [16, 6]),
               A(outoff, [16, 6]))
        t0 = self.v("reduce_sum", A(P2_N2, [16]), A(P2_SQ, [16, 6]), axis=AX.X,
                    inc="tV")
        t1 = self.p.do("vector", lambda ve: ve.drain(), inc="tV")
        t1b = self.v("tensor_scalar_max", A(P2_N2, [16]), A(P2_N2, [16]), 1e-24,
                     wait=[t1], inc="tV")
        ts = self.act(A(P2_N2, [16]), A(P2_N2, [16]), AF.Sqrt, wait=[t1b], inc="tS")
        t2 = self.v("reciprocal", A(P2_N2, [16]), A(P2_N2, [16]), wait=[ts], inc="tV")
        t3 = self.p.do("vector", lambda ve: ve.drain(), inc="tV")
        rn_b = bass.AP(self.ar, P2_N2, [[ARN, 128], [1, 16], [0, 6]])
        t4 = self.v("tensor_mul", A(outoff, [16, 6]), A(outoff, [16, 6]), rn_b,
                    wait=[t3], inc="tV")
        return t4

    def p2(self):
        p, A = self.p, self.A
        WWTS = A2 + 5280
        t_ww = p.dma("sync", A(WWTS, [8, 256]),
                     self.wwt.rearrange("(c p) m -> p c m", p=128))
        MGWS = A2 + 7328
        t_mg = p.dma("sync", A(MGWS, [8, 16]),
                     self.memgwt.rearrange("(c p) m -> p c m", p=128))
        t_cz = self.g("memset", A(P2_CROW, [576], pn=1), 0.0, inc="tG")

        cihb = bass.AP(self.ar, C_CIH, [[ARN, 128], [1, 16], [0, 6]])
        t_prev_upd = t_cz
        for b in range(NB):
            wb = P2_WB + (b % 2) * 1056
            t_w = p.dma("gpsimd", A(wb, [8, 132])[:, :, 0:129],
                        self.hsp[:, :, SPAD + b * 128 - 1: SPAD + b * 128 + 128],
                        wait=[])
            l0 = l1 = None
            for c in range(8):
                w = [t_w, t_ww] if c == 0 else []
                l0 = self.mm(self.ps[:, 0, 0:64], A(wb, [8, 132])[:, c, 0:128],
                             A(WWTS, [8, 256])[:, c, 0:64], c == 0, c == 7,
                             wait=w, inc=("tT" if c == 7 else None))
            for c in range(8):
                l1 = self.mm(self.ps[:, 1, 0:192], A(wb, [8, 132])[:, c, 1:129],
                             A(WWTS, [8, 256])[:, c, 64:256], c == 0, c == 7,
                             inc=("tT" if c == 7 else None))
            self.v("tensor_copy", A(P2_W1L, [16, 4]),
                   self.ps[:, 0, 0:64].rearrange("p (h f) -> p h f", h=16),
                   wait=[l0])
            t_ev = self.v("tensor_copy", A(P2_W2L, [16, 4]),
                          self.ps[:, 1, 0:64].rearrange("p (h f) -> p h f", h=16),
                          wait=[l1])
            self.v("tensor_copy", A(P2_R1L, [16, 4]),
                   self.ps[:, 1, 64:128].rearrange("p (h f) -> p h f", h=16))
            t_ev2 = self.v("tensor_copy", A(P2_R2L, [16, 4]),
                           self.ps[:, 1, 128:192].rearrange("p (h f) -> p h f", h=16),
                           inc="tV")
            if b == 1:
                self.dump("w1l", A(P2_W1L, [16, 4]), [t_ev2])
                self.dump("w2l", A(P2_W2L, [16, 4]), [t_ev2])
            if b == 1:
                self.dump("lw1b", A(P2_LW, [16, 6]), [self.p.tok("tV")])
            t_extw = self.exterior(P2_W1L, P2_W2L, P2_LW, [t_ev2])
            if b == 1:
                self.dump("lw", A(P2_LW, [16, 6]), [t_extw])
                self.dump("jws1", A(P2_JWS, [16, 6]), [self.p.tok("tV"), t_extw])
            rev = bass.AP(self.ar, P2_LW + 5, [[ARN, 128], [6, 16], [-1, 6]])
            self.v("tensor_copy", A(P2_JWS, [16, 6]), rev)
            t_jw = self.v("tensor_mul", A(P2_JWS, [16, 6]), A(P2_JWS, [16, 6]),
                          A(C_CIHS, [16, 6]), inc="tV")
            jw_j = bass.AP(self.ar, P2_JWS, [[ARN, 128], [6, 16], [1, 6], [0, 6]])
            jw_i = bass.AP(self.ar, P2_JWS, [[ARN, 128], [6, 16], [0, 6], [1, 6]])
            t_xp = self.v("tensor_mul", A(P2_XP, [16, 6, 6]), jw_j, jw_i, inc="tV")
            if b == 0:
                self.dump("jws", A(P2_JWS, [16, 6]), [t_xp])
                self.dump("xp", A(P2_XP, [16, 6, 6]), [t_xp])
            t_rd = self.exterior(P2_R1L, P2_R2L, P2_RD, [t_ev2])
            tE = []
            for hh in range(2):
                m0 = self.mm(self.ps[:, 2 + hh, 0:288], A(C_UCL, [128]),
                             A(P2_XP + 288 * hh, [288]), True, False,
                             wait=[t_xp, t_prev_upd])
                m1 = self.mm(self.ps[:, 2 + hh, 0:288], A(R_ONES, [128], pn=1),
                             A(P2_CROW + 288 * hh, [288], pn=1), False, True,
                             inc="tT")
                tE.append(m1)
            tT_ = []
            for hh in range(2):
                m = self.mm(self.ps[0:1, 4 + hh, 0:288], A(C_ONEC, [1]),
                            A(P2_XP + 288 * hh, [288]), True, True, inc="tT")
                tT_.append(m)
            e1 = self.scopy(A(P2_TROW, [288], pn=1), self.ps[0:1, 4, 0:288],
                            wait=[tT_[0]], inc="tS")
            e2 = self.scopy(A(P2_TROW + 288, [288], pn=1), self.ps[0:1, 5, 0:288],
                            wait=[tT_[1]], inc="tS")
            g1 = self.g("tensor_add", A(P2_CROW, [576], pn=1),
                        A(P2_CROW, [576], pn=1), A(P2_TROW, [576], pn=1),
                        wait=[e1, e2] + tE)
            c128b = bass.AP(self.ar, R_C128, [[ARN, 1], [1, 16], [0, 36]])
            t_prev_upd = self.g("tensor_mul", A(P2_CROW, [576], pn=1),
                                bass.AP(self.ar, P2_CROW, [[ARN, 1], [36, 16], [1, 36]]),
                                c128b, inc="tG")
            if b == 0:
                tcw = self.v("tensor_copy", av(self.ar, A3 + 8000, [576], 0, 1),
                             A(P2_CROW, [576], pn=1), wait=[t_prev_upd], inc="tV")
                self.dump("crow1", av(self.ar, A3 + 8000, [576], 0, 1), [tcw])
                tcw2 = self.v("tensor_copy", av(self.ar, A3 + 8600, [576], 0, 1),
                              A(P2_TROW, [576], pn=1), wait=[t_prev_upd], inc="tV")
                self.dump("trow0", av(self.ar, A3 + 8600, [576], 0, 1), [tcw2])
            rd_i = bass.AP(self.ar, P2_RD, [[ARN, 128], [6, 16], [0, 6], [1, 6]])
            self.v("tensor_mul", A(P2_PP, [16, 6, 6])[:, 0:8, :, :],
                   self.ps[:, 2, 0:288].rearrange("p (h j i) -> p h j i", h=8, j=6),
                   bass.AP(self.ar, P2_RD, [[ARN, 128], [6, 8], [0, 6], [1, 6]]),
                   wait=[t_rd, tE[0]])
            self.v("tensor_mul", A(P2_PP, [16, 6, 6])[:, 8:16, :, :],
                   self.ps[:, 3, 0:288].rearrange("p (h j i) -> p h j i", h=8, j=6),
                   bass.AP(self.ar, P2_RD + 48, [[ARN, 128], [6, 8], [0, 6], [1, 6]]),
                   wait=[tE[1]])
            t_r0 = self.v("reduce_sum", A(P2_RDM, [16, 6]), A(P2_PP, [16, 6, 6]),
                          axis=AX.X, inc="tV")
            if b == 1:
                self.dump("pp1", A(P2_PP, [16, 6, 6]), [t_r0])
                self.dump("rd1", A(P2_RD, [16, 6]), [t_r0])
            self.v("tensor_mul", A(P2_SQ, [16, 6]), A(P2_RDM, [16, 6]),
                   A(P2_RD, [16, 6]))
            self.v("reduce_sum", A(P2_S1, [16]), A(P2_SQ, [16, 6]), axis=AX.X)
            self.v("tensor_mul", A(P2_SQ, [16, 6]), A(P2_RDM, [16, 6]),
                   A(P2_RDM, [16, 6]))
            t_r1 = self.v("reduce_sum", A(P2_S2, [16]), A(P2_SQ, [16, 6]),
                          axis=AX.X, inc="tV")
            t_dr = self.p.do("vector", lambda ve: ve.drain(), inc="tV")
            if b == 0:
                self.dump("rdm0", A(P2_RDM, [16, 6]), [t_dr])
                self.dump("s10", A(P2_S1, [16]), [t_dr])
                self.dump("s20", A(P2_S2, [16]), [t_dr])
                tec = self.v("tensor_copy", A(A2 + 8192, [288]), self.ps[:, 2, 0:288],
                             wait=[t_dr], inc="tV")
                self.dump("e0", A(A2 + 8192, [288]), [tec])
            self.v("tensor_mul", A(P2_S1, [16]), A(P2_S1, [16]), A(C_CFW, [16]),
                   wait=[t_dr])
            self.v("tensor_mul", A(P2_S2, [16]), A(P2_S2, [16]), A(C_CF2, [16]))
            self.v("tensor_scalar", A(P2_S1, [16]), A(P2_S1, [16]),
                   A(C_APM + 1, [1]), None, op0=OP.mult)
            t_ms = self.v("scalar_tensor_tensor", A(P2_MSCB, [16]),
                          A(P2_S2, [16]), A(C_APM, [1]), A(P2_S1, [16]),
                          op0=OP.mult, op1=OP.add, inc="tV")
            t_out = p.dma("sync",
                          bass.AP(self.dscr.tensor, DS_MSC + b * 2048,
                                  [[16, 128], [1, 16]]),
                          A(P2_MSCB, [16]), wait=[t_ms])
            p.do("tensor", None, wait=[p.tok("tV"), p.tok("tS"), t_out])
        for r in range(NRUN):
            for tt in range(4):
                last = None
                for c in range(8):
                    last = self.mm(self.ps[:, 6, 0:16],
                                   A(A1 + (c * NRUN + r) * 512 + tt * 128, [128]),
                                   A(MGWS, [8, 16])[:, c, :], c == 0, False,
                                   wait=([t_mg] if c == 0 else []))
                last = self.mm(self.ps[:, 6, 0:16], A(R_ONES, [128], pn=1),
                               A(R_MGB, [16], pn=1), False, True, inc="tT")
                t_sg = self.act(A(P2_GT, [16]), self.ps[:, 6, 0:16], AF.Sigmoid,
                                wait=[last], inc="tS")
                t_go = p.dma("sync",
                             bass.AP(self.dscr.tensor,
                                     DS_GATE + (r * 4 + tt) * 2048,
                                     [[16, 128], [1, 16]]),
                             A(P2_GT, [16]), wait=[t_sg])
                p.do("tensor", None, wait=[t_go])
        p.barrier()
        self.dump("msc", self.dscr[DS_MSC:DS_MSC + NB * 2048],
                  [p.tok("dS")])
        self.dump("gate", self.dscr[DS_GATE:DS_GATE + 16 * 2048],
                  [p.tok("dS")])

    def p3(self):
        p, A = self.p, self.A
        t_atl = self.g("memset", A(XSPARE + 200, [1]), 0.0, inc="tG")
        nmask = [0]

        for pair in range(NPAIR):
            t_prevpair = [p.tok("tT"), p.tok("tV"), p.tok("tS")]
            t_wk = p.dma("sync", A(P3_WS, [8, 128]),
                         self.qkvwt.rearrange("(c p) m -> p c m", p=128)
                         [:, :, D + pair * 128: D + (pair + 1) * 128],
                         wait=t_prevpair)
            t_wv = p.dma("sync", A(P3_WS + 1024, [8, 128]),
                         self.qkvwt.rearrange("(c p) m -> p c m", p=128)
                         [:, :, 2 * D + pair * 128: 2 * D + (pair + 1) * 128])
            self.v("memset", bass.AP(self.ar, P3_VE + 64, [[ARN, 128], [130, 32], [1, 1]]), 1.0,
                   wait=t_prevpair)
            t_vo = self.v("memset", bass.AP(self.ar, P3_VE + 129, [[ARN, 128], [130, 32], [1, 1]]),
                          1.0, inc="tV")
            t_hf_prev = p.tok("tT")
            for gc in range(16):
                t_hf = p.dma("gpsimd", A(P3_HF, [8, 256]),
                             self.hsp[:, :, SPAD + gc * 256: SPAD + (gc + 1) * 256],
                             wait=[t_hf_prev])
                lk = None
                for c in range(8):
                    w = [t_hf, t_wk, p.tok("tS")] if c == 0 else []
                    lk = self.mm(self.ps[:, 5, 0:256], A(P3_WS, [8, 128])[:, c, :],
                                 A(P3_HF, [8, 256])[:, c, :], c == 0, c == 7,
                                 wait=w, inc=("tT" if c == 7 else None))
                t_ke = self.act(A(P3_KT + gc * 256, [256]), self.ps[:, 5, 0:256],
                                AF.Identity, bias=A(C_QB + 8 + pair, [1]),
                                wait=[lk], inc="tS")
                t_ve = None
                for jj in range(2):
                    kt = gc * 2 + jj
                    lv = None
                    for c in range(8):
                        w = [t_hf, t_wv, t_vo, p.tok("tV")] if c == 0 else []
                        lv = self.mm(self.ps[:, 6, 0:128],
                                     A(P3_HF, [8, 256])[:, c, jj * 128:(jj + 1) * 128],
                                     A(P3_WS + 1024, [8, 128])[:, c, :],
                                     c == 0, c == 7, wait=w,
                                     inc=("tT" if c == 7 else None))
                    t_ve = self.v("tensor_copy",
                                  bass.AP(self.ar, P3_VE + kt * 130, [[ARN, 128], [65, 2], [1, 64]]),
                                  self.ps[:, 6, 0:128].rearrange("p (h f) -> p h f", h=2),
                                  wait=[lv], inc="tV")
                    p.do("tensor", None, wait=[t_ve])
                t_hf_prev = p.tok("tT")
                p.do("gpsimd", None, wait=[t_ke, t_ve])
            if pair == 3:
                self.dump("kt0", A(P3_KT, [32, 128]), [p.tok("tS")])
                self.dump("ve0", A(P3_VE, [32, 130]), [p.tok("tV")])
            t_wq = p.dma("sync", A(P3_WS, [8, 128]),
                         self.qkvwt.rearrange("(c p) m -> p c m", p=128)
                         [:, :, pair * 128:(pair + 1) * 128],
                         wait=[p.tok("tT")])
            for r in range(NRUN):
                lq = None
                for c in range(8):
                    w = [t_wq, p.tok("tS")] if c == 0 else []
                    lq = self.mm(self.ps[:, 5, :], A(P3_WS, [8, 128])[:, c, :],
                                 A(A1 + (c * NRUN + r) * 512, [512]),
                                 c == 0, c == 7, wait=w,
                                 inc=("tT" if c == 7 else None))
                t_qe = self.act(A(P3_QT, [512]), self.ps[:, 5, :], AF.Identity,
                                bias=A(C_QB + pair, [1]),
                                wait=[lq, p.tok("tT")], inc="tS")
                if pair == 4 and r == 0:
                    self.dump("qt0", A(P3_QT, [512]), [t_qe])
                kl = KLEN[r]
                for head in range(2):
                    hp0 = head * 64
                    tp = (64, 0) if head else None
                    t_prevav = getattr(self, "_t_lastav", None)
                    t_preve = None
                    for kt in range(kl):
                        w_s = [t_qe, t_atl] + ([t_preve] if t_preve else [])
                        t_s = self.mm(self.ps[:, head, :],
                                      A(P3_KT + kt * 128, [128], p0=hp0, pn=64),
                                      A(P3_QT, [512], p0=hp0, pn=64),
                                      True, True, wait=w_s, inc="tT",
                                      tp=tp)
                        masked = kt >= kl - 8
                        if masked:
                            mi = r * 8 + (kt - (kl - 8))
                            mslot = P3_ATL + (nmask[0] % 2) * 512
                            nmask[0] += 1
                            t_ml = p.dma("gpsimd", A(mslot, [512]),
                                         self.maskt[mi], wait=[p.tok("tV")])
                        w_exp = [t_s] + ([t_prevav] if t_prevav else [])
                        t_e = self.act(A(P3_PB, [512]), self.ps[:, head, :],
                                       AF.Exp, scale=SCALE, wait=w_exp, inc="tS")
                        t_preve = t_e
                        t_m = t_e
                        if masked:
                            t_m = self.v("tensor_mul", A(P3_PB, [512]),
                                         A(P3_PB, [512]), A(mslot, [512]),
                                         wait=[t_e, t_ml], inc="tV")
                        w_av = [t_m] + ([p.tok("tS"), p.tok("tV")] if kt == 0 else [])
                        t_prevav = self.mm(self.ps[0:65, 2 + head, :],
                                           A(P3_VE + kt * 130 + head * 65, [65]),
                                           A(P3_PB, [512]), kt == 0, kt == kl - 1,
                                           wait=w_av, inc="tT")
                        self._t_lastav = t_prevav
                    self.scopy(av(self.ar, P3_OS, [512], 0, 64),
                               self.ps[0:64, 2 + head, :],
                               wait=[t_prevav, p.tok("tV")], inc="tS")
                    t_oe = self.scopy(av(self.ar, P3_OS, [512], 64, 1),
                                      self.ps[64:65, 2 + head, :],
                                      wait=[t_prevav], inc="tS")
                    if pair == 0 and r == 0 and head == 0:
                        self.dump("os0", av(self.ar, P3_OS, [512], 0, 65), [t_oe])
                    t_rc = self.v("reciprocal", av(self.ar, P3_OS, [512], 64, 1),
                                  av(self.ar, P3_OS, [512], 64, 1), wait=[t_oe],
                                  inc="tV")
                    t_dr = self.p.do("vector", lambda ve: ve.drain(), inc="tV")
                    t_rb = self.mm(self.ps[0:64, 4, :],
                                   A(R_ONES, [64], p0=64, pn=1),
                                   av(self.ar, P3_OS, [512], 64, 1), True, True,
                                   wait=[t_dr, p.tok("tV")], inc="tT")
                    dst = av(self.ar, A2 + (pair * NRUN + r) * 512, [512],
                             head * 64, 64)
                    self.v("tensor_mul", dst, av(self.ar, P3_OS, [512], 0, 64),
                           self.ps[0:64, 4, :], wait=[t_rb])
                    t_sq = self.v("tensor_scalar_add", dst, dst,
                                  A(C_QB + 16 + pair, [1], p0=head * 64, pn=64),
                                  inc="tV")
                    p.do("tensor", None, wait=[t_sq])
        p.barrier()
        self.dump("seqo", A(A2, [8, NRUN, 512]), [p.tok("tV")])

    def p4(self):
        p, A = self.p, self.A
        t_ms = p.dma("sync", A(P4_MSC, [32, 16]),
                     self.dscr[DS_MSC:DS_MSC + NB * 2048]
                     .rearrange("(b p f) -> p b f", p=128, f=16)
                     if False else
                     bass.AP(self.dscr.tensor, DS_MSC,
                             [[16, 128], [2048, 32], [1, 16]]))
        t_gt = p.dma("sync", A(P4_GATE, [16, 16]),
                     bass.AP(self.dscr.tensor, DS_GATE,
                             [[16, 128], [2048, 16], [1, 16]]))
        msb_b = bass.AP(self.ar, C_MSB, [[ARN, 128], [0, 4], [1, 16]])
        for r in range(NRUN):
            def mscop(ve, r=r):
                off = ve.value_load(self.itab[0:1, 128 + r:129 + r])
                src = bass.AP(self.ar, off, [[ARN, 128], [16, 4], [1, 16]])
                return ve.tensor_tensor(av(self.ar, P4_TMP, [4, 16]), src, msb_b,
                                        op=OP.mult)
            t1 = p.do("vector", mscop, wait=[t_ms, t_gt], inc="tV")
            t2 = self.act(A(P4_TMP, [4, 16]), A(P4_TMP, [4, 16]), AF.Sigmoid,
                          wait=[t1], inc="tS")
            t3 = self.v("tensor_mul", A(P4_TMP, [4, 16]), A(P4_TMP, [4, 16]),
                        A(P4_GATE, [16, 16])[:, r * 4:(r + 1) * 4, :], wait=[t2])
            self.p.do("vector", lambda ve: ve.drain())
            t4 = self.v("reduce_sum", A(P4_GCOL + r * 4, [4]), A(P4_TMP, [4, 16]),
                        axis=AX.X, inc="tV")
        t_dr = self.p.do("vector", lambda ve: ve.drain(), inc="tV")
        t5 = self.v("tensor_scalar_mul", A(P4_GCOL, [16]), A(P4_GCOL, [16]),
                    1.0 / 16, wait=[t_dr], inc="tV")
        t_tr = self.p.do("tensor", lambda t: t.transpose(
            self.ps[0:16, 0, 0:128], av(self.ar, P4_GCOL, [16]),
            av(self.ar, C_ID, [128])), wait=[t5], inc="tT")
        t_ge = self.scopy(av(self.ar, P4_TMP, [128], 0, 16), self.ps[0:16, 0, 0:128],
                          wait=[t_tr], inc="tS")
        t_gw = p.dma("sync",
                     bass.AP(self.dscr.tensor, DS_GROW, [[128, 16], [1, 128]]),
                     av(self.ar, P4_TMP, [128], 0, 16), wait=[t_ge])
        t_gr = p.dma("sync", A(P4_GROW, [2048], pn=1),
                     self.dscr[None, DS_GROW:DS_GROW + 2048],
                     wait=[t_gw])
        self.dump("gcol", A(P4_GCOL, [16]), [t5])
        tgrs = self.v("tensor_copy", av(self.ar, P4_TMP + 200, [2048], 0, 1),
                      A(P4_GROW, [2048], pn=1), wait=[t_gr], inc="tV")
        self.dump("grow", av(self.ar, P4_TMP + 200, [2048], 0, 1), [tgrs])
        MVWS = A3 + 7040
        OWS = A3 + 8064
        MVT = A3 + 9088
        for r in range(NRUN):
            t_xo = p.dma("sync", A(P4_XO, [8, 512]), self.xo[:, :, r, :],
                         wait=[p.tok("tV")])
            t_gb = self.mm(self.ps[:, 1, :], A(R_ONES, [128], pn=1),
                           A(P4_GROW + r * 512, [512], pn=1), True, True,
                           wait=[t_gr, p.tok("tV")], inc="tT")
            for dc in range(8):
                t_wv = p.dma("sync", A(MVWS, [8, 128]),
                             self.memvwt.rearrange("(c p) m -> p c m", p=128)
                             [:, :, dc * 128:(dc + 1) * 128], wait=[p.tok("tT")])
                lm = None
                for c in range(8):
                    w = [t_wv, p.tok("tS")] if c == 0 else []
                    lm = self.mm(self.ps[:, 0, :], A(MVWS, [8, 128])[:, c, :],
                                 A(A1 + (c * NRUN + r) * 512, [512]),
                                 c == 0, c == 7, wait=w,
                                 inc=("tT" if c == 7 else None))
                t_me = self.act(A(MVT, [512]), self.ps[:, 0, :], AF.Identity,
                                bias=A(C_MVB + dc, [1]),
                                wait=[lm, p.tok("tV")], inc="tS")
                tmvg = self.v("tensor_mul", A(MVT, [512]), A(MVT, [512]),
                       self.ps[:, 1, :], wait=[t_me, t_gb], inc="tV")
                if r == 0 and dc == 0:
                    self.dump("mvg0", A(MVT, [512]), [tmvg])
                t_ac = self.v("tensor_add",
                              av(self.ar, A2 + (dc * NRUN + r) * 512, [512]),
                              av(self.ar, A2 + (dc * NRUN + r) * 512, [512]),
                              A(MVT, [512]), inc="tV")
                if r == 0 and dc == 0:
                    self.dump("ain0", av(self.ar, A2 + (dc * NRUN + r) * 512, [512]), [t_ac])
                p.do("tensor", None, wait=[t_ac])
            for dc in range(8):
                t_wo = p.dma("sync", A(OWS, [8, 128]),
                             self.outwt.rearrange("(c p) m -> p c m", p=128)
                             [:, :, dc * 128:(dc + 1) * 128], wait=[p.tok("tT")])
                lo = None
                for c in range(8):
                    w = [t_wo, p.tok("tV"), p.tok("tS")] if c == 0 else []
                    lo = self.mm(self.ps[:, 2, :], A(OWS, [8, 128])[:, c, :],
                                 av(self.ar, A2 + (c * NRUN + r) * 512, [512]),
                                 c == 0, c == 7, wait=w,
                                 inc=("tT" if c == 7 else None))
                t_oe = self.act(A(MVT, [512]), self.ps[:, 2, :], AF.Identity,
                                bias=A(C_OB + dc, [1]),
                                wait=[lo, p.tok("tV")], inc="tS")
                if r == 0 and dc == 0:
                    self.dump("y0", A(MVT, [512]), [t_oe])
                t_x2 = self.v("scalar_tensor_tensor",
                              av(self.ar, A1 + (dc * NRUN + r) * 512, [512]),
                              A(MVT, [512]), A(C_MOD + 16 + dc, [1]),
                              A(P4_XO, [8, 512])[:, dc, :],
                              op0=OP.mult, op1=OP.add, wait=[t_oe, t_xo], inc="tV")
                p.do("tensor", None, wait=[t_x2])
        p.barrier()
        self.dump("x2", A(A1, [8, NRUN, 512]), [p.tok("tV")])

    def p5(self):
        p, A = self.p, self.A
        for r in range(NRUN):
            tsq = []
            for c in range(8):
                tsq.append(self.act(A(P5_XSQ + c * 512, [512]),
                                    av(self.ar, A1 + (c * NRUN + r) * 512, [512]),
                                    AF.Square, inc="tS"))
            lm = ls = None
            for c in range(8):
                lm = self.mm(self.ps[0:1, 0, :], A(C_ONEC, [1]),
                             av(self.ar, A1 + (c * NRUN + r) * 512, [512]),
                             c == 0, c == 7, inc=("tT" if c == 7 else None))
            for c in range(8):
                ls = self.mm(self.ps[0:1, 1, :], A(C_ONEC, [1]),
                             A(P5_XSQ + c * 512, [512]), c == 0, c == 7,
                             wait=[tsq[c]], inc=("tT" if c == 7 else None))
            self.act(A(P5_M2 + r * 512, [512], pn=1), self.ps[0:1, 0, :], AF.Copy,
                     scale=1.0 / D, wait=[lm], inc="tS")
            t_e = self.act(A(P5_RS2 + r * 512, [512], pn=1), self.ps[0:1, 1, :],
                           AF.Copy, scale=1.0 / D, wait=[ls], inc="tS")
            p.do("tensor", None, wait=[t_e, p.tok("tS")])
        t1 = self.act(A(P5_XSQ, [2048], pn=1), A(P5_M2, [2048], pn=1), AF.Square,
                      wait=[p.tok("tS")], inc="tS")
        t2 = self.v("tensor_sub", A(P5_RS2, [2048], pn=1), A(P5_RS2, [2048], pn=1),
                    A(P5_XSQ, [2048], pn=1), wait=[t1])
        t2 = self.v("tensor_scalar_add", A(P5_RS2, [2048], pn=1),
                    A(P5_RS2, [2048], pn=1), EPS, inc="tV")
        t3 = self.act(A(P5_RS2, [2048], pn=1), A(P5_RS2, [2048], pn=1), AF.Sqrt,
                      wait=[t2], inc="tS")
        t4 = self.v("reciprocal", A(P5_RS2, [2048], pn=1), A(P5_RS2, [2048], pn=1),
                    wait=[t3], inc="tV")
        t_rs = self.p.do("vector", lambda ve: ve.drain(), inc="tV")
        t5 = self.v("tensor_mul", A(P5_XSQ, [2048], pn=1), A(P5_M2, [2048], pn=1),
                    A(P5_RS2, [2048], pn=1), wait=[t_rs], inc="tV")
        t_w1 = p.dma("sync", self.dscr[None, DS_ROW:DS_ROW + 2048],
                     A(P5_RS2, [2048], pn=1), wait=[t_rs])
        t_w2 = p.dma("sync", self.dscr[None, DS_ROW + 2048:DS_ROW + 4096],
                     A(P5_XSQ, [2048], pn=1), wait=[t5])
        t_c1 = p.dma("sync", A(C_RS2C, [16]),
                     bass.AP(self.dscr.tensor, DS_ROW, [[1, 128], [128, 16]]),
                     wait=[t_w1])
        t_c2 = p.dma("sync", A(C_M2RS, [16]),
                     bass.AP(self.dscr.tensor, DS_ROW + 2048,
                             [[1, 128], [128, 16]]), wait=[t_w2])

        t_wg = p.dma("sync", A(C_WGP, [8, 8]),
                     self.wgt.rearrange("(c p) e -> p c e", p=128))
        lc2 = None
        for c in range(8):
            lc2 = self.mm(self.ps[0:1, 5, 8:16], A(C_MOD + 24 + c, [1]),
                          A(C_WGP, [8, 8])[:, c, :], c == 0, c == 7,
                          wait=([t_wg] if c == 0 else []),
                          inc=("tT" if c == 7 else None))
        cb1b = bass.AP(self.ar, C_MOD + 32, [[ARN, 128], [1, 8], [0, 8]])
        t_sc = self.v("tensor_mul", A(C_WGP, [8, 8]), A(C_WGP, [8, 8]), cb1b,
                      wait=[lc2], inc="tV")
        lc1 = None
        for c in range(8):
            lc1 = self.mm(self.ps[0:1, 5, 0:8], A(C_ONEC, [1]),
                          A(C_WGP, [8, 8])[:, c, :], c == 0, c == 7,
                          wait=([t_sc] if c == 0 else []),
                          inc=("tT" if c == 7 else None))
        t_e1 = self.scopy(A(P5_LG, [16], pn=1), self.ps[0:1, 5, 0:16],
                          wait=[lc1], inc="tS")
        t_b1 = self.mm(self.ps[:, 6, 0:16], A(R_ONES, [128], pn=1),
                       A(P5_LG, [16], pn=1), True, True, wait=[t_e1], inc="tT")
        t_bc = self.scopy(A(C_C1B, [16]), self.ps[:, 6, 0:16], wait=[t_b1], inc="tS")

        for tt in range(16):
            r, sub = tt // 4, tt % 4
            lg = None
            for c in range(8):
                lg = self.mm(self.ps[:, 4, 0:8],
                             av(self.ar, A1 + (c * NRUN + r) * 512 + sub * 128, [128]),
                             A(C_WGP, [8, 8])[:, c, :], c == 0, c == 7,
                             wait=([t_sc, t_bc, t_c1, t_c2] if c == 0 else []),
                             inc=("tT" if c == 7 else None))
            self.v("tensor_scalar", A(P5_T1, [8]), self.ps[:, 4, 0:8],
                   A(C_RS2C + tt, [1]), None, op0=OP.mult, wait=[lg])
            self.v("tensor_scalar", A(P5_T2, [8]), A(C_C1B, [16])[:, 0:8],
                   A(C_M2RS + tt, [1]), None, op0=OP.mult)
            self.p.do("vector", lambda ve: ve.drain())
            self.v("tensor_sub", A(P5_T1, [8]), A(P5_T1, [8]), A(P5_T2, [8]))
            self.p.do("vector", lambda ve: ve.drain())
            t_lg = self.v("tensor_add", A(P5_LG, [8]), A(P5_T1, [8]),
                          A(C_C1B, [16])[:, 8:16], inc="tV")
            self.p.do("vector", lambda ve: ve.drain())
            t_m1 = self.v("reduce_max", A(P5_M1, [1]), A(P5_LG, [8]), axis=AX.X,
                          inc="tV")
            t_d1 = self.p.do("vector", lambda ve: ve.drain(), inc="tV")
            if tt == 0:
                self.dump("lg0", A(P5_LG, [8]), [t_d1])
            tm1 = self.v("tensor_scalar", A(P5_T1, [8]), A(P5_LG, [8]), A(P5_M1, [1]),
                   None, op0=OP.is_equal, wait=[t_d1], inc="tV")
            self.p.do("vector", lambda ve: ve.drain())
            if tt == 0:
                self.dump("msk1", A(P5_T1, [8]), [p.tok("tV")])
                self.dump("m1v", A(P5_M1, [1]), [p.tok("tV")])
            self.v("scalar_tensor_tensor", A(P5_T2, [8]), A(P5_T1, [8]), -1e30,
                   A(P5_LG, [8]), op0=OP.mult, op1=OP.add)
            self.p.do("vector", lambda ve: ve.drain())
            t_m2 = self.v("reduce_max", A(P5_M2V, [1]), A(P5_T2, [8]), axis=AX.X,
                          inc="tV")
            t_d2 = self.p.do("vector", lambda ve: ve.drain(), inc="tV")
            t_sb = self.v("tensor_sub", A(P5_W1N, [1]), A(P5_M1, [1]),
                          A(P5_M2V, [1]), wait=[t_d2], inc="tV")
            t_db = self.p.do("vector", lambda ve: ve.drain(), inc="tV")
            t_sg = self.act(A(P5_W1N, [1]), A(P5_W1N, [1]), AF.Sigmoid,
                            wait=[t_db], inc="tS")
            t_w2n = self.act(A(P5_W2N, [1]), A(P5_W1N, [1]), AF.Copy, scale=-1.0,
                             bias=1.0, wait=[t_sg], inc="tS")
            self.v("tensor_scalar", A(P5_T2, [8]), A(P5_T2, [8]), A(P5_M2V, [1]),
                   None, op0=OP.is_equal, wait=[t_w2n])
            self.p.do("vector", lambda ve: ve.drain())
            self.v("tensor_scalar", A(P5_T1, [8]), A(P5_T1, [8]), A(P5_W1N, [1]),
                   None, op0=OP.mult)
            self.v("tensor_scalar", A(P5_T2, [8]), A(P5_T2, [8]), A(P5_W2N, [1]),
                   None, op0=OP.mult)
            self.p.do("vector", lambda ve: ve.drain())
            t_g = self.v("tensor_add", A(XGATES + tt * 8, [8]), A(P5_T1, [8]),
                         A(P5_T2, [8]), inc="tV")
            p.do("tensor", None, wait=[t_g])
        self.dump("gates", A(XGATES, [16, 8]), [p.tok("tV")])

        for r in range(NRUN):
            tmb = self.mm(self.ps[:, 2, :], A(R_ONES, [128], pn=1),
                          A(P5_M2 + r * 512, [512], pn=1), True, True,
                          wait=[t_rs, p.tok("tV")], inc="tT")
            trb = self.mm(self.ps[:, 3, :], A(R_ONES, [128], pn=1),
                          A(P5_RS2 + r * 512, [512], pn=1), True, True, inc="tT")
            td = None
            for c in range(8):
                dst = av(self.ar, A2 + (c * NRUN + r) * 512, [512])
                self.v("tensor_sub", dst,
                       av(self.ar, A1 + (c * NRUN + r) * 512, [512]),
                       self.ps[:, 2, :], wait=([tmb, trb] if c == 0 else []))
                self.v("tensor_mul", dst, dst, self.ps[:, 3, :])
                td = self.v("tensor_scalar", dst, dst,
                            A(C_MOD + 32 + c, [1]), A(C_MOD + 24 + c, [1]),
                            op0=OP.mult, op1=OP.add, inc="tV")
            p.do("tensor", None, wait=[td])
        self.dump("h2", A(A2, [8, NRUN, 512]), [p.tok("tV")])

        t_b2 = p.dma("sync", av(self.ar, XB2GB, [1024], 0, 8), self.b2[:, :])
        t_gbr = p.dma("sync", A(P5_LG + 64, [1024], pn=1),
                      self.dscr[None, DS_GBROW:DS_GBROW + 1024])
        for hh in range(2):
            t_r1m = self.mm(self.ps[0:8, 6 + hh, :], A(R_ONES, [8], pn=1),
                            A(P5_LG + 64 + hh * 512, [512], pn=1), True, True,
                            wait=[t_gbr, t_b2], inc="tT")
            self.v("tensor_mul", av(self.ar, XB2GB + hh * 512, [512], 0, 8),
                   av(self.ar, XB2GB + hh * 512, [512], 0, 8),
                   self.ps[0:8, 6 + hh, :], wait=[t_r1m], inc="tV")

        for tt in range(16):
            r, sub = tt // 4, tt % 4
            t_tr = self.p.do("tensor", lambda t, tt=tt: t.transpose(
                self.ps[0:8, 5, 0:128], av(self.ar, XGATES + tt * 8, [8]),
                av(self.ar, C_ID, [128])), wait=[p.tok("tV")], inc="tT")
            t_gte = self.scopy(av(self.ar, XGT, [128], 0, 8), self.ps[0:8, 5, 0:128],
                               wait=[t_tr], inc="tS")
            tx = []
            for c in range(8):
                w_t = (tx[-1:] or []) + ([p.tok("tV"), p.tok("tS")] if c == 0 else [])
                t_t2 = self.p.do("tensor", lambda t, c=c, r=r, sub=sub: t.transpose(
                    self.ps[:, 4, 0:128],
                    av(self.ar, A1 + (c * NRUN + r) * 512 + sub * 128, [128]),
                    av(self.ar, C_ID, [128])), wait=w_t, inc="tT")
                t_ae = self.scopy(av(self.ar, A3 + tt * 1024 + c * 128, [128]),
                                  self.ps[:, 4, 0:128], wait=[t_t2], inc="tS")
                tx.append(t_ae)
            for hh in range(2):
                t_bm = self.mm(self.ps[:, 6, :], av(self.ar, XGT, [128], 0, 8),
                               av(self.ar, XB2GB + hh * 512, [512], 0, 8),
                               True, True, wait=[t_gte, p.tok("tV")], inc="tT")
                t_aa = self.v("tensor_add",
                              av(self.ar, A3 + tt * 1024 + hh * 512, [512]),
                              av(self.ar, A3 + tt * 1024 + hh * 512, [512]),
                              self.ps[:, 6, :], wait=[t_bm, tx[-1]], inc="tV")
                p.do("tensor", None, wait=[t_aa])
        p.barrier()
        self.dump("accinit", A(A3, [16, 1024]), [p.tok("tV")])

    def p6(self):
        p, A = self.p, self.A
        sub = self.stop_after if (self.stop_after or "").startswith("p6") else None
        FA2S, FB2TS, FFB, FGBB, FGBR = A1, A1 + 2048, A1 + 4096, A1 + 5120, A1 + 6144
        t_gbr = p.dma("sync", A(FGBR, [1024], pn=1),
                      self.dscr[None, DS_GBROW:DS_GBROW + 1024])
        if sub == "p6z":
            p.barrier()
            return
        t_gc = None
        for hh in range(2):
            t_m = self.mm(self.ps[:, 7, :], A(R_ONES, [128], pn=1),
                          A(FGBR + hh * 512, [512], pn=1), True, True,
                          wait=([t_gbr] if hh == 0 else [t_gc]), inc="tT")
            t_gc = self.v("tensor_copy", A(FGBB + hh * 512, [512]), self.ps[:, 7, :],
                          wait=[t_m], inc="tV")
        if sub == "p6g":
            p.barrier()
            return
        for e in (range(E) if sub != "p6f0" else [0]):
            t_a2 = p.dma("sync", av(self.ar, FA2S, [2048], 0, 16), self.a2[e])
            t_b2t = p.dma("sync", av(self.ar, FB2TS, [1024], 0, 16), self.b2t[e])
            for i in range(32 if sub != "p6f0" else 2):
                hc, doh = i // 2, i % 2
                fb = FFB + (i % 2) * 512
                t_f = self.mm(self.ps[:, 4, :],
                              av(self.ar, FA2S + hc * 128, [128], 0, 16),
                              av(self.ar, FB2TS + doh * 512, [512], 0, 16),
                              True, True, wait=[t_a2, t_b2t, p.tok("dG"),
                                                p.tok("dS")], inc="tT")
                t_fe = self.scopy(A(fb, [512]), self.ps[:, 4, :], wait=[t_f],
                                  inc="tS")
                t_acc = p.dma("gpsimd", A(fb, [512]),
                              self.w2t[e, hc * 128:(hc + 1) * 128,
                                       doh * 512:(doh + 1) * 512],
                              wait=[t_fe], accum=True)
                t_gb = self.v("tensor_mul", A(fb, [512]), A(fb, [512]),
                              A(FGBB + doh * 512, [512]),
                              wait=[t_acc, p.tok("tV")], inc="tV")
                p.dma("sync", self.w2scr[e, hc * 128:(hc + 1) * 128,
                                         doh * 512:(doh + 1) * 512],
                      A(fb, [512]), wait=[t_gb])
        p.barrier()
        if sub == "p6f":
            return

        g_gate = {}
        slab_gate = {}
        w2m_gate = {}
        last_l3 = None
        for tg in (range(NRUN) if sub in (None, "p6") else [0]):
            for (src, uoff, bank) in ((self.a1t, P6_U1, 5), (self.a3t, P6_U3, 6)):
                t_as = p.dma("sync", A(P6_A1S if uoff == P6_U1 else P6_A3S, [8, 128]),
                             src.rearrange("(c p) m -> p c m", p=128),
                             wait=[p.tok("tT")])
                lu = None
                for c in range(8):
                    lu = self.mm(self.ps[:, bank, :],
                                 A((P6_A1S if uoff == P6_U1 else P6_A3S), [8, 128])[:, c, :],
                                 av(self.ar, A2 + (c * NRUN + tg) * 512, [512]),
                                 c == 0, c == 7,
                                 wait=([t_as, p.tok("tV")] if c == 0 else []),
                                 inc=("tT" if c == 7 else None))
                t_ue = self.v("tensor_copy", A(uoff, [512]), self.ps[:, bank, :],
                              wait=[lu], inc="tV")
                t_ust = p.dma("sync",
                              bass.AP(self.dscr.tensor,
                                      (DS_U1 if uoff == P6_U1 else DS_U3),
                                      [[512, 128], [1, 512]]),
                              A(uoff, [512]), wait=[t_ue])
                if uoff == P6_U1:
                    t_u1st = t_ust
                else:
                    t_u3st = t_ust
            for e in (range(E) if sub != "p6e0" else [0]):
                t_u1 = p.dma("sync", av(self.ar, P6_U1, [512], 0, 16),
                             bass.AP(self.dscr.tensor, DS_U1 + e * 16 * 512,
                                     [[512, 16], [1, 512]]),
                             wait=[p.tok("tT"), t_u1st])
                t_u3 = p.dma("sync", av(self.ar, P6_U3, [512], 0, 16),
                             bass.AP(self.dscr.tensor, DS_U3 + e * 16 * 512,
                                     [[512, 16], [1, 512]]),
                             wait=[t_u3st])
                for hg in range(8):
                    w1r = self.w1t.rearrange("e (c p) m -> e p c m", p=128)
                    w3r = self.w3t.rearrange("e (c p) m -> e p c m", p=128)
                    w1s = P6_W1S if hg % 2 == 0 else P6_W1SA
                    w3s = P6_W3S if hg % 2 == 0 else P6_W3SA
                    sg = slab_gate.get(hg % 2)
                    t_w1 = p.dma("sync", A(w1s, [8, 2, 128]),
                                 w1r[e][:, :, hg * 256:(hg + 1) * 256]
                                 .rearrange("p c (h f) -> p c h f", h=2),
                                 wait=([sg] if sg is not None else [p.tok("tT")]))
                    t_w3 = p.dma("sync", A(w3s, [8, 2, 128]),
                                 w3r[e][:, :, hg * 256:(hg + 1) * 256]
                                 .rearrange("p c (h f) -> p c h f", h=2))
                    t_bb1 = p.dma("sync", av(self.ar, P6_B1T, [256], 0, 16),
                                  self.b1t[e][:, hg * 256:(hg + 1) * 256],
                                  wait=([last_l3] if last_l3 is not None else []))
                    t_bb3 = p.dma("sync", av(self.ar, P6_B3T, [256], 0, 16),
                                  self.b3t[e][:, hg * 256:(hg + 1) * 256])
                    for hcs in range(2):
                        hc = hg * 2 + hcs
                        b1k, b3k = (0, 1) if hcs == 0 else (2, 3)
                        gg = g_gate.get(hcs)
                        l1 = l3 = None
                        for c in range(8):
                            w = ([t_w1, t_w3] + ([gg] if gg is not None else [])
                                 if c == 0 else [])
                            l1 = self.mm(self.ps[:, b1k, :],
                                         A(w1s, [8, 2, 128])[:, c, hcs, :],
                                         av(self.ar, A2 + (c * NRUN + tg) * 512, [512]),
                                         c == 0, False, wait=w)
                        l1 = self.mm(self.ps[:, b1k, :],
                                     av(self.ar, P6_B1T + hcs * 128, [128], 0, 16),
                                     av(self.ar, P6_U1, [512], 0, 16),
                                     False, True, wait=[t_bb1, t_u1], inc="tT")
                        for c in range(8):
                            l3 = self.mm(self.ps[:, b3k, :],
                                         A(w3s, [8, 2, 128])[:, c, hcs, :],
                                         av(self.ar, A2 + (c * NRUN + tg) * 512, [512]),
                                         c == 0, False)
                        l3 = self.mm(self.ps[:, b3k, :],
                                     av(self.ar, P6_B3T + hcs * 128, [128], 0, 16),
                                     av(self.ar, P6_U3, [512], 0, 16),
                                     False, True, wait=[t_bb3, t_u3], inc="tT")
                        t_si = self.act(A(P6_G + hc * 512, [512]), self.ps[:, b1k, :],
                                        AF.Silu, bias=A(C_B1 + e * 16 + hc, [1]),
                                        wait=[l1], inc="tS")
                        t_g = self.v("scalar_tensor_tensor", A(P6_G + hc * 512, [512]),
                                     self.ps[:, b3k, :], A(C_B3 + e * 16 + hc, [1]),
                                     A(P6_G + hc * 512, [512]),
                                     op0=OP.add, op1=OP.mult,
                                     wait=[l3, t_si], inc="tV")
                        g_gate[hcs] = t_g
                        slab_gate[hg % 2] = l3
                        last_l3 = l3
                for doh in range(2):
                    lys = [None] * 4
                    for hc in range(16):
                        wslot = P6_W2M if hc % 2 == 0 else P6_W2MA
                        wg = w2m_gate.get(hc % 2)
                        t_wm = p.dma("sync", A(wslot, [512]),
                                     self.w2scr[e, hc * 128:(hc + 1) * 128,
                                                doh * 512:(doh + 1) * 512],
                                     wait=([wg] if wg is not None
                                           else [p.tok("tT")]))
                        for tt in range(4):
                            lys[tt] = self.mm(self.ps[:, 4 + tt, :],
                                              av(self.ar, P6_G + hc * 512 + tt * 128,
                                                 [128]),
                                              A(wslot, [512]), hc == 0, hc == 15,
                                              wait=([t_wm, p.tok("tV")] if tt == 0 else []),
                                              inc="tT")
                        w2m_gate[hc % 2] = lys[3]
                    for tt in range(4):
                        gtt = tg * 4 + tt
                        t_y = self.v("scalar_tensor_tensor",
                                     av(self.ar, A3 + gtt * 1024 + doh * 512, [512]),
                                     self.ps[:, 4 + tt, :],
                                     A(XGATES + gtt * 8 + e, [1]),
                                     av(self.ar, A3 + gtt * 1024 + doh * 512, [512]),
                                     op0=OP.mult, op1=OP.add,
                                     wait=[lys[tt]], inc="tV")
        p.barrier()

    def p7(self):
        p, A = self.p, self.A
        for tt in range(16):
            self.p.dma("sync", self.out[tt * 128:(tt + 1) * 128, :],
                       av(self.ar, A3 + tt * 1024, [1024]),
                       wait=[p.tok("tV"), p.tok("tT")])
        tok = p.tok("dS")
        for e in ENGS:
            p.do(e, None, wait=[tok])

    def build(self):
        self.declare_io()
        es = contextlib.ExitStack()
        with es:
            self.ar = es.enter_context(self.nc.sbuf_tensor("arena", [128, ARN], F32))
            self.ps = es.enter_context(self.nc.psum_tensor("ps", [128, 8, 512], F32))
            self.itab = es.enter_context(self.nc.sbuf_tensor("itab", [1, 256], I32))
            sems = {s: es.enter_context(self.nc.semaphore(s)) for s in self.p.cnt}
            block = es.enter_context(self.nc.Block())
            self._cm = self.nc.allow_non_contiguous_dma(reason="small strided pp/row loads")
            phases = ["p0", "p1", "p2", "p3", "p4", "p5", "p6"]
            if self.stop_after is not None:
                sa = "p6" if self.stop_after.startswith("p6") else self.stop_after
                phases = phases[:phases.index(sa) + 1]
            for ph in phases:
                getattr(self, ph)()
            self.p7()
            with self._cm:
                self.p.emit(block, sems)
        return self.nc


def _half_runs(half):
    return RUNS_A if half == 0 else RUNS_B


def _stage_shared(i):
    s = {}
    s["wmod"] = np.ascontiguousarray(i["L0_wmod"])
    s["qkvwt"] = np.ascontiguousarray(i["L0_attn_qkv_w"].T)
    s["qkvb"] = np.ascontiguousarray(i["L0_attn_qkv_b"])
    s["wwt"] = np.ascontiguousarray(np.concatenate(
        [i["L0_attn_w1w"].T, i["L0_attn_w2w"].T,
         i["L0_attn_r1w"].T, i["L0_attn_r2w"].T], axis=1))
    s["memvwt"] = np.ascontiguousarray(i["L0_attn_memv_w"].T)
    s["memvb"] = np.ascontiguousarray(i["L0_attn_memv_b"])
    s["memgwt"] = np.ascontiguousarray(i["L0_attn_memg_w"].T)
    s["memgb"] = np.ascontiguousarray(i["L0_attn_memg_b"])
    s["mscale"] = np.ascontiguousarray(i["L0_attn_mem_scale"])
    s["imix"] = np.ascontiguousarray(np.asarray(i["L0_attn_iter_mix"]).reshape(1))
    s["outwt"] = np.ascontiguousarray(i["L0_attn_out_w"].T)
    s["outb"] = np.ascontiguousarray(i["L0_attn_out_b"])
    s["dlog"] = np.ascontiguousarray(i["L0_attn_decay_logits"])
    s["wgt"] = np.ascontiguousarray(i["L0_ffn_Wg"].T)
    s["w1t"] = np.ascontiguousarray(i["L0_ffn_W1"].transpose(0, 2, 1))
    s["w2t"] = np.ascontiguousarray(i["L0_ffn_W2"].transpose(0, 2, 1))
    s["w3t"] = np.ascontiguousarray(i["L0_ffn_W3"].transpose(0, 2, 1))
    s["b1"] = np.ascontiguousarray(i["L0_ffn_b1"])
    s["b2"] = np.ascontiguousarray(i["L0_ffn_b2"])
    s["b3"] = np.ascontiguousarray(i["L0_ffn_b3"])
    s["a1t"] = np.ascontiguousarray(
        i["L0_ffn_A1"].transpose(2, 0, 1).reshape(D, E * R))
    s["a3t"] = np.ascontiguousarray(
        i["L0_ffn_A3"].transpose(2, 0, 1).reshape(D, E * R))
    s["b1t"] = np.ascontiguousarray(i["L0_ffn_B1"].transpose(0, 2, 1))
    s["b3t"] = np.ascontiguousarray(i["L0_ffn_B3"].transpose(0, 2, 1))
    s["a2"] = np.ascontiguousarray(i["L0_ffn_A2"])
    s["b2t"] = np.ascontiguousarray(i["L0_ffn_B2"].transpose(0, 2, 1))
    return s


def _stage_core(i, shared, core):
    seq, half = core // 2, core % 2
    runs = _half_runs(half)
    x = np.asarray(i["L0_attn_x"])[seq]
    xt = np.ascontiguousarray(x.T).reshape(8, 128, T).transpose(1, 0, 2)
    xg = np.ascontiguousarray(xt)
    xo = np.stack([xg[:, :, rb * 128: rb * 128 + RUNTOK] for rb in runs], axis=2)
    xo = np.ascontiguousarray(xo)
    crow = np.ascontiguousarray(np.asarray(i["L0_c"])[seq])
    msel = np.zeros(NRUN * 32, np.int32)
    maskt = np.zeros((32, 128, 512), np.float32)
    colv = np.arange(512)[None, :]
    rowv = np.arange(128)[:, None]
    for r in range(NRUN):
        qb0 = runs[r]
        kl = KLEN[r]
        for idx in range(8):
            kt = kl - 8 + idx
            d = qb0 - kt
            maskt[r * 8 + idx] = (d * 128 + colv >= rowv).astype(np.float32)
    roff = np.zeros(8, np.int32)
    for r in range(NRUN):
        roff[r] = P4_MSC + runs[r] * 16
    m = dict(shared)
    m.update({"xg": xg, "xo": xo, "crow": crow, "masksel": msel, "roff_": roff,
              "runoff": roff, "maskt": maskt})
    m.pop("roff_")
    return m


_NC_CACHE = {}


def _get_nc(dbg=(), stop_after=None):
    key = (tuple(sorted(dict(dbg).items())), stop_after)
    if key not in _NC_CACHE:
        kb = KB(dbg, stop_after)
        _NC_CACHE[key] = (kb.build(), kb)
    return _NC_CACHE[key]


def kernel(**inputs):
    nc, kb = _get_nc()
    shared = _stage_shared(inputs)
    in_maps = [_stage_core(inputs, shared, core) for core in range(8)]
    res = run_bass_kernel_spmd(nc, in_maps, core_ids=list(range(8)))
    full = np.zeros((B, T, D), np.float32)
    for core in range(8):
        seq, half = core // 2, core % 2
        runs = _half_runs(half)
        o = np.asarray(res.results[core]["out"])
        for r, rb in enumerate(runs):
            full[seq, rb * 128: rb * 128 + RUNTOK] = o[r * RUNTOK:(r + 1) * RUNTOK]
    return full


# version 03 (final)
# speedup vs baseline: 1.2522x; 1.0122x over previous
import contextlib
import numpy as np

import concourse.bass as bass
import concourse.mybir as mybir
from concourse.bass_utils import run_bass_kernel_spmd

F32 = mybir.dt.float32
I32 = mybir.dt.int32
AF = mybir.ActivationFunctionType
OP = mybir.AluOpType
AX = mybir.AxisListType

B, T, D, H, DH, PD = 4, 4096, 1024, 16, 64, 6
HID, E, R = 2048, 8, 16
NB = T // 128
NRUN, RUNTOK = 4, 512
OWN = NRUN * RUNTOK
KLEN = [8, 16, 24, 32]
SCALE = 0.125
EPS = 1e-5
RUNS_A = [0, 8, 20, 28]
RUNS_B = [4, 12, 16, 24]
NPAIR = 8
SPAD = 8

ENGS = ("sync", "tensor", "vector", "scalar", "gpsimd")
ARN = 52736

A1 = 0
A2 = 16384
A3 = 32768
XT = 49152

XC = XT
XROW = XT + 640
XB2GB = XT + 1024
XGT = XT + 2048
XMONE = XT + 2176
C_CIHS = XT + 2176
XMZERO = XT + 2688
XGATES = XT + 3200
XSPARE = XT + 3328

C_UCL = XC + 0
C_CIH = XC + 128
C_CFW = XC + 144
C_CF2 = XC + 160
C_MSB = XC + 176
C_MOD = XC + 192
C_QB = XC + 240
C_OB = XC + 264
C_MVB = XC + 272
C_B1 = XC + 280
C_B3 = XC + 408
C_APM = XC + 536
C_WGP = XC + 538
C_C1B = XC + 602
C_C2B = XC + 610
C_RS2C = XC + 618
C_M2RS = XC + 634
XROW = XT + 660

R_ONES = XROW + 0
R_IOTA = XROW + 128
R_LND = XROW + 256
R_C128 = XROW + 272
R_MGB = XROW + 288
R_MSC = XROW + 304
R_IMX = XROW + 320
R_DSG = XROW + 321
assert R_DSG + 16 <= XB2GB

P0_MODROW = A2
P0_WMB = A2 + 6144
P0_CSC = A2 + 10240

P1_XB = A2
P1_HB = A2 + 4096
P1_MROW = A2 + 8192
P1_SROW = A2 + 12288

P2_WB = A2
P2_W1L = A2 + 2112
P2_W2L = A2 + 2176
P2_R1L = A2 + 2240
P2_R2L = A2 + 2304
P2_LW = A2 + 2368
P2_SQ = A2 + 2464
P2_N2 = A2 + 2560
P2_JWS = A2 + 2576
P2_XP = A2 + 2672
P2_RD = A2 + 3248
P2_PP = A2 + 3344
P2_RDM = A2 + 3920
P2_S1 = A2 + 4016
P2_S2 = A2 + 4032
P2_S3 = A2 + 4048
P2_CROW = A2 + 4096
P2_TROW = A2 + 4672
P2_MSCB = A2 + 5248
P2_GT = A2 + 5264

P3_KT = A3
P3_VE = A3 + 4096
P3_QT = A3 + 8256
P3_PB = A3 + 8768
P3_HF = A3 + 9280
P3_WS = A3 + 11328
P3_OS = A3 + 13376
P3_ATL = A3 + 13888

P4_XO = A3
P4_MSC = A3 + 4096
P4_GATE = A3 + 4608
P4_GROW = A3 + 4864
P4_GCOL = A3 + 6912
P4_TMP = A3 + 6928

P5_XSQ = A3
P5_M2 = A3 + 4096
P5_RS2 = A3 + 6144
P5_LG = A3 + 8192
P5_T1 = A3 + 8320
P5_T2 = A3 + 8448
P5_M1 = A3 + 8576
P5_M2V = A3 + 8592
P5_W1N = A3 + 8608
P5_W2N = A3 + 8624

P6_G = A1
P6_W1S = A1 + 8192
P6_W3S = A1 + 10240
P6_W2M = A1 + 12288
P6_U1 = A1 + 12800
P6_U3 = A1 + 13312
P6_A1S = A1 + 13824
P6_A3S = A1 + 14848
P6_W1SA = XT + 1152
P6_W3SA = A1 + 13824
P6_W2MA = XT + 640
P6_B1T = A1 + 15872
P6_B3T = A1 + 16128

C_ID = XSPARE
C_ONEC = XSPARE + 128
C_DUM = XSPARE + 192

DS_MOD = 0
DS_GBROW = 6144
DS_MSC = 8192
DS_GATE = 73728
DS_GROW = 106496
DS_ROW = 108544
DS_U1 = 112640
DS_U3 = 178176
DS_END = 243712


def av(handle, off, shape, p0=0, pn=128):
    n = int(np.prod(shape))
    full = handle[p0:p0 + pn, off:off + n]
    if len(shape) == 1:
        return full
    spec = " ".join(f"s{i}" for i in range(len(shape)))
    return full.rearrange(f"p ({spec}) -> p {spec}",
                          **{f"s{i}": shape[i] for i in range(len(shape))})


class PB:
    def __init__(self):
        self.steps = {e: [] for e in ENGS}
        self.cnt = {s: 0 for s in ("dS", "dG", "tT", "tV", "tS", "tG")}

    def tok(self, sem):
        return (sem, self.cnt[sem])

    def do(self, eng, fn=None, wait=(), inc=None, incn=1, is_dma=False):
        import traceback
        waits = [(s, v) for (s, v) in wait if v > 0]
        tokn = None
        if eng == "gpsimd" and fn is not None and not is_dma:
            if getattr(self, "_gprev", None):
                waits.append(self._gprev)
            if inc is None:
                inc = "tG"
        if inc is not None:
            self.cnt[inc] += incn
            tokn = (inc, self.cnt[inc])
        if eng == "gpsimd" and fn is not None and not is_dma:
            self._gprev = tokn
        site = traceback.extract_stack(limit=4)
        self.steps[eng].append((fn, waits, inc, incn, site))
        return tokn

    def dma(self, q, out, in_, wait=(), accum=False):
        sem = "dS" if q == "sync" else "dG"
        kw = {"accum_op": OP.add} if accum else {}
        if q == "gpsimd":
            wait = list(wait) + ([self._gdma_prev]
                                 if getattr(self, "_gdma_prev", None) else [])
        tok = self.do(q, lambda e, o=out, i=in_, kw=kw: e.dma_start(out=o, in_=i, **kw),
                      wait=wait, inc=sem, incn=16, is_dma=True)
        if q == "gpsimd":
            self._gdma_prev = tok
        return tok

    def barrier(self):
        toks = [self.tok(s) for s in self.cnt]
        for e in ENGS:
            self.do(e, None, wait=toks)

    def emit(self, block, sems):
        def mk(name):
            def run(eng):
                for (fn, waits, inc, incn, site) in self.steps[name]:
                    for (s, v) in waits:
                        eng.wait_ge(sems[s], v)
                    try:
                        ins = fn(eng) if fn is not None else None
                    except Exception as ex:
                        raise RuntimeError(
                            "step failed, created at:\n" +
                            "".join(x for x in map(str, site[:-1]))) from ex
                    if inc is not None:
                        ins.then_inc(sems[inc], incn)
            return run
        block.sync(mk("sync"))
        block.tensor(mk("tensor"))
        block.vector(mk("vector"))
        block.scalar(mk("scalar"))
        block.gpsimd(mk("gpsimd"))


class KB:
    def __init__(self, dbg=(), stop_after=None):
        self.stop_after = stop_after
        self.nc = bass.Bass("TRN2", target_bir_lowering=False)
        self.p = PB()
        self.dbg = dict(dbg)
        self.dbg_out = {}
        self.inp = {}

    def A(self, off, shape, p0=0, pn=128):
        return av(self.ar, off, shape, p0, pn)

    def mm(self, outap, lhsT, rhs, start, stop, wait=(), inc=None, tp=None):
        def f(t, o=outap, l=lhsT, r=rhs, st=start, sp=stop, tp=tp):
            kw = {} if tp is None else {"tile_position": tp}
            return t.matmul(o, l, r, start=st, stop=sp, **kw)
        return self.p.do("tensor", f, wait=wait, inc=inc)

    def act(self, outap, inap, func, wait=(), inc=None, bias=0.0, scale=1.0,
            accum=None):
        def f(s, o=outap, i=inap, fu=func, b=bias, sc=scale, a=accum):
            kw = {} if a is None else {"accum_out": a}
            return s.activation(out=o, in_=i, func=fu, bias=b, scale=sc, **kw)
        return self.p.do("scalar", f, wait=wait, inc=inc)

    def scopy(self, outap, inap, wait=(), inc=None):
        return self.p.do("scalar", lambda s, o=outap, i=inap: s.copy(o, i),
                         wait=wait, inc=inc)

    def v(self, fname, *args, wait=(), inc=None, **kw):
        return self.p.do("vector",
                         lambda ve, fn=fname, a=args, k=kw: getattr(ve, fn)(*a, **k),
                         wait=wait, inc=inc)

    def g(self, fname, *args, wait=(), inc=None, **kw):
        return self.p.do("gpsimd",
                         lambda ge, fn=fname, a=args, k=kw: getattr(ge, fn)(*a, **k),
                         wait=wait, inc=inc)

    def dump(self, name, src_ap, wait):
        if name in self.dbg:
            self.p.dma("sync", self.dbg_out[name], src_ap, wait=wait)

    def declare_io(self):
        nc = self.nc

        def di(name, shape, dt=F32):
            self.inp[name] = nc.dram_tensor(name, list(shape), dt,
                                            kind="ExternalInput").ap()
            return self.inp[name]

        self.xg = di("xg", [128, 8, T])
        self.xo = di("xo", [128, 8, NRUN, RUNTOK])
        self.crow = di("crow", [D])
        self.wmod = di("wmod", [D, 6 * D])
        self.qkvwt = di("qkvwt", [D, 3 * D])
        self.qkvb = di("qkvb", [3 * D])
        self.wwt = di("wwt", [D, 256])
        self.memvwt = di("memvwt", [D, D])
        self.memvb = di("memvb", [D])
        self.memgwt = di("memgwt", [D, H])
        self.memgb = di("memgb", [H])
        self.mscale = di("mscale", [H])
        self.imix = di("imix", [1])
        self.outwt = di("outwt", [D, D])
        self.outb = di("outb", [D])
        self.dlog = di("dlog", [H])
        self.wgt = di("wgt", [D, E])
        self.w1t = di("w1t", [E, D, HID])
        self.w2t = di("w2t", [E, HID, D])
        self.w3t = di("w3t", [E, D, HID])
        self.b1 = di("b1", [E, HID])
        self.b2 = di("b2", [E, D])
        self.b3 = di("b3", [E, HID])
        self.a1t = di("a1t", [D, E * R])
        self.a3t = di("a3t", [D, E * R])
        self.b1t = di("b1t", [E, R, HID])
        self.b3t = di("b3t", [E, R, HID])
        self.a2 = di("a2", [E, R, HID])
        self.b2t = di("b2t", [E, R, D])
        self.masksel = di("masksel", [NRUN * 32], I32)
        self.maskt = di("maskt", [32, 128, 512])
        self.runoff = di("runoff", [8], I32)

        self.out = nc.dram_tensor("out", [OWN, D], F32, kind="ExternalOutput").ap()
        for name, shape in self.dbg.items():
            self.dbg_out[name] = nc.dram_tensor(
                "dbg_" + name, list(shape), F32, kind="ExternalOutput").ap()

        self.hsp = nc.dram_tensor("hsp", [128, 8, SPAD + T], F32, kind="Internal").ap()
        self.dscr = nc.dram_tensor("dscr", [1 << 18], F32, kind="Internal").ap()
        self.w2scr = nc.dram_tensor("w2scr", [E, HID, D], F32, kind="Internal").ap()


    def p0(self):
        p, A = self.p, self.A
        p.dma("sync", A(C_QB, [24]), self.qkvb.rearrange("(c p) -> p c", p=128))
        p.dma("sync", A(C_OB, [8]), self.outb.rearrange("(c p) -> p c", p=128))
        p.dma("sync", A(C_MVB, [8]), self.memvb.rearrange("(c p) -> p c", p=128))
        p.dma("sync", A(C_B1, [8, 16]), self.b1.rearrange("e (c p) -> p e c", p=128))
        p.dma("sync", A(C_B3, [8, 16]), self.b3.rearrange("e (c p) -> p e c", p=128))
        t_r2 = p.dma("sync", A(R_MGB, [16], pn=1), self.memgb[None, :])
        t_r3 = p.dma("sync", A(R_DSG, [16], pn=1), self.dlog[None, :])
        t_r4 = p.dma("sync", A(R_MSC, [16], pn=1), self.mscale[None, :])
        t_r5 = p.dma("sync", A(R_IMX, [1], pn=1), self.imix[None, :])
        t_cc = p.dma("sync", A(P0_CSC, [8, 1]),
                     self.crow.rearrange("(c p) -> p c", p=128)[:, :, None])
        p.dma("sync", self.itab[0:1, 0:128], self.masksel[None, :])
        p.dma("sync", self.itab[0:1, 128:136], self.runoff[None, :])

        self.g("memset", A(C_UCL, [128]), 1.0)
        self.g("affine_select", A(C_UCL, [128]), A(C_UCL, [128]),
               pattern=[[1, 128]], compare_op=OP.is_gt, fill=0.0, base=0,
               channel_multiplier=-1)
        self.g("iota", A(R_IOTA, [128], pn=1), pattern=[[1, 128]], base=0,
               channel_multiplier=0, allow_small_or_imprecise_dtypes=True)

        self._slot = [A3 + 8000]
        def _stage_dump(name, off, n, waits, pn=1, src_ap=None):
            slot = self._slot[0]
            self._slot[0] += n
            src = src_ap if src_ap is not None else A(off, [n], pn=pn)
            tcpx = self.v("tensor_copy", A(slot, [n], pn=pn), src,
                          wait=waits, inc="tV")
            self.dump(name, A(slot, [n], pn=pn), [tcpx])
        self.g("memset", A(C_ID, [128]), 1.0)
        self.g("affine_select", A(C_ID, [128]), A(C_ID, [128]),
               pattern=[[1, 128]], compare_op=OP.is_equal, fill=0.0, base=0,
               channel_multiplier=-1)
        self.g("memset", A(C_ONEC, [1]), 1.0)
        self.g("memset", A(R_ONES, [128], p0=64, pn=1), 1.0)
        t_g = self.g("memset", A(R_ONES, [128], pn=1), 1.0, inc="tG")

        t_dsg = self.act(A(R_DSG, [16], pn=1), A(R_DSG, [16], pn=1), AF.Sigmoid,
                         wait=[t_r3], inc="tS")
        t_lnd = self.act(A(R_LND, [16], pn=1), A(R_DSG, [16], pn=1), AF.Ln,
                         wait=[t_dsg], inc="tS")
        t_c128 = self.act(A(R_C128, [16], pn=1), A(R_LND, [16], pn=1), AF.Exp,
                          scale=128.0, wait=[t_lnd], inc="tS")
        t_al = self.act(A(R_IMX, [1], pn=1), A(R_IMX, [1], pn=1), AF.Sigmoid,
                        wait=[t_r5], inc="tS")
        t_m0 = self.mm(self.ps[:, 0, 0:1], A(R_ONES, [128], pn=1),
                       A(R_IMX, [1], pn=1), True, True, wait=[t_al, t_g], inc="tT")
        t_a1 = self.scopy(A(C_APM, [1]), self.ps[:, 0, 0:1], wait=[t_m0], inc="tS")
        t_a2 = self.act(A(C_APM + 1, [1]), self.ps[:, 0, 0:1], AF.Copy, scale=-1.0,
                        bias=1.0, wait=[t_m0], inc="tS")
        t_m1 = self.mm(self.ps[:, 1, 0:16], A(R_IOTA, [128], pn=1),
                       A(R_LND, [16], pn=1), True, True,
                       wait=[t_lnd, t_g, self.p.tok("tV")], inc="tT")
        t_ci = self.act(A(C_CIH, [16]), self.ps[:, 1, 0:16], AF.Exp, scale=-0.5,
                        wait=[t_m1], inc="tS")
        self.act(A(C_CFW, [16]), self.ps[:, 1, 0:16], AF.Exp, scale=1.0, inc="tS")
        self.act(A(C_CF2, [16]), self.ps[:, 1, 0:16], AF.Exp, scale=2.0, inc="tS")
        t_m2 = self.mm(self.ps[:, 2, 0:16], A(R_ONES, [128], pn=1),
                       A(R_MSC, [16], pn=1), True, True, wait=[t_r4, t_g], inc="tT")
        self.scopy(A(C_MSB, [16]), self.ps[:, 2, 0:16], wait=[t_m2], inc="tS")
        t_sg1 = self.g("memset", A(C_CIHS, [16, 6]), 1.0)
        t_sg2 = self.v("tensor_scalar_mul",
                       bass.AP(self.ar, C_CIHS + 1, [[ARN, 128], [6, 16], [3, 2]]),
                       bass.AP(self.ar, C_CIHS + 1, [[ARN, 128], [6, 16], [3, 2]]),
                       -1.0, wait=[t_sg1], inc="tV")
        t_sg3 = self.p.do("vector", lambda ve: ve.drain(), inc="tV")
        cihb0 = bass.AP(self.ar, C_CIH, [[ARN, 128], [1, 16], [0, 6]])
        t_cihs = self.v("tensor_mul", A(C_CIHS, [16, 6]), A(C_CIHS, [16, 6]),
                        cihb0, wait=[t_sg3, t_ci], inc="tV")
        _stage_dump("cihsf", C_CIHS, 96, [t_cihs], pn=128)

        t_sc = self.act(A(P0_CSC, [8, 1]), A(P0_CSC, [8, 1]), AF.Silu,
                        wait=[t_cc], inc="tS")
        prev = None
        for j in range(12):
            tw = p.dma("sync", A(P0_WMB, [8, 512]),
                       self.wmod.rearrange("(c p) m -> p c m", p=128)
                       [:, :, j * 512:(j + 1) * 512], wait=[p.tok("tT")])
            last = None
            for cc in range(8):
                w = [tw, t_sc] + ([prev] if (prev and cc == 0) else [])
                last = self.mm(self.ps[0:1, 3, :], A(P0_CSC + cc, [1]),
                               A(P0_WMB + cc * 512, [512]), cc == 0, cc == 7,
                               wait=w, inc=("tT" if cc == 7 else None))
            prev = self.scopy(A(P0_MODROW + j * 512, [512], pn=1),
                              self.ps[0:1, 3, :], wait=[last], inc="tS")
        t_md = p.dma("sync", self.dscr[None, DS_MOD:DS_MOD + 6144],
                     A(P0_MODROW, [6144], pn=1), wait=[prev])
        for vv in range(6):
            src = bass.AP(self.dscr.tensor, DS_MOD + vv * 1024,
                          [[1, 128], [128, 8]])
            p.dma("sync", A(C_MOD + 8 * vv, [8]), src, wait=[t_md])
        t_ml = p.tok("dS")
        tmp = P0_WMB
        for vv in (2, 5):
            gv = A(C_MOD + 8 * vv, [8])
            self.v("tensor_scalar_max", A(tmp, [8]), gv, 0.25, wait=[t_ml])
            self.v("tensor_scalar_min", A(tmp + 8, [8]), gv, -0.25)
            self.v("tensor_scalar", A(tmp + 16, [8]), gv, 0.0, None, op0=OP.is_ge)
            self.p.do("vector", lambda ve: ve.drain())
            self.v("tensor_sub", A(tmp, [8]), A(tmp, [8]), A(tmp + 8, [8]))
            self.p.do("vector", lambda ve: ve.drain())
            self.v("tensor_mul", A(tmp, [8]), A(tmp + 16, [8]), A(tmp, [8]))
            self.p.do("vector", lambda ve: ve.drain())
            self.v("tensor_add", gv, A(tmp, [8]), A(tmp + 8, [8]), inc="tV")
            self.p.do("vector", lambda ve: ve.drain())
        for vv in (1, 4):
            gv = A(C_MOD + 8 * vv, [8])
            self.v("tensor_scalar_add", gv, gv, 1.0, inc="tV")
        t_mod = p.tok("tV")
        p.dma("sync", bass.AP(self.dscr.tensor, DS_GBROW, [[1, 128], [128, 8]]),
              A(C_MOD + 8 * 5, [8]), wait=[t_mod])
        p.barrier()

    def ln_h_pass(self, src_dram, nchunk, mrow, srow, out_spill, own):
        p, A = self.p, self.A
        for ch in range(nchunk):
            tx = p.dma("sync", A(P1_XB, [8, 512]), src_dram[:, :, ch * 512:(ch + 1) * 512],
                       wait=[p.tok("tS"), p.tok("tT")])
            tsq = []
            for cc in range(8):
                tsq.append(self.act(A(P1_HB + cc * 512, [512]),
                                    A(P1_XB + cc * 512, [512]), AF.Square,
                                    wait=[tx] if cc == 0 else [], inc="tS"))
            lastm = lasts = None
            for cc in range(8):
                w = [tx] if cc == 0 else []
                lastm = self.mm(self.ps[0:1, 0, :], A(C_ONEC, [1]),
                                A(P1_XB + cc * 512, [512]),
                                cc == 0, cc == 7, wait=w,
                                inc=("tT" if cc == 7 else None))
            for cc in range(8):
                w = [tsq[cc]] if cc == 0 else [tsq[cc]]
                lasts = self.mm(self.ps[0:1, 1, :], A(C_ONEC, [1]),
                                A(P1_HB + cc * 512, [512]), cc == 0, cc == 7,
                                wait=w, inc=("tT" if cc == 7 else None))
            self.act(A(mrow + ch * 512, [512], pn=1), self.ps[0:1, 0, :], AF.Copy,
                     scale=1.0 / D, wait=[lastm], inc="tS")
            t_ev = self.act(A(srow + ch * 512, [512], pn=1), self.ps[0:1, 1, :],
                            AF.Copy, scale=1.0 / D, wait=[lasts], inc="tS")
            p.do("tensor", None, wait=[t_ev, p.tok("tS")])
        n = nchunk * 512
        tv1 = self.act(A(P1_HB, [n], pn=1), A(mrow, [n], pn=1), AF.Square,
                       wait=[p.tok("tS"), p.tok("tV"), p.tok("dG")], inc="tS")
        tv2 = self.v("tensor_sub", A(srow, [n], pn=1), A(srow, [n], pn=1),
                     A(P1_HB, [n], pn=1), wait=[tv1])
        tv2 = self.v("tensor_scalar_add", A(srow, [n], pn=1), A(srow, [n], pn=1),
                     EPS, inc="tV")
        tv3 = self.act(A(srow, [n], pn=1), A(srow, [n], pn=1), AF.Sqrt,
                       wait=[tv2], inc="tS")
        tv4 = self.v("reciprocal", A(srow, [n], pn=1), A(srow, [n], pn=1),
                     wait=[tv3], inc="tV")
        t_rs = self.p.do("vector", lambda ve: ve.drain(), wait=[tv4], inc="tV")
        for ch in range(nchunk):
            tx = p.dma("sync", A(P1_XB, [8, 512]), src_dram[:, :, ch * 512:(ch + 1) * 512],
                       wait=[p.tok("tV")])
            tmb = self.mm(self.ps[:, 2, :], A(R_ONES, [128], pn=1),
                          A(mrow + ch * 512, [512], pn=1), True, True,
                          wait=[t_rs, p.tok("tV")], inc="tT")
            trb = self.mm(self.ps[:, 3, :], A(R_ONES, [128], pn=1),
                          A(srow + ch * 512, [512], pn=1), True, True, inc="tT")
            tdone = []
            for cc in range(8):
                w = [tx, tmb, trb] if cc == 0 else []
                self.v("tensor_sub", A(P1_HB + cc * 512, [512]),
                       A(P1_XB + cc * 512, [512]), self.ps[:, 2, :], wait=w)
                self.v("tensor_mul", A(P1_HB + cc * 512, [512]),
                       A(P1_HB + cc * 512, [512]), self.ps[:, 3, :])
                td = self.v("tensor_scalar", A(P1_HB + cc * 512, [512]),
                            A(P1_HB + cc * 512, [512]),
                            A(C_MOD + 8 * 1 + cc, [1]), A(C_MOD + 8 * 0 + cc, [1]),
                            op0=OP.mult, op1=OP.add, inc="tV")
                tdone.append(td)
            if own:
                tc = self.v("tensor_copy", A(A1 + 0, [8, NRUN, 512])[:, :, ch, :],
                            A(P1_HB, [8, 512]), wait=[tdone[-1]], inc="tV")
                self.p.do("vector", None, wait=[tc])
            else:
                tspill = p.dma("gpsimd", self.hsp[:, :, SPAD + ch * 512:SPAD + (ch + 1) * 512],
                               A(P1_HB, [8, 512]), wait=[tdone[-1]])
                p.do("vector", None, wait=[tspill])
            p.do("tensor", None, wait=[tdone[-1]])

    def p1(self):
        p, A = self.p, self.A
        tz = self.g("memset", A(P1_HB, [8, SPAD]), 0.0, inc="tG")
        p.dma("gpsimd", self.hsp[:, :, 0:SPAD], A(P1_HB, [8, SPAD]), wait=[tz])
        self.ln_h_pass(self.xg, 8, P1_MROW, P1_SROW, True, False)
        p.barrier()
        self.ln_h_pass(self.xo.rearrange("p c r t -> p c (r t)"), 4,
                       P1_MROW, P1_SROW, False, True)
        p.barrier()
        self.dump("h_own", A(A1, [8, NRUN, 512]), [p.tok("tV")])
        self.dump("mrow", A(P1_MROW, [4096], pn=1), [p.tok("tS")])

    PAIRS4 = [(0, 1), (0, 2), (0, 3), (1, 2), (1, 3), (2, 3)]

    def exterior(self, p1off, p2off, outoff, wait):
        A = self.A
        def e6(off, k):
            return bass.AP(self.ar, off + k, [[ARN, 128], [6, 16], [1, 1]])

        def e4(off, k):
            return bass.AP(self.ar, off + k, [[ARN, 128], [4, 16], [1, 1]])

        first = True
        for k, (i, j) in enumerate(self.PAIRS4):
            self.v("tensor_mul", e6(outoff, k), e4(p1off, i), e4(p2off, j),
                   wait=(wait if first else ()))
            first = False
            self.v("tensor_mul", e6(P2_SQ, k), e4(p1off, j), e4(p2off, i))
        self.p.do("vector", lambda ve: ve.drain())
        self.v("tensor_sub", A(outoff, [16, 6]), A(outoff, [16, 6]),
               A(P2_SQ, [16, 6]))
        self.p.do("vector", lambda ve: ve.drain())
        self.v("tensor_mul", A(P2_SQ, [16, 6]), A(outoff, [16, 6]),
               A(outoff, [16, 6]))
        t0 = self.v("reduce_sum", A(P2_N2, [16]), A(P2_SQ, [16, 6]), axis=AX.X,
                    inc="tV")
        t1 = self.p.do("vector", lambda ve: ve.drain(), inc="tV")
        t1b = self.v("tensor_scalar_max", A(P2_N2, [16]), A(P2_N2, [16]), 1e-24,
                     wait=[t1], inc="tV")
        ts = self.act(A(P2_N2, [16]), A(P2_N2, [16]), AF.Sqrt, wait=[t1b], inc="tS")
        t2 = self.v("reciprocal", A(P2_N2, [16]), A(P2_N2, [16]), wait=[ts], inc="tV")
        t3 = self.p.do("vector", lambda ve: ve.drain(), inc="tV")
        rn_b = bass.AP(self.ar, P2_N2, [[ARN, 128], [1, 16], [0, 6]])
        t4 = self.v("tensor_mul", A(outoff, [16, 6]), A(outoff, [16, 6]), rn_b,
                    wait=[t3], inc="tV")
        return t4

    def p2(self):
        p, A = self.p, self.A
        WWTS = A2 + 5280
        t_ww = p.dma("sync", A(WWTS, [8, 256]),
                     self.wwt.rearrange("(c p) m -> p c m", p=128))
        MGWS = A2 + 7328
        t_mg = p.dma("sync", A(MGWS, [8, 16]),
                     self.memgwt.rearrange("(c p) m -> p c m", p=128))
        t_cz = self.g("memset", A(P2_CROW, [576], pn=1), 0.0, inc="tG")

        cihb = bass.AP(self.ar, C_CIH, [[ARN, 128], [1, 16], [0, 6]])
        t_prev_upd = t_cz
        for b in range(NB):
            wb = P2_WB + (b % 2) * 1056
            t_w = p.dma("gpsimd", A(wb, [8, 132])[:, :, 0:129],
                        self.hsp[:, :, SPAD + b * 128 - 1: SPAD + b * 128 + 128],
                        wait=[])
            l0 = l1 = None
            for c in range(8):
                w = [t_w, t_ww] if c == 0 else []
                l0 = self.mm(self.ps[:, 0, 0:64], A(wb, [8, 132])[:, c, 0:128],
                             A(WWTS, [8, 256])[:, c, 0:64], c == 0, c == 7,
                             wait=w, inc=("tT" if c == 7 else None))
            for c in range(8):
                l1 = self.mm(self.ps[:, 1, 0:192], A(wb, [8, 132])[:, c, 1:129],
                             A(WWTS, [8, 256])[:, c, 64:256], c == 0, c == 7,
                             inc=("tT" if c == 7 else None))
            self.v("tensor_copy", A(P2_W1L, [16, 4]),
                   self.ps[:, 0, 0:64].rearrange("p (h f) -> p h f", h=16),
                   wait=[l0])
            t_ev = self.v("tensor_copy", A(P2_W2L, [16, 4]),
                          self.ps[:, 1, 0:64].rearrange("p (h f) -> p h f", h=16),
                          wait=[l1])
            self.v("tensor_copy", A(P2_R1L, [16, 4]),
                   self.ps[:, 1, 64:128].rearrange("p (h f) -> p h f", h=16))
            t_ev2 = self.v("tensor_copy", A(P2_R2L, [16, 4]),
                           self.ps[:, 1, 128:192].rearrange("p (h f) -> p h f", h=16),
                           inc="tV")
            if b == 1:
                self.dump("w1l", A(P2_W1L, [16, 4]), [t_ev2])
                self.dump("w2l", A(P2_W2L, [16, 4]), [t_ev2])
            if b == 1:
                self.dump("lw1b", A(P2_LW, [16, 6]), [self.p.tok("tV")])
            t_extw = self.exterior(P2_W1L, P2_W2L, P2_LW, [t_ev2])
            if b == 1:
                self.dump("lw", A(P2_LW, [16, 6]), [t_extw])
                self.dump("jws1", A(P2_JWS, [16, 6]), [self.p.tok("tV"), t_extw])
            rev = bass.AP(self.ar, P2_LW + 5, [[ARN, 128], [6, 16], [-1, 6]])
            self.v("tensor_copy", A(P2_JWS, [16, 6]), rev)
            t_jw = self.v("tensor_mul", A(P2_JWS, [16, 6]), A(P2_JWS, [16, 6]),
                          A(C_CIHS, [16, 6]), inc="tV")
            jw_j = bass.AP(self.ar, P2_JWS, [[ARN, 128], [6, 16], [1, 6], [0, 6]])
            jw_i = bass.AP(self.ar, P2_JWS, [[ARN, 128], [6, 16], [0, 6], [1, 6]])
            t_xp = self.v("tensor_mul", A(P2_XP, [16, 6, 6]), jw_j, jw_i, inc="tV")
            if b == 0:
                self.dump("jws", A(P2_JWS, [16, 6]), [t_xp])
                self.dump("xp", A(P2_XP, [16, 6, 6]), [t_xp])
            t_rd = self.exterior(P2_R1L, P2_R2L, P2_RD, [t_ev2])
            tE = []
            for hh in range(2):
                m0 = self.mm(self.ps[:, 2 + hh, 0:288], A(C_UCL, [128]),
                             A(P2_XP + 288 * hh, [288]), True, False,
                             wait=[t_xp, t_prev_upd])
                m1 = self.mm(self.ps[:, 2 + hh, 0:288], A(R_ONES, [128], pn=1),
                             A(P2_CROW + 288 * hh, [288], pn=1), False, True,
                             inc="tT")
                tE.append(m1)
            tT_ = []
            for hh in range(2):
                m = self.mm(self.ps[0:1, 4 + hh, 0:288], A(C_ONEC, [1]),
                            A(P2_XP + 288 * hh, [288]), True, True, inc="tT")
                tT_.append(m)
            e1 = self.scopy(A(P2_TROW, [288], pn=1), self.ps[0:1, 4, 0:288],
                            wait=[tT_[0]], inc="tS")
            e2 = self.scopy(A(P2_TROW + 288, [288], pn=1), self.ps[0:1, 5, 0:288],
                            wait=[tT_[1]], inc="tS")
            g1 = self.g("tensor_add", A(P2_CROW, [576], pn=1),
                        A(P2_CROW, [576], pn=1), A(P2_TROW, [576], pn=1),
                        wait=[e1, e2] + tE)
            c128b = bass.AP(self.ar, R_C128, [[ARN, 1], [1, 16], [0, 36]])
            t_prev_upd = self.g("tensor_mul", A(P2_CROW, [576], pn=1),
                                bass.AP(self.ar, P2_CROW, [[ARN, 1], [36, 16], [1, 36]]),
                                c128b, inc="tG")
            if b == 0:
                tcw = self.v("tensor_copy", av(self.ar, A3 + 8000, [576], 0, 1),
                             A(P2_CROW, [576], pn=1), wait=[t_prev_upd], inc="tV")
                self.dump("crow1", av(self.ar, A3 + 8000, [576], 0, 1), [tcw])
                tcw2 = self.v("tensor_copy", av(self.ar, A3 + 8600, [576], 0, 1),
                              A(P2_TROW, [576], pn=1), wait=[t_prev_upd], inc="tV")
                self.dump("trow0", av(self.ar, A3 + 8600, [576], 0, 1), [tcw2])
            rd_i = bass.AP(self.ar, P2_RD, [[ARN, 128], [6, 16], [0, 6], [1, 6]])
            self.v("tensor_mul", A(P2_PP, [16, 6, 6])[:, 0:8, :, :],
                   self.ps[:, 2, 0:288].rearrange("p (h j i) -> p h j i", h=8, j=6),
                   bass.AP(self.ar, P2_RD, [[ARN, 128], [6, 8], [0, 6], [1, 6]]),
                   wait=[t_rd, tE[0]])
            self.v("tensor_mul", A(P2_PP, [16, 6, 6])[:, 8:16, :, :],
                   self.ps[:, 3, 0:288].rearrange("p (h j i) -> p h j i", h=8, j=6),
                   bass.AP(self.ar, P2_RD + 48, [[ARN, 128], [6, 8], [0, 6], [1, 6]]),
                   wait=[tE[1]])
            t_r0 = self.v("reduce_sum", A(P2_RDM, [16, 6]), A(P2_PP, [16, 6, 6]),
                          axis=AX.X, inc="tV")
            if b == 1:
                self.dump("pp1", A(P2_PP, [16, 6, 6]), [t_r0])
                self.dump("rd1", A(P2_RD, [16, 6]), [t_r0])
            self.v("tensor_mul", A(P2_SQ, [16, 6]), A(P2_RDM, [16, 6]),
                   A(P2_RD, [16, 6]))
            self.v("reduce_sum", A(P2_S1, [16]), A(P2_SQ, [16, 6]), axis=AX.X)
            self.v("tensor_mul", A(P2_SQ, [16, 6]), A(P2_RDM, [16, 6]),
                   A(P2_RDM, [16, 6]))
            t_r1 = self.v("reduce_sum", A(P2_S2, [16]), A(P2_SQ, [16, 6]),
                          axis=AX.X, inc="tV")
            t_dr = self.p.do("vector", lambda ve: ve.drain(), inc="tV")
            if b == 0:
                self.dump("rdm0", A(P2_RDM, [16, 6]), [t_dr])
                self.dump("s10", A(P2_S1, [16]), [t_dr])
                self.dump("s20", A(P2_S2, [16]), [t_dr])
                tec = self.v("tensor_copy", A(A2 + 8192, [288]), self.ps[:, 2, 0:288],
                             wait=[t_dr], inc="tV")
                self.dump("e0", A(A2 + 8192, [288]), [tec])
            self.v("tensor_mul", A(P2_S1, [16]), A(P2_S1, [16]), A(C_CFW, [16]),
                   wait=[t_dr])
            self.v("tensor_mul", A(P2_S2, [16]), A(P2_S2, [16]), A(C_CF2, [16]))
            self.v("tensor_scalar", A(P2_S1, [16]), A(P2_S1, [16]),
                   A(C_APM + 1, [1]), None, op0=OP.mult)
            t_ms = self.v("scalar_tensor_tensor", A(P2_MSCB, [16]),
                          A(P2_S2, [16]), A(C_APM, [1]), A(P2_S1, [16]),
                          op0=OP.mult, op1=OP.add, inc="tV")
            t_out = p.dma("sync",
                          bass.AP(self.dscr.tensor, DS_MSC + b * 2048,
                                  [[16, 128], [1, 16]]),
                          A(P2_MSCB, [16]), wait=[t_ms])
            p.do("tensor", None, wait=[p.tok("tV"), p.tok("tS"), t_out])
        for r in range(NRUN):
            for tt in range(4):
                last = None
                for c in range(8):
                    last = self.mm(self.ps[:, 6, 0:16],
                                   A(A1 + (c * NRUN + r) * 512 + tt * 128, [128]),
                                   A(MGWS, [8, 16])[:, c, :], c == 0, False,
                                   wait=([t_mg] if c == 0 else []))
                last = self.mm(self.ps[:, 6, 0:16], A(R_ONES, [128], pn=1),
                               A(R_MGB, [16], pn=1), False, True, inc="tT")
                t_sg = self.act(A(P2_GT, [16]), self.ps[:, 6, 0:16], AF.Sigmoid,
                                wait=[last], inc="tS")
                t_go = p.dma("sync",
                             bass.AP(self.dscr.tensor,
                                     DS_GATE + (r * 4 + tt) * 2048,
                                     [[16, 128], [1, 16]]),
                             A(P2_GT, [16]), wait=[t_sg])
                p.do("tensor", None, wait=[t_go])
        p.barrier()
        self.dump("msc", self.dscr[DS_MSC:DS_MSC + NB * 2048],
                  [p.tok("dS")])
        self.dump("gate", self.dscr[DS_GATE:DS_GATE + 16 * 2048],
                  [p.tok("dS")])

    def p3(self):
        p, A = self.p, self.A
        t_atl = self.g("memset", A(XSPARE + 200, [1]), 0.0, inc="tG")
        nmask = [0]

        for pair in range(NPAIR):
            t_prevpair = [p.tok("tT"), p.tok("tV"), p.tok("tS")]
            t_wk = p.dma("sync", A(P3_WS, [8, 128]),
                         self.qkvwt.rearrange("(c p) m -> p c m", p=128)
                         [:, :, D + pair * 128: D + (pair + 1) * 128],
                         wait=t_prevpair)
            t_wv = p.dma("sync", A(P3_WS + 1024, [8, 128]),
                         self.qkvwt.rearrange("(c p) m -> p c m", p=128)
                         [:, :, 2 * D + pair * 128: 2 * D + (pair + 1) * 128])
            self.v("memset", bass.AP(self.ar, P3_VE + 64, [[ARN, 128], [130, 32], [1, 1]]), 1.0,
                   wait=t_prevpair)
            t_vo = self.v("memset", bass.AP(self.ar, P3_VE + 129, [[ARN, 128], [130, 32], [1, 1]]),
                          1.0, inc="tV")
            t_hf_prev = p.tok("tT")
            for gc in range(16):
                t_hf = p.dma("gpsimd", A(P3_HF, [8, 256]),
                             self.hsp[:, :, SPAD + gc * 256: SPAD + (gc + 1) * 256],
                             wait=[t_hf_prev])
                lk = None
                for c in range(8):
                    w = [t_hf, t_wk, p.tok("tS")] if c == 0 else []
                    lk = self.mm(self.ps[:, 5, 0:256], A(P3_WS, [8, 128])[:, c, :],
                                 A(P3_HF, [8, 256])[:, c, :], c == 0, c == 7,
                                 wait=w, inc=("tT" if c == 7 else None))
                t_ke = self.act(A(P3_KT + gc * 256, [256]), self.ps[:, 5, 0:256],
                                AF.Identity, bias=A(C_QB + 8 + pair, [1]),
                                wait=[lk], inc="tS")
                t_ve = None
                for jj in range(2):
                    kt = gc * 2 + jj
                    lv = None
                    for c in range(8):
                        w = [t_hf, t_wv, t_vo, p.tok("tV")] if c == 0 else []
                        lv = self.mm(self.ps[:, 6, 0:128],
                                     A(P3_HF, [8, 256])[:, c, jj * 128:(jj + 1) * 128],
                                     A(P3_WS + 1024, [8, 128])[:, c, :],
                                     c == 0, c == 7, wait=w,
                                     inc=("tT" if c == 7 else None))
                    t_ve = self.v("tensor_copy",
                                  bass.AP(self.ar, P3_VE + kt * 130, [[ARN, 128], [65, 2], [1, 64]]),
                                  self.ps[:, 6, 0:128].rearrange("p (h f) -> p h f", h=2),
                                  wait=[lv], inc="tV")
                    p.do("tensor", None, wait=[t_ve])
                t_hf_prev = p.tok("tT")
                p.do("gpsimd", None, wait=[t_ke, t_ve])
            if pair == 3:
                self.dump("kt0", A(P3_KT, [32, 128]), [p.tok("tS")])
                self.dump("ve0", A(P3_VE, [32, 130]), [p.tok("tV")])
            t_wq = p.dma("sync", A(P3_WS, [8, 128]),
                         self.qkvwt.rearrange("(c p) m -> p c m", p=128)
                         [:, :, pair * 128:(pair + 1) * 128],
                         wait=[p.tok("tT")])
            for r in range(NRUN):
                lq = None
                for c in range(8):
                    w = [t_wq, p.tok("tS")] if c == 0 else []
                    lq = self.mm(self.ps[:, 5, :], A(P3_WS, [8, 128])[:, c, :],
                                 A(A1 + (c * NRUN + r) * 512, [512]),
                                 c == 0, c == 7, wait=w,
                                 inc=("tT" if c == 7 else None))
                t_qe = self.act(A(P3_QT, [512]), self.ps[:, 5, :], AF.Identity,
                                bias=A(C_QB + pair, [1]),
                                wait=[lq, p.tok("tT")], inc="tS")
                if pair == 4 and r == 0:
                    self.dump("qt0", A(P3_QT, [512]), [t_qe])
                kl = KLEN[r]
                for head in range(2):
                    hp0 = head * 64
                    tp = (64, 0) if head else None
                    t_prevav = getattr(self, "_t_lastav", None)
                    t_preve = None
                    for kt in range(kl):
                        w_s = [t_qe, t_atl] + ([t_preve] if t_preve else [])
                        t_s = self.mm(self.ps[:, head, :],
                                      A(P3_KT + kt * 128, [128], p0=hp0, pn=64),
                                      A(P3_QT, [512], p0=hp0, pn=64),
                                      True, True, wait=w_s, inc="tT",
                                      tp=tp)
                        masked = kt >= kl - 8
                        if masked:
                            mi = r * 8 + (kt - (kl - 8))
                            mslot = P3_ATL + (nmask[0] % 2) * 512
                            nmask[0] += 1
                            t_ml = p.dma("gpsimd", A(mslot, [512]),
                                         self.maskt[mi], wait=[p.tok("tV")])
                        w_exp = [t_s] + ([t_prevav] if t_prevav else [])
                        t_e = self.act(A(P3_PB, [512]), self.ps[:, head, :],
                                       AF.Exp, scale=SCALE, wait=w_exp, inc="tS")
                        t_preve = t_e
                        t_m = t_e
                        if masked:
                            t_m = self.v("tensor_mul", A(P3_PB, [512]),
                                         A(P3_PB, [512]), A(mslot, [512]),
                                         wait=[t_e, t_ml], inc="tV")
                        w_av = [t_m] + ([p.tok("tS"), p.tok("tV")] if kt == 0 else [])
                        t_prevav = self.mm(self.ps[0:65, 2 + head, :],
                                           A(P3_VE + kt * 130 + head * 65, [65]),
                                           A(P3_PB, [512]), kt == 0, kt == kl - 1,
                                           wait=w_av, inc="tT")
                        self._t_lastav = t_prevav
                    self.scopy(av(self.ar, P3_OS, [512], 0, 64),
                               self.ps[0:64, 2 + head, :],
                               wait=[t_prevav, p.tok("tV")], inc="tS")
                    t_oe = self.scopy(av(self.ar, P3_OS, [512], 64, 1),
                                      self.ps[64:65, 2 + head, :],
                                      wait=[t_prevav], inc="tS")
                    if pair == 0 and r == 0 and head == 0:
                        self.dump("os0", av(self.ar, P3_OS, [512], 0, 65), [t_oe])
                    t_rc = self.v("reciprocal", av(self.ar, P3_OS, [512], 64, 1),
                                  av(self.ar, P3_OS, [512], 64, 1), wait=[t_oe],
                                  inc="tV")
                    t_dr = self.p.do("vector", lambda ve: ve.drain(), inc="tV")
                    t_rb = self.mm(self.ps[0:64, 4, :],
                                   A(R_ONES, [64], p0=64, pn=1),
                                   av(self.ar, P3_OS, [512], 64, 1), True, True,
                                   wait=[t_dr, p.tok("tV")], inc="tT")
                    dst = av(self.ar, A2 + (pair * NRUN + r) * 512, [512],
                             head * 64, 64)
                    self.v("tensor_mul", dst, av(self.ar, P3_OS, [512], 0, 64),
                           self.ps[0:64, 4, :], wait=[t_rb])
                    t_sq = self.v("tensor_scalar_add", dst, dst,
                                  A(C_QB + 16 + pair, [1], p0=head * 64, pn=64),
                                  inc="tV")
                    p.do("tensor", None, wait=[t_sq])
        p.barrier()
        self.dump("seqo", A(A2, [8, NRUN, 512]), [p.tok("tV")])

    def p4(self):
        p, A = self.p, self.A
        t_ms = p.dma("sync", A(P4_MSC, [32, 16]),
                     self.dscr[DS_MSC:DS_MSC + NB * 2048]
                     .rearrange("(b p f) -> p b f", p=128, f=16)
                     if False else
                     bass.AP(self.dscr.tensor, DS_MSC,
                             [[16, 128], [2048, 32], [1, 16]]))
        t_gt = p.dma("sync", A(P4_GATE, [16, 16]),
                     bass.AP(self.dscr.tensor, DS_GATE,
                             [[16, 128], [2048, 16], [1, 16]]))
        msb_b = bass.AP(self.ar, C_MSB, [[ARN, 128], [0, 4], [1, 16]])
        for r in range(NRUN):
            def mscop(ve, r=r):
                off = ve.value_load(self.itab[0:1, 128 + r:129 + r])
                src = bass.AP(self.ar, off, [[ARN, 128], [16, 4], [1, 16]])
                return ve.tensor_tensor(av(self.ar, P4_TMP, [4, 16]), src, msb_b,
                                        op=OP.mult)
            t1 = p.do("vector", mscop, wait=[t_ms, t_gt], inc="tV")
            t2 = self.act(A(P4_TMP, [4, 16]), A(P4_TMP, [4, 16]), AF.Sigmoid,
                          wait=[t1], inc="tS")
            t3 = self.v("tensor_mul", A(P4_TMP, [4, 16]), A(P4_TMP, [4, 16]),
                        A(P4_GATE, [16, 16])[:, r * 4:(r + 1) * 4, :], wait=[t2])
            self.p.do("vector", lambda ve: ve.drain())
            t4 = self.v("reduce_sum", A(P4_GCOL + r * 4, [4]), A(P4_TMP, [4, 16]),
                        axis=AX.X, inc="tV")
        t_dr = self.p.do("vector", lambda ve: ve.drain(), inc="tV")
        t5 = self.v("tensor_scalar_mul", A(P4_GCOL, [16]), A(P4_GCOL, [16]),
                    1.0 / 16, wait=[t_dr], inc="tV")
        t_tr = self.p.do("tensor", lambda t: t.transpose(
            self.ps[0:16, 0, 0:128], av(self.ar, P4_GCOL, [16]),
            av(self.ar, C_ID, [128])), wait=[t5], inc="tT")
        t_ge = self.scopy(av(self.ar, P4_TMP, [128], 0, 16), self.ps[0:16, 0, 0:128],
                          wait=[t_tr], inc="tS")
        t_gw = p.dma("sync",
                     bass.AP(self.dscr.tensor, DS_GROW, [[128, 16], [1, 128]]),
                     av(self.ar, P4_TMP, [128], 0, 16), wait=[t_ge])
        t_gr = p.dma("sync", A(P4_GROW, [2048], pn=1),
                     self.dscr[None, DS_GROW:DS_GROW + 2048],
                     wait=[t_gw])
        self.dump("gcol", A(P4_GCOL, [16]), [t5])
        tgrs = self.v("tensor_copy", av(self.ar, P4_TMP + 200, [2048], 0, 1),
                      A(P4_GROW, [2048], pn=1), wait=[t_gr], inc="tV")
        self.dump("grow", av(self.ar, P4_TMP + 200, [2048], 0, 1), [tgrs])
        MVWS = A3 + 7040
        OWS = A3 + 8064
        MVT = A3 + 9088
        for r in range(NRUN):
            t_xo = p.dma("sync", A(P4_XO, [8, 512]), self.xo[:, :, r, :],
                         wait=[p.tok("tV")])
            t_gb = self.mm(self.ps[:, 1, :], A(R_ONES, [128], pn=1),
                           A(P4_GROW + r * 512, [512], pn=1), True, True,
                           wait=[t_gr, p.tok("tV")], inc="tT")
            for dc in range(8):
                t_wv = p.dma("sync", A(MVWS, [8, 128]),
                             self.memvwt.rearrange("(c p) m -> p c m", p=128)
                             [:, :, dc * 128:(dc + 1) * 128], wait=[p.tok("tT")])
                lm = None
                for c in range(8):
                    w = [t_wv, p.tok("tS")] if c == 0 else []
                    lm = self.mm(self.ps[:, 0, :], A(MVWS, [8, 128])[:, c, :],
                                 A(A1 + (c * NRUN + r) * 512, [512]),
                                 c == 0, c == 7, wait=w,
                                 inc=("tT" if c == 7 else None))
                t_me = self.act(A(MVT, [512]), self.ps[:, 0, :], AF.Identity,
                                bias=A(C_MVB + dc, [1]),
                                wait=[lm, p.tok("tV")], inc="tS")
                tmvg = self.v("tensor_mul", A(MVT, [512]), A(MVT, [512]),
                       self.ps[:, 1, :], wait=[t_me, t_gb], inc="tV")
                if r == 0 and dc == 0:
                    self.dump("mvg0", A(MVT, [512]), [tmvg])
                t_ac = self.v("tensor_add",
                              av(self.ar, A2 + (dc * NRUN + r) * 512, [512]),
                              av(self.ar, A2 + (dc * NRUN + r) * 512, [512]),
                              A(MVT, [512]), inc="tV")
                if r == 0 and dc == 0:
                    self.dump("ain0", av(self.ar, A2 + (dc * NRUN + r) * 512, [512]), [t_ac])
                p.do("tensor", None, wait=[t_ac])
            for dc in range(8):
                t_wo = p.dma("sync", A(OWS, [8, 128]),
                             self.outwt.rearrange("(c p) m -> p c m", p=128)
                             [:, :, dc * 128:(dc + 1) * 128], wait=[p.tok("tT")])
                lo = None
                for c in range(8):
                    w = [t_wo, p.tok("tV"), p.tok("tS")] if c == 0 else []
                    lo = self.mm(self.ps[:, 2, :], A(OWS, [8, 128])[:, c, :],
                                 av(self.ar, A2 + (c * NRUN + r) * 512, [512]),
                                 c == 0, c == 7, wait=w,
                                 inc=("tT" if c == 7 else None))
                t_oe = self.act(A(MVT, [512]), self.ps[:, 2, :], AF.Identity,
                                bias=A(C_OB + dc, [1]),
                                wait=[lo, p.tok("tV")], inc="tS")
                if r == 0 and dc == 0:
                    self.dump("y0", A(MVT, [512]), [t_oe])
                t_x2 = self.v("scalar_tensor_tensor",
                              av(self.ar, A1 + (dc * NRUN + r) * 512, [512]),
                              A(MVT, [512]), A(C_MOD + 16 + dc, [1]),
                              A(P4_XO, [8, 512])[:, dc, :],
                              op0=OP.mult, op1=OP.add, wait=[t_oe, t_xo], inc="tV")
                p.do("tensor", None, wait=[t_x2])
        p.barrier()
        self.dump("x2", A(A1, [8, NRUN, 512]), [p.tok("tV")])

    def p5(self):
        p, A = self.p, self.A
        for r in range(NRUN):
            tsq = []
            for c in range(8):
                tsq.append(self.act(A(P5_XSQ + c * 512, [512]),
                                    av(self.ar, A1 + (c * NRUN + r) * 512, [512]),
                                    AF.Square, inc="tS"))
            lm = ls = None
            for c in range(8):
                lm = self.mm(self.ps[0:1, 0, :], A(C_ONEC, [1]),
                             av(self.ar, A1 + (c * NRUN + r) * 512, [512]),
                             c == 0, c == 7, inc=("tT" if c == 7 else None))
            for c in range(8):
                ls = self.mm(self.ps[0:1, 1, :], A(C_ONEC, [1]),
                             A(P5_XSQ + c * 512, [512]), c == 0, c == 7,
                             wait=[tsq[c]], inc=("tT" if c == 7 else None))
            self.act(A(P5_M2 + r * 512, [512], pn=1), self.ps[0:1, 0, :], AF.Copy,
                     scale=1.0 / D, wait=[lm], inc="tS")
            t_e = self.act(A(P5_RS2 + r * 512, [512], pn=1), self.ps[0:1, 1, :],
                           AF.Copy, scale=1.0 / D, wait=[ls], inc="tS")
            p.do("tensor", None, wait=[t_e, p.tok("tS")])
        t1 = self.act(A(P5_XSQ, [2048], pn=1), A(P5_M2, [2048], pn=1), AF.Square,
                      wait=[p.tok("tS")], inc="tS")
        t2 = self.v("tensor_sub", A(P5_RS2, [2048], pn=1), A(P5_RS2, [2048], pn=1),
                    A(P5_XSQ, [2048], pn=1), wait=[t1])
        t2 = self.v("tensor_scalar_add", A(P5_RS2, [2048], pn=1),
                    A(P5_RS2, [2048], pn=1), EPS, inc="tV")
        t3 = self.act(A(P5_RS2, [2048], pn=1), A(P5_RS2, [2048], pn=1), AF.Sqrt,
                      wait=[t2], inc="tS")
        t4 = self.v("reciprocal", A(P5_RS2, [2048], pn=1), A(P5_RS2, [2048], pn=1),
                    wait=[t3], inc="tV")
        t_rs = self.p.do("vector", lambda ve: ve.drain(), inc="tV")
        t5 = self.v("tensor_mul", A(P5_XSQ, [2048], pn=1), A(P5_M2, [2048], pn=1),
                    A(P5_RS2, [2048], pn=1), wait=[t_rs], inc="tV")
        t_w1 = p.dma("sync", self.dscr[None, DS_ROW:DS_ROW + 2048],
                     A(P5_RS2, [2048], pn=1), wait=[t_rs])
        t_w2 = p.dma("sync", self.dscr[None, DS_ROW + 2048:DS_ROW + 4096],
                     A(P5_XSQ, [2048], pn=1), wait=[t5])
        t_c1 = p.dma("sync", A(C_RS2C, [16]),
                     bass.AP(self.dscr.tensor, DS_ROW, [[1, 128], [128, 16]]),
                     wait=[t_w1])
        t_c2 = p.dma("sync", A(C_M2RS, [16]),
                     bass.AP(self.dscr.tensor, DS_ROW + 2048,
                             [[1, 128], [128, 16]]), wait=[t_w2])

        t_wg = p.dma("sync", A(C_WGP, [8, 8]),
                     self.wgt.rearrange("(c p) e -> p c e", p=128))
        lc2 = None
        for c in range(8):
            lc2 = self.mm(self.ps[0:1, 5, 8:16], A(C_MOD + 24 + c, [1]),
                          A(C_WGP, [8, 8])[:, c, :], c == 0, c == 7,
                          wait=([t_wg] if c == 0 else []),
                          inc=("tT" if c == 7 else None))
        cb1b = bass.AP(self.ar, C_MOD + 32, [[ARN, 128], [1, 8], [0, 8]])
        t_sc = self.v("tensor_mul", A(C_WGP, [8, 8]), A(C_WGP, [8, 8]), cb1b,
                      wait=[lc2], inc="tV")
        lc1 = None
        for c in range(8):
            lc1 = self.mm(self.ps[0:1, 5, 0:8], A(C_ONEC, [1]),
                          A(C_WGP, [8, 8])[:, c, :], c == 0, c == 7,
                          wait=([t_sc] if c == 0 else []),
                          inc=("tT" if c == 7 else None))
        t_e1 = self.scopy(A(P5_LG, [16], pn=1), self.ps[0:1, 5, 0:16],
                          wait=[lc1], inc="tS")
        t_b1 = self.mm(self.ps[:, 6, 0:16], A(R_ONES, [128], pn=1),
                       A(P5_LG, [16], pn=1), True, True, wait=[t_e1], inc="tT")
        t_bc = self.scopy(A(C_C1B, [16]), self.ps[:, 6, 0:16], wait=[t_b1], inc="tS")

        for tt in range(16):
            r, sub = tt // 4, tt % 4
            lg = None
            for c in range(8):
                lg = self.mm(self.ps[:, 4, 0:8],
                             av(self.ar, A1 + (c * NRUN + r) * 512 + sub * 128, [128]),
                             A(C_WGP, [8, 8])[:, c, :], c == 0, c == 7,
                             wait=([t_sc, t_bc, t_c1, t_c2] if c == 0 else []),
                             inc=("tT" if c == 7 else None))
            self.v("tensor_scalar", A(P5_T1, [8]), self.ps[:, 4, 0:8],
                   A(C_RS2C + tt, [1]), None, op0=OP.mult, wait=[lg])
            self.v("tensor_scalar", A(P5_T2, [8]), A(C_C1B, [16])[:, 0:8],
                   A(C_M2RS + tt, [1]), None, op0=OP.mult)
            self.p.do("vector", lambda ve: ve.drain())
            self.v("tensor_sub", A(P5_T1, [8]), A(P5_T1, [8]), A(P5_T2, [8]))
            self.p.do("vector", lambda ve: ve.drain())
            t_lg = self.v("tensor_add", A(P5_LG, [8]), A(P5_T1, [8]),
                          A(C_C1B, [16])[:, 8:16], inc="tV")
            self.p.do("vector", lambda ve: ve.drain())
            t_m1 = self.v("reduce_max", A(P5_M1, [1]), A(P5_LG, [8]), axis=AX.X,
                          inc="tV")
            t_d1 = self.p.do("vector", lambda ve: ve.drain(), inc="tV")
            if tt == 0:
                self.dump("lg0", A(P5_LG, [8]), [t_d1])
            tm1 = self.v("tensor_scalar", A(P5_T1, [8]), A(P5_LG, [8]), A(P5_M1, [1]),
                   None, op0=OP.is_equal, wait=[t_d1], inc="tV")
            self.p.do("vector", lambda ve: ve.drain())
            if tt == 0:
                self.dump("msk1", A(P5_T1, [8]), [p.tok("tV")])
                self.dump("m1v", A(P5_M1, [1]), [p.tok("tV")])
            self.v("scalar_tensor_tensor", A(P5_T2, [8]), A(P5_T1, [8]), -1e30,
                   A(P5_LG, [8]), op0=OP.mult, op1=OP.add)
            self.p.do("vector", lambda ve: ve.drain())
            t_m2 = self.v("reduce_max", A(P5_M2V, [1]), A(P5_T2, [8]), axis=AX.X,
                          inc="tV")
            t_d2 = self.p.do("vector", lambda ve: ve.drain(), inc="tV")
            t_sb = self.v("tensor_sub", A(P5_W1N, [1]), A(P5_M1, [1]),
                          A(P5_M2V, [1]), wait=[t_d2], inc="tV")
            t_db = self.p.do("vector", lambda ve: ve.drain(), inc="tV")
            t_sg = self.act(A(P5_W1N, [1]), A(P5_W1N, [1]), AF.Sigmoid,
                            wait=[t_db], inc="tS")
            t_w2n = self.act(A(P5_W2N, [1]), A(P5_W1N, [1]), AF.Copy, scale=-1.0,
                             bias=1.0, wait=[t_sg], inc="tS")
            self.v("tensor_scalar", A(P5_T2, [8]), A(P5_T2, [8]), A(P5_M2V, [1]),
                   None, op0=OP.is_equal, wait=[t_w2n])
            self.p.do("vector", lambda ve: ve.drain())
            self.v("tensor_scalar", A(P5_T1, [8]), A(P5_T1, [8]), A(P5_W1N, [1]),
                   None, op0=OP.mult)
            self.v("tensor_scalar", A(P5_T2, [8]), A(P5_T2, [8]), A(P5_W2N, [1]),
                   None, op0=OP.mult)
            self.p.do("vector", lambda ve: ve.drain())
            t_g = self.v("tensor_add", A(XGATES + tt * 8, [8]), A(P5_T1, [8]),
                         A(P5_T2, [8]), inc="tV")
            p.do("tensor", None, wait=[t_g])
        self.dump("gates", A(XGATES, [16, 8]), [p.tok("tV")])

        for r in range(NRUN):
            tmb = self.mm(self.ps[:, 2, :], A(R_ONES, [128], pn=1),
                          A(P5_M2 + r * 512, [512], pn=1), True, True,
                          wait=[t_rs, p.tok("tV")], inc="tT")
            trb = self.mm(self.ps[:, 3, :], A(R_ONES, [128], pn=1),
                          A(P5_RS2 + r * 512, [512], pn=1), True, True, inc="tT")
            td = None
            for c in range(8):
                dst = av(self.ar, A2 + (c * NRUN + r) * 512, [512])
                self.v("tensor_sub", dst,
                       av(self.ar, A1 + (c * NRUN + r) * 512, [512]),
                       self.ps[:, 2, :], wait=([tmb, trb] if c == 0 else []))
                self.v("tensor_mul", dst, dst, self.ps[:, 3, :])
                td = self.v("tensor_scalar", dst, dst,
                            A(C_MOD + 32 + c, [1]), A(C_MOD + 24 + c, [1]),
                            op0=OP.mult, op1=OP.add, inc="tV")
            p.do("tensor", None, wait=[td])
        self.dump("h2", A(A2, [8, NRUN, 512]), [p.tok("tV")])

        t_b2 = p.dma("sync", av(self.ar, XB2GB, [1024], 0, 8), self.b2[:, :])
        t_gbr = p.dma("sync", A(P5_LG + 64, [1024], pn=1),
                      self.dscr[None, DS_GBROW:DS_GBROW + 1024])
        for hh in range(2):
            t_r1m = self.mm(self.ps[0:8, 6 + hh, :], A(R_ONES, [8], pn=1),
                            A(P5_LG + 64 + hh * 512, [512], pn=1), True, True,
                            wait=[t_gbr, t_b2], inc="tT")
            self.v("tensor_mul", av(self.ar, XB2GB + hh * 512, [512], 0, 8),
                   av(self.ar, XB2GB + hh * 512, [512], 0, 8),
                   self.ps[0:8, 6 + hh, :], wait=[t_r1m], inc="tV")

        for tt in range(16):
            r, sub = tt // 4, tt % 4
            t_tr = self.p.do("tensor", lambda t, tt=tt: t.transpose(
                self.ps[0:8, 5, 0:128], av(self.ar, XGATES + tt * 8, [8]),
                av(self.ar, C_ID, [128])), wait=[p.tok("tV")], inc="tT")
            t_gte = self.scopy(av(self.ar, XGT, [128], 0, 8), self.ps[0:8, 5, 0:128],
                               wait=[t_tr], inc="tS")
            tx = []
            for c in range(8):
                w_t = (tx[-1:] or []) + ([p.tok("tV"), p.tok("tS")] if c == 0 else [])
                t_t2 = self.p.do("tensor", lambda t, c=c, r=r, sub=sub: t.transpose(
                    self.ps[:, 4, 0:128],
                    av(self.ar, A1 + (c * NRUN + r) * 512 + sub * 128, [128]),
                    av(self.ar, C_ID, [128])), wait=w_t, inc="tT")
                t_ae = self.scopy(av(self.ar, A3 + tt * 1024 + c * 128, [128]),
                                  self.ps[:, 4, 0:128], wait=[t_t2], inc="tS")
                tx.append(t_ae)
            for hh in range(2):
                t_bm = self.mm(self.ps[:, 6, :], av(self.ar, XGT, [128], 0, 8),
                               av(self.ar, XB2GB + hh * 512, [512], 0, 8),
                               True, True, wait=[t_gte, p.tok("tV")], inc="tT")
                t_aa = self.v("tensor_add",
                              av(self.ar, A3 + tt * 1024 + hh * 512, [512]),
                              av(self.ar, A3 + tt * 1024 + hh * 512, [512]),
                              self.ps[:, 6, :], wait=[t_bm, tx[-1]], inc="tV")
                p.do("tensor", None, wait=[t_aa])
        p.barrier()
        self.dump("accinit", A(A3, [16, 1024]), [p.tok("tV")])

    def p6(self):
        p, A = self.p, self.A
        sub = self.stop_after if (self.stop_after or "").startswith("p6") else None
        FA2S, FB2TS, FFB, FGBB, FGBR = A1, A1 + 2048, A1 + 4096, A1 + 6144, A1 + 7168
        t_gbr = p.dma("sync", A(FGBR, [1024], pn=1),
                      self.dscr[None, DS_GBROW:DS_GBROW + 1024])
        if sub == "p6z":
            p.barrier()
            return
        t_gc = None
        for hh in range(2):
            t_m = self.mm(self.ps[:, 7, :], A(R_ONES, [128], pn=1),
                          A(FGBR + hh * 512, [512], pn=1), True, True,
                          wait=([t_gbr] if hh == 0 else [t_gc]), inc="tT")
            t_gc = self.v("tensor_copy", A(FGBB + hh * 512, [512]), self.ps[:, 7, :],
                          wait=[t_m], inc="tV")
        if sub == "p6g":
            p.barrier()
            return
        slot_store = {}
        bank_fe = {}
        prev_acc = None
        for e in (range(E) if sub != "p6f0" else [0]):
            t_a2 = p.dma("sync", av(self.ar, FA2S, [2048], 0, 16), self.a2[e],
                         wait=[p.tok("tT")])
            t_b2t = p.dma("sync", av(self.ar, FB2TS, [1024], 0, 16), self.b2t[e])
            for i in range(32 if sub != "p6f0" else 2):
                hc, doh = i // 2, i % 2
                s, bk = i % 4, 4 + i % 2
                fb = FFB + s * 512
                w = ([t_a2, t_b2t] if i == 0 else []) + \
                    ([bank_fe[bk]] if bk in bank_fe else [])
                t_f = self.mm(self.ps[:, bk, :],
                              av(self.ar, FA2S + hc * 128, [128], 0, 16),
                              av(self.ar, FB2TS + doh * 512, [512], 0, 16),
                              True, True, wait=w, inc="tT")
                t_fe = self.scopy(A(fb, [512]), self.ps[:, bk, :],
                                  wait=[t_f] + ([slot_store[s]]
                                                if s in slot_store else []),
                                  inc="tS")
                t_acc = p.dma("gpsimd", A(fb, [512]),
                              self.w2t[e, hc * 128:(hc + 1) * 128,
                                       doh * 512:(doh + 1) * 512],
                              wait=[t_fe] + ([prev_acc] if prev_acc else []),
                              accum=True)
                prev_acc = t_acc
                self.v("tensor_mul", A(fb, [512]), A(fb, [512]),
                       A(FGBB + doh * 512, [512]), wait=[t_acc])
                t_gb = self.v("drain", inc="tV")
                slot_store[s] = p.dma(
                    "sync", self.w2scr[e, hc * 128:(hc + 1) * 128,
                                       doh * 512:(doh + 1) * 512],
                    A(fb, [512]), wait=[t_gb])
                bank_fe[bk] = t_fe
        p.barrier()
        if "wscr2" in self.dbg:
            p.dma("sync", self.dbg_out["wscr2"], self.w2scr)
            p.barrier()
        if sub == "p6f":
            return

        g_gate = {}
        slab_gate = {}
        w2m_gate = {}
        last_l3 = None
        for tg in (range(NRUN) if sub in (None, "p6") else [0]):
            for (src, uoff, bank) in ((self.a1t, P6_U1, 5), (self.a3t, P6_U3, 6)):
                t_as = p.dma("sync", A(P6_A1S if uoff == P6_U1 else P6_A3S, [8, 128]),
                             src.rearrange("(c p) m -> p c m", p=128),
                             wait=[p.tok("tT")])
                lu = None
                for c in range(8):
                    lu = self.mm(self.ps[:, bank, :],
                                 A((P6_A1S if uoff == P6_U1 else P6_A3S), [8, 128])[:, c, :],
                                 av(self.ar, A2 + (c * NRUN + tg) * 512, [512]),
                                 c == 0, c == 7,
                                 wait=([t_as, p.tok("tV")] if c == 0 else []),
                                 inc=("tT" if c == 7 else None))
                t_ue = self.v("tensor_copy", A(uoff, [512]), self.ps[:, bank, :],
                              wait=[lu], inc="tV")
                t_ust = p.dma("sync",
                              bass.AP(self.dscr.tensor,
                                      (DS_U1 if uoff == P6_U1 else DS_U3),
                                      [[512, 128], [1, 512]]),
                              A(uoff, [512]), wait=[t_ue])
                if uoff == P6_U1:
                    t_u1st = t_ust
                else:
                    t_u3st = t_ust
            for e in (range(E) if sub != "p6e0" else [0]):
                t_u1 = p.dma("sync", av(self.ar, P6_U1, [512], 0, 16),
                             bass.AP(self.dscr.tensor, DS_U1 + e * 16 * 512,
                                     [[512, 16], [1, 512]]),
                             wait=[p.tok("tT"), t_u1st])
                t_u3 = p.dma("sync", av(self.ar, P6_U3, [512], 0, 16),
                             bass.AP(self.dscr.tensor, DS_U3 + e * 16 * 512,
                                     [[512, 16], [1, 512]]),
                             wait=[t_u3st])
                for hg in range(8):
                    w1r = self.w1t.rearrange("e (c p) m -> e p c m", p=128)
                    w3r = self.w3t.rearrange("e (c p) m -> e p c m", p=128)
                    w1s = P6_W1S if hg % 2 == 0 else P6_W1SA
                    w3s = P6_W3S if hg % 2 == 0 else P6_W3SA
                    sg = slab_gate.get(hg % 2)
                    t_w1 = p.dma("sync", A(w1s, [8, 2, 128]),
                                 w1r[e][:, :, hg * 256:(hg + 1) * 256]
                                 .rearrange("p c (h f) -> p c h f", h=2),
                                 wait=([sg] if sg is not None else [p.tok("tT")]))
                    t_w3 = p.dma("sync", A(w3s, [8, 2, 128]),
                                 w3r[e][:, :, hg * 256:(hg + 1) * 256]
                                 .rearrange("p c (h f) -> p c h f", h=2))
                    t_bb1 = p.dma("sync", av(self.ar, P6_B1T, [256], 0, 16),
                                  self.b1t[e][:, hg * 256:(hg + 1) * 256],
                                  wait=([last_l3] if last_l3 is not None else []))
                    t_bb3 = p.dma("sync", av(self.ar, P6_B3T, [256], 0, 16),
                                  self.b3t[e][:, hg * 256:(hg + 1) * 256])
                    for hcs in range(2):
                        hc = hg * 2 + hcs
                        b1k, b3k = (0, 1) if hcs == 0 else (2, 3)
                        gg = g_gate.get(hcs)
                        l1 = l3 = None
                        for c in range(8):
                            w = ([t_w1, t_w3] + ([gg] if gg is not None else [])
                                 if c == 0 else [])
                            l1 = self.mm(self.ps[:, b1k, :],
                                         A(w1s, [8, 2, 128])[:, c, hcs, :],
                                         av(self.ar, A2 + (c * NRUN + tg) * 512, [512]),
                                         c == 0, False, wait=w)
                        l1 = self.mm(self.ps[:, b1k, :],
                                     av(self.ar, P6_B1T + hcs * 128, [128], 0, 16),
                                     av(self.ar, P6_U1, [512], 0, 16),
                                     False, True, wait=[t_bb1, t_u1], inc="tT")
                        for c in range(8):
                            l3 = self.mm(self.ps[:, b3k, :],
                                         A(w3s, [8, 2, 128])[:, c, hcs, :],
                                         av(self.ar, A2 + (c * NRUN + tg) * 512, [512]),
                                         c == 0, False)
                        l3 = self.mm(self.ps[:, b3k, :],
                                     av(self.ar, P6_B3T + hcs * 128, [128], 0, 16),
                                     av(self.ar, P6_U3, [512], 0, 16),
                                     False, True, wait=[t_bb3, t_u3], inc="tT")
                        t_si = self.act(A(P6_G + hc * 512, [512]), self.ps[:, b1k, :],
                                        AF.Silu, bias=A(C_B1 + e * 16 + hc, [1]),
                                        wait=[l1], inc="tS")
                        t_g = self.v("scalar_tensor_tensor", A(P6_G + hc * 512, [512]),
                                     self.ps[:, b3k, :], A(C_B3 + e * 16 + hc, [1]),
                                     A(P6_G + hc * 512, [512]),
                                     op0=OP.add, op1=OP.mult,
                                     wait=[l3, t_si], inc="tV")
                        g_gate[hcs] = t_g
                        slab_gate[hg % 2] = l3
                        last_l3 = l3
                for doh in range(2):
                    lys = [None] * 4
                    for hc in range(16):
                        wslot = P6_W2M if hc % 2 == 0 else P6_W2MA
                        wg = w2m_gate.get(hc % 2)
                        t_wm = p.dma("sync", A(wslot, [512]),
                                     self.w2scr[e, hc * 128:(hc + 1) * 128,
                                                doh * 512:(doh + 1) * 512],
                                     wait=([wg] if wg is not None
                                           else [p.tok("tT")]))
                        for tt in range(4):
                            lys[tt] = self.mm(self.ps[:, 4 + tt, :],
                                              av(self.ar, P6_G + hc * 512 + tt * 128,
                                                 [128]),
                                              A(wslot, [512]), hc == 0, hc == 15,
                                              wait=([t_wm, p.tok("tV")] if tt == 0 else []),
                                              inc="tT")
                        w2m_gate[hc % 2] = lys[3]
                    for tt in range(4):
                        gtt = tg * 4 + tt
                        t_y = self.v("scalar_tensor_tensor",
                                     av(self.ar, A3 + gtt * 1024 + doh * 512, [512]),
                                     self.ps[:, 4 + tt, :],
                                     A(XGATES + gtt * 8 + e, [1]),
                                     av(self.ar, A3 + gtt * 1024 + doh * 512, [512]),
                                     op0=OP.mult, op1=OP.add,
                                     wait=[lys[tt]], inc="tV")
        p.barrier()

    def p7(self):
        p, A = self.p, self.A
        for tt in range(16):
            self.p.dma("sync", self.out[tt * 128:(tt + 1) * 128, :],
                       av(self.ar, A3 + tt * 1024, [1024]),
                       wait=[p.tok("tV"), p.tok("tT")])
        tok = p.tok("dS")
        for e in ENGS:
            p.do(e, None, wait=[tok])

    def build(self):
        self.declare_io()
        es = contextlib.ExitStack()
        with es:
            self.ar = es.enter_context(self.nc.sbuf_tensor("arena", [128, ARN], F32))
            self.ps = es.enter_context(self.nc.psum_tensor("ps", [128, 8, 512], F32))
            self.itab = es.enter_context(self.nc.sbuf_tensor("itab", [1, 256], I32))
            sems = {s: es.enter_context(self.nc.semaphore(s)) for s in self.p.cnt}
            block = es.enter_context(self.nc.Block())
            self._cm = self.nc.allow_non_contiguous_dma(reason="small strided pp/row loads")
            phases = ["p0", "p1", "p2", "p3", "p4", "p5", "p6"]
            if self.stop_after is not None:
                sa = "p6" if self.stop_after.startswith("p6") else self.stop_after
                phases = phases[:phases.index(sa) + 1]
            for ph in phases:
                getattr(self, ph)()
            self.p7()
            with self._cm:
                self.p.emit(block, sems)
        return self.nc


def _half_runs(half):
    return RUNS_A if half == 0 else RUNS_B


def _stage_shared(i):
    s = {}
    s["wmod"] = np.ascontiguousarray(i["L0_wmod"])
    s["qkvwt"] = np.ascontiguousarray(i["L0_attn_qkv_w"].T)
    s["qkvb"] = np.ascontiguousarray(i["L0_attn_qkv_b"])
    s["wwt"] = np.ascontiguousarray(np.concatenate(
        [i["L0_attn_w1w"].T, i["L0_attn_w2w"].T,
         i["L0_attn_r1w"].T, i["L0_attn_r2w"].T], axis=1))
    s["memvwt"] = np.ascontiguousarray(i["L0_attn_memv_w"].T)
    s["memvb"] = np.ascontiguousarray(i["L0_attn_memv_b"])
    s["memgwt"] = np.ascontiguousarray(i["L0_attn_memg_w"].T)
    s["memgb"] = np.ascontiguousarray(i["L0_attn_memg_b"])
    s["mscale"] = np.ascontiguousarray(i["L0_attn_mem_scale"])
    s["imix"] = np.ascontiguousarray(np.asarray(i["L0_attn_iter_mix"]).reshape(1))
    s["outwt"] = np.ascontiguousarray(i["L0_attn_out_w"].T)
    s["outb"] = np.ascontiguousarray(i["L0_attn_out_b"])
    s["dlog"] = np.ascontiguousarray(i["L0_attn_decay_logits"])
    s["wgt"] = np.ascontiguousarray(i["L0_ffn_Wg"].T)
    s["w1t"] = np.ascontiguousarray(i["L0_ffn_W1"].transpose(0, 2, 1))
    s["w2t"] = np.ascontiguousarray(i["L0_ffn_W2"].transpose(0, 2, 1))
    s["w3t"] = np.ascontiguousarray(i["L0_ffn_W3"].transpose(0, 2, 1))
    s["b1"] = np.ascontiguousarray(i["L0_ffn_b1"])
    s["b2"] = np.ascontiguousarray(i["L0_ffn_b2"])
    s["b3"] = np.ascontiguousarray(i["L0_ffn_b3"])
    s["a1t"] = np.ascontiguousarray(
        i["L0_ffn_A1"].transpose(2, 0, 1).reshape(D, E * R))
    s["a3t"] = np.ascontiguousarray(
        i["L0_ffn_A3"].transpose(2, 0, 1).reshape(D, E * R))
    s["b1t"] = np.ascontiguousarray(i["L0_ffn_B1"].transpose(0, 2, 1))
    s["b3t"] = np.ascontiguousarray(i["L0_ffn_B3"].transpose(0, 2, 1))
    s["a2"] = np.ascontiguousarray(i["L0_ffn_A2"])
    s["b2t"] = np.ascontiguousarray(i["L0_ffn_B2"].transpose(0, 2, 1))
    return s


def _stage_core(i, shared, core):
    seq, half = core // 2, core % 2
    runs = _half_runs(half)
    x = np.asarray(i["L0_attn_x"])[seq]
    xt = np.ascontiguousarray(x.T).reshape(8, 128, T).transpose(1, 0, 2)
    xg = np.ascontiguousarray(xt)
    xo = np.stack([xg[:, :, rb * 128: rb * 128 + RUNTOK] for rb in runs], axis=2)
    xo = np.ascontiguousarray(xo)
    crow = np.ascontiguousarray(np.asarray(i["L0_c"])[seq])
    msel = np.zeros(NRUN * 32, np.int32)
    maskt = np.zeros((32, 128, 512), np.float32)
    colv = np.arange(512)[None, :]
    rowv = np.arange(128)[:, None]
    for r in range(NRUN):
        qb0 = runs[r]
        kl = KLEN[r]
        for idx in range(8):
            kt = kl - 8 + idx
            d = qb0 - kt
            maskt[r * 8 + idx] = (d * 128 + colv >= rowv).astype(np.float32)
    roff = np.zeros(8, np.int32)
    for r in range(NRUN):
        roff[r] = P4_MSC + runs[r] * 16
    m = dict(shared)
    m.update({"xg": xg, "xo": xo, "crow": crow, "masksel": msel, "roff_": roff,
              "runoff": roff, "maskt": maskt})
    m.pop("roff_")
    return m


_NC_CACHE = {}


def _get_nc(dbg=(), stop_after=None):
    key = (tuple(sorted(dict(dbg).items())), stop_after)
    if key not in _NC_CACHE:
        kb = KB(dbg, stop_after)
        _NC_CACHE[key] = (kb.build(), kb)
    return _NC_CACHE[key]


def kernel(**inputs):
    nc, kb = _get_nc()
    shared = _stage_shared(inputs)
    in_maps = [_stage_core(inputs, shared, core) for core in range(8)]
    res = run_bass_kernel_spmd(nc, in_maps, core_ids=list(range(8)))
    full = np.zeros((B, T, D), np.float32)
    for core in range(8):
        seq, half = core // 2, core % 2
        runs = _half_runs(half)
        o = np.asarray(res.results[core]["out"])
        for r, rb in enumerate(runs):
            full[seq, rb * 128: rb * 128 + RUNTOK] = o[r * RUNTOK:(r + 1) * RUNTOK]
    return full
